# version 02
# baseline (speedup 1.0000x reference)
import contextlib
import numpy as np
import ml_dtypes

import concourse.bass as bass
import concourse.mybir as mybir
from concourse.bass_utils import run_bass_kernel_spmd

F32 = mybir.dt.float32
BF16 = mybir.dt.bfloat16
U8 = mybir.dt.uint8
AF_ = mybir.ActivationFunctionType
ALU = mybir.AluOpType
AX = mybir.AxisListType

T = 2048
D = 1024
NT = 16
KD = 8
E = 8
INNER = 4096
LN_EPS = 1e-5

MU = float(np.sqrt(0.5))
STD = float(np.sqrt(0.25 * np.pi))
RELSCALE = float(128 ** 0.5)
ALPHA_L1 = 1.0 / (2048.0 * STD)
C0 = float(1.0 / np.sqrt(2.0 * np.pi))

ALL_STAGES = ("l0f_a", "l0a", "l0f_b", "l1prep", "l1f_a", "l1a", "l1f_b")
KNOBS = {"l0f_halves": 2, "l0f_experts": 8, "l0f_gate": True,
         "l0f_h12": True, "l0f_down": True, "l0f_tr": True, "l0f_ln": True,
         "top2": True, "scmm": True}


class Sem:
    _next = 0

    def __init__(self, h):
        self.h = h
        self.n = 0
        self.i = Sem._next
        Sem._next += 1


class Q:
    def __init__(self, name):
        self.name = name
        self.ops = []
        self.lastwait = {}

    def wait(self, sem, val):
        if val <= 0:
            return
        if self.lastwait.get(sem.i, -1) >= val:
            return
        self.lastwait[sem.i] = val
        self.ops.append(lambda e, s=sem, v=val: e.wait_ge(s.h, v))

    def do(self, fn, inc=None, k=1, sw=False):
        if inc is None:
            self.ops.append(fn)
            return None
        inc.n += k
        self.ops.append(lambda e, f=fn, s=inc, kk=k: f(e).then_inc(s.h, kk))
        if sw:
            self.wait(inc, inc.n)
        return inc.n


class Arena:
    def __init__(self, tensor, n):
        self.t = tensor
        self.n = n
        self.off = 0
        self.hwm = 0

    def take(self, cols):
        o = self.off
        self.off += cols
        self.hwm = max(self.hwm, self.off)
        assert self.off <= self.n, f"arena overflow {self.off} > {self.n}"
        return self.t[:, o:o + cols]

    def at(self, off, cols):
        assert off + cols <= self.n
        self.hwm = max(self.hwm, off + cols)
        return self.t[:, off:off + cols]

    def reset(self):
        self.off = 0


class Bank:
    def __init__(self, tensor):
        self.t = tensor
        self.ready = 0
        self.freed = None

    def claim(self, qT):
        if self.freed is not None:
            qT.wait(self.freed[0], self.freed[1])


def build_nc(stages=ALL_STAGES, dbg=False):
    nc = bass.Bass("TRN2", target_bir_lowering=False,
                   detect_race_conditions=False)
    dt = {}

    def din(name, shape, dtype=F32):
        dt[name] = nc.dram_tensor(name, list(shape), dtype, kind="ExternalInput")
        return dt[name]

    din("x_in", (NT, 128, D))
    din("wqT", (KD, 128, KD * 128))
    din("wkT", (KD, 128, KD * 128))
    din("wvT", (128, KD * D))
    din("woT", (128, KD * D))
    din("bq_c", (KD, 128, 1))
    din("bk_c", (KD, 128, 1))
    din("bv_row", (1, D))
    din("bo_row", (1, D))
    din("w12P", (E, 128, 16, 2048))
    din("w3T", (E, 128, 16 * 1024))
    din("gwT", (128, KD * E))
    din("wqk1", (128, KD * 128))
    din("bqk_c", (128, 1))
    din("gb_c", (4, 128, 1))
    din("wv1T", (128, KD * D))
    din("bv1_row", (1, D))
    din("braw", (2, 128, 128))
    din("wgT_raw", (128, KD * INNER))
    din("wuT_raw", (128, KD * INNER))
    din("wdT_raw", (128, 32 * D))
    din("identb", (128, 128), BF16)
    din("identf", (128, 128))
    din("ones_row", (1, 128))
    din("ones_col", (128, 1))
    din("jrel", (128, 384))
    din("madd", (128, 128))
    din("m01", (128, 128))
    din("negp", (128, 3))

    y_out = nc.dram_tensor("y", [NT, 128, D], F32, kind="ExternalOutput")
    wQ = {
        "g": nc.dram_tensor("wgQ", [128, KD * INNER], BF16),
        "u": nc.dram_tensor("wuQ", [128, KD * INNER], BF16),
        "d": nc.dram_tensor("wdQ", [128, 32 * D], BF16),
    }

    with contextlib.ExitStack() as ctx:
        def sb(name, shape, dtype=F32):
            return ctx.enter_context(nc.sbuf_tensor(name, shape, dtype))

        x_sb = sb("x_sb", [128, NT * D])
        ABt = sb("AB", [128, 45568], BF16)
        AFt = sb("AF", [128, 8192])
        identb = sb("identb_s", [128, 128], BF16)
        identf = sb("identf_s", [128, 128])
        ones_r = sb("ones_s", [1, 128])
        ones_c = sb("onesc_s", [128, 1])
        jrel = sb("jrel_s", [128, 384])
        madd = sb("madd_s", [128, 128])
        m01 = sb("m01_s", [128, 128])
        negp = sb("negp_s", [128, 3])
        gw_sb = sb("gw_sb", [128, KD * E])
        bq_sb = sb("bq_sb", [128, KD])
        bk_sb = sb("bk_sb", [128, KD])
        bvr = sb("bvr_s", [1, D])
        bor = sb("bor_s", [1, D])
        bqk_sb = sb("bqk_sb", [128, 1])
        gb_sb = sb("gb_sb", [128, 4])
        bn_sb = sb("bn_sb", [128, 256])
        bv1r = sb("bv1r_s", [1, D])
        wqk_sb = sb("wqk_sb", [128, KD * 128], BF16)
        cols = sb("cols", [128, 64])
        tp_sb = sb("tp_sb", [128, NT * E])
        dump = sb("dump", [128, 512])
        msku8 = sb("msku8", [128, 128], U8)

        AB = Arena(ABt, 45568)
        AF = Arena(AFt, 8192)

        pA = [ctx.enter_context(nc.psum_tensor(f"pA{i}", [128, 512], F32))
              for i in range(6)]
        pTb = [ctx.enter_context(nc.psum_tensor(f"pT{i}", [128, 1024], BF16))
               for i in range(2)]
        bA = [Bank(p) for p in pA]
        bT = [Bank(p) for p in pTb]

        sWS = Sem(ctx.enter_context(nc.semaphore("sWS")))
        sWG = Sem(ctx.enter_context(nc.semaphore("sWG")))
        sT = Sem(ctx.enter_context(nc.semaphore("sT")))
        sV = Sem(ctx.enter_context(nc.semaphore("sV")))
        sS = Sem(ctx.enter_context(nc.semaphore("sS")))
        sST = Sem(ctx.enter_context(nc.semaphore("sST")))

        qT, qV, qS, qY, qG = Q("t"), Q("v"), Q("s"), Q("y"), Q("g")

        def DBG(name, ap, shape, dtype=F32):
            if not dbg:
                return
            if not isinstance(ap, bass.AP):
                ap = ap[:]
            dten = nc.dram_tensor(f"dbg_{name}", list(shape), dtype,
                                  kind="ExternalOutput")
            qY.wait(sV, sV.n)
            qY.wait(sS, sS.n)
            qY.wait(sT, sT.n)
            qY.do(lambda e, d=dten, a=ap: e.dma_start(out=d[:], in_=a),
                  inc=sST, k=16)
            barrier = sST.n
            for q in (qT, qV, qS, qG):
                q.wait(sST, barrier)

        for t in range(NT):
            qY.do(lambda e, t=t: e.dma_start(out=x_sb[:, t * D:(t + 1) * D],
                                             in_=dt["x_in"][t]), inc=sWS, k=16)
        for dst, src in [(identb, "identb"), (identf, "identf"),
                         (ones_r, "ones_row"), (ones_c, "ones_col"),
                         (jrel, "jrel"), (madd, "madd"), (m01, "m01"),
                         (negp, "negp"), (bvr, "bv_row"), (bor, "bo_row"),
                         (bqk_sb, "bqk_c"), (bv1r, "bv1_row"),
                         (gw_sb, "gwT")]:
            qY.do(lambda e, d=dst, s=src: e.dma_start(out=d[:], in_=dt[s][:]),
                  inc=sWS, k=16)
        for k in range(KD):
            qY.do(lambda e, k=k: e.dma_start(out=bq_sb[:, k:k + 1],
                                             in_=dt["bq_c"][k]), inc=sWS, k=16)
            qY.do(lambda e, k=k: e.dma_start(out=bk_sb[:, k:k + 1],
                                             in_=dt["bk_c"][k]), inc=sWS, k=16)
        for j in range(4):
            qY.do(lambda e, j=j: e.dma_start(out=gb_sb[:, j:j + 1],
                                             in_=dt["gb_c"][j]), inc=sWS, k=16)
        for r in range(2):
            qY.do(lambda e, r=r: e.dma_start(out=bn_sb[:, r * 128:(r + 1) * 128],
                                             in_=dt["braw"][r]), inc=sWS, k=16)
        qG.do(lambda e: e.dma_start(out=wqk_sb[:], in_=dt["wqk1"][:]),
              inc=sWG, k=16)
        LWS, LWG = sWS.n, sWG.n
        for q in (qT, qV, qS, qG, qY):
            q.wait(sWS, LWS)
        qT.wait(sWG, LWG)
        qV.wait(sWG, LWG)

        qV.do(lambda e: e.memset(cols[:, 29:30], -128.0), inc=sV)
        for r in range(2):
            qV.do(lambda e, r=r: e.tensor_scalar(
                bn_sb[:, r * 128:(r + 1) * 128], bn_sb[:, r * 128:(r + 1) * 128],
                RELSCALE / STD, -MU / STD, ALU.mult, ALU.add), inc=sV)

        C_M, C_V, C_R, C_SS, C_S1, C_A, C_SX, C_G1 = 0, 1, 2, 3, 4, 5, 6, 7
        C_D, C_RD, C_M1, C_M2, C_P2 = 8, 9, 10, 11, 12
        C_CM = {"g": 16, "u": 17, "d": 18}
        C_TP = {"g": 20, "u": 21, "d": 22}
        C_TN = {"g": 23, "u": 24, "d": 25}
        C_C1, C_C2, C_A2, C_NEGK = 26, 27, 28, 29
        C_SXC = 32

        def col(i):
            return cols[:, i:i + 1]

        def emit_ln_tile(t, dest):
            xt = x_sb[:, t * D:(t + 1) * D]
            qS.wait(sV, sV.n)
            qS.do(lambda e, xt=xt: e.activation(dump[:, 0:512], xt[:, 0:512],
                                                AF_.Square,
                                                accum_out=col(C_SS)), inc=sS)
            qS.do(lambda e, xt=xt: e.activation(dump[:, 0:512], xt[:, 512:1024],
                                                AF_.Square,
                                                accum_out=col(C_G1)), inc=sS)
            qV.do(lambda e, xt=xt: e.reduce_sum(col(C_S1), xt, AX.X), inc=sV, sw=True)
            qV.do(lambda e: e.tensor_scalar(col(C_M), col(C_S1), 1.0 / D, None,
                                            ALU.mult), inc=sV, sw=True)
            qV.wait(sS, sS.n)
            qV.do(lambda e: e.tensor_tensor(col(C_SS), col(C_SS), col(C_G1),
                                            ALU.add), inc=sV, sw=True)
            qV.do(lambda e: e.tensor_tensor(col(C_G1), col(C_M), col(C_M),
                                            ALU.mult), inc=sV, sw=True)
            qV.do(lambda e: e.tensor_scalar(col(C_V), col(C_SS), 1.0 / D, None,
                                            ALU.mult), inc=sV, sw=True)
            qV.do(lambda e: e.tensor_tensor(col(C_V), col(C_V), col(C_G1),
                                            ALU.subtract), inc=sV, sw=True)
            qV.do(lambda e: e.tensor_scalar(col(C_V), col(C_V), LN_EPS, None,
                                            ALU.add), inc=sV, sw=True)
            qS.wait(sV, sV.n)
            qS.do(lambda e: e.activation(col(C_G1), col(C_V), AF_.Sqrt), inc=sS)
            qV.wait(sS, sS.n)
            qV.do(lambda e: e.reciprocal(col(C_R), col(C_G1)), inc=sV, sw=True)
            qV.do(lambda e, xt=xt, dest=dest: e.tensor_scalar(
                dest, xt, col(C_M), col(C_R), ALU.subtract, ALU.mult), inc=sV)

        tr_free = {}

        def emit_transpose_tile(t, src, xlnT, ident, banks, out_cast_q=None):
            qT.wait(sV, sV.n)
            for g in range(2):
                bk = banks[g % 2]
                bk.claim(qT)
                for j in range(4):
                    k = g * 4 + j
                    qT.do(lambda e, bk=bk, j=j, k=k, s=src, ident=ident:
                          e.matmul(bk.t[:, j * 128:(j + 1) * 128],
                                   s[:, k * 128:(k + 1) * 128], ident[:],
                                   is_transpose=True, start=(j == 0),
                                   stop=(j == 3)),
                          inc=sT if j == 3 else None)
                bk.ready = sT.n
                qV.wait(sT, bk.ready)
                for j in range(4):
                    k = g * 4 + j
                    qV.do(lambda e, bk=bk, j=j, k=k, t=t, xlnT=xlnT:
                          e.tensor_copy(xlnT[k][:, t * 128:(t + 1) * 128],
                                        bk.t[:, j * 128:(j + 1) * 128]),
                          inc=sV if j == 3 else None)
                bk.freed = (sV, sV.n)

        gate_state = {"x32_free": 0}

        def emit_transpose_gate_tile(t, src, xlnT, x32buf, scbank):
            qT.wait(sV, sV.n)
            for g in range(2):
                bk = bA[4 + (g % 2)]
                bk.claim(qT)
                for j in range(4):
                    k = g * 4 + j
                    qT.do(lambda e, bk=bk, j=j, k=k, s=src:
                          e.matmul(bk.t[:, j * 128:(j + 1) * 128],
                                   s[:, k * 128:(k + 1) * 128], identf[:],
                                   is_transpose=True, start=(j == 0),
                                   stop=(j == 3)),
                          inc=sT if j == 3 else None)
                bk.ready = sT.n
                qV.wait(sT, bk.ready)
                for j in range(4):
                    k = g * 4 + j
                    qV.do(lambda e, bk=bk, j=j, k=k, t=t, xlnT=xlnT:
                          e.tensor_copy(xlnT[k][:, t * 128:(t + 1) * 128],
                                        bk.t[:, j * 128:(j + 1) * 128]),
                          inc=sV if j == 3 else None)
                qS.wait(sV, sV.n)
                qS.wait(sT, gate_state["x32_free"])
                for j in range(4):
                    qS.do(lambda e, bk=bk, j=j, g=g, x32buf=x32buf:
                          e.copy(x32buf[g][:, j * 128:(j + 1) * 128],
                                 bk.t[:, j * 128:(j + 1) * 128]),
                          inc=sS if j == 3 else None)
                bk.freed = (sS, sS.n)
                sdone = sS.n
                qT.wait(sS, sdone)
                for j in range(4 if KNOBS["scmm"] else 0):
                    k = g * 4 + j
                    qT.do(lambda e, j=j, k=k, g=g, x32buf=x32buf, scbank=scbank:
                          e.matmul(scbank.t[:, 0:8],
                                   x32buf[g][:, j * 128:(j + 1) * 128],
                                   gw_sb[:, k * 8:(k + 1) * 8],
                                   start=(k == 0), stop=(k == 7)),
                          inc=sT if k == 7 else None)
            scbank.ready = sT.n
            gate_state["x32_free"] = sT.n

        def emit_top2(t, sc8):
            qV.do(lambda e, s8=sc8: e.reduce_max(col(C_M1), s8, AX.X), inc=sV, sw=True)
            qV.do(lambda e, s8=sc8: e.tensor_scalar(
                dump[:, 0:8], s8, col(C_M1), None, ALU.is_equal), inc=sV, sw=True)
            qV.do(lambda e, s8=sc8: e.scalar_tensor_tensor(
                dump[:, 8:16], dump[:, 0:8], -1e30, s8, ALU.mult, ALU.add),
                inc=sV, sw=True)
            qV.do(lambda e: e.reduce_max(col(C_M2), dump[:, 8:16], AX.X), inc=sV, sw=True)
            qV.do(lambda e: e.tensor_tensor(col(C_P2), col(C_M2), col(C_M1),
                                            ALU.subtract), inc=sV, sw=True)
            qS.wait(sV, sV.n)
            qS.do(lambda e: e.activation(col(C_P2), col(C_P2), AF_.Sigmoid),
                  inc=sS)
            qV.wait(sS, sS.n)
            qV.do(lambda e: e.tensor_scalar(col(C_G1), col(C_P2), -0.5, 0.5,
                                            ALU.mult, ALU.add), inc=sV, sw=True)
            qV.do(lambda e: e.tensor_scalar(col(C_P2), col(C_P2), 0.5, None,
                                            ALU.mult), inc=sV, sw=True)
            qV.do(lambda e: e.tensor_scalar(dump[:, 8:16], dump[:, 8:16],
                                            col(C_M2), None, ALU.is_equal),
                  inc=sV, sw=True)
            qV.do(lambda e: e.tensor_scalar(dump[:, 0:8], dump[:, 0:8],
                                            col(C_G1), None, ALU.mult), inc=sV, sw=True)
            qV.do(lambda e, t=t: e.scalar_tensor_tensor(
                tp_sb[:, t * 8:(t + 1) * 8], dump[:, 8:16], col(C_P2),
                dump[:, 0:8], ALU.mult, ALU.add), inc=sV, sw=True)

        def emit_round_ladder(y_ap, q_ap, tst, sgn, w):
            qV.do(lambda e, y=y_ap, t=tst: e.tensor_scalar(
                t, y, 128.5, None, ALU.add), inc=sV)
            Cacc = 0.0
            for k in (128, 64, 32, 16, 8, 4, 2, 1):
                qS.wait(sV, sV.n)
                qS.do(lambda e, t=tst, s=sgn:
                      e.activation(s, t, AF_.Sign, bias=col(C_NEGK)), inc=sS)
                qV.wait(sS, sS.n)
                qV.do(lambda e, t=tst, s=sgn, k=k:
                      e.scalar_tensor_tensor(t, s, -float(k) / 2.0, t,
                                             ALU.mult, ALU.add), inc=sV)
                Cacc += float(k) / 2.0
            for c0 in range(0, w, 512):
                qV.do(lambda e, y=y_ap, t=tst, c0=c0: e.scalar_tensor_tensor(
                    dump[:, 0:512], t[:, c0:c0 + 512] if hasattr(t, '__getitem__') else t,
                    -1.0, y[:, c0:c0 + 512] if hasattr(y, '__getitem__') else y,
                    ALU.mult, ALU.add), inc=sV)
                qV.do(lambda e, q=q_ap, C=Cacc, c0=c0: e.tensor_scalar(
                    q[:, c0:c0 + 512], dump[:, 0:512], 0.5 + C, None,
                    ALU.add), inc=sV)

        def phase_guard():
            qG.wait(sT, sT.n)
            qG.wait(sV, sV.n)
            qG.wait(sS, sS.n)
            qY.wait(sT, sT.n)
            qY.wait(sV, sV.n)
            qY.wait(sS, sS.n)

        l0f_state = {"w3_free": 0, "w12_free": [0, 0]}

        def emit_l0_ffn():
            AB.reset()
            AF.reset()
            phase_guard()
            xlnT = [AB.take(T // 2) for _ in range(KD)]
            hid = [AB.take(1024) for _ in range(16)]
            w12b = [AB.take(2048), AB.take(2048)]
            w3b = AB.take(16 * 1024)
            xlnb = [AF.take(D), AF.take(D)]
            x32buf = [AF.take(512), AF.take(512)]
            scbuf = AF.take(128)
            xln_free = [0, 0]

            sil = [0, 0]
            for half in range(KNOBS["l0f_halves"]):
                for tt in range(8):
                    t = half * 8 + tt
                    qV.wait(sT, xln_free[t % 2])
                    if KNOBS["l0f_ln"]:
                        emit_ln_tile(t, xlnb[t % 2])
                    if not KNOBS["l0f_tr"]:
                        continue
                    if not KNOBS["l0f_gate"]:
                        emit_transpose_tile(tt, xlnb[t % 2], xlnT, identf,
                                            [bA[4], bA[5]])
                        xln_free[t % 2] = sT.n
                        continue
                    scb = bA[2 + t % 2]
                    scb.claim(qT)
                    emit_transpose_gate_tile(tt, xlnb[t % 2], xlnT, x32buf, scb)
                    xln_free[t % 2] = sT.n
                    qV.wait(sT, scb.ready)
                    qV.do(lambda e, t=t, scb=scb: e.tensor_copy(
                        scbuf[:, t % 8 * 8:(t % 8 + 1) * 8], scb.t[:, 0:8]),
                        inc=sV, sw=True)
                    scb.freed = (sV, sV.n)
                if half == 0:
                    DBG("xln0", xlnb[0], (128, D))
                    DBG("xlnT0", xlnT[0], (128, T // 2), BF16)
                    DBG("scbuf", scbuf[:, 0:64], (128, 64))
                for tt in range(8 if KNOBS["l0f_gate"] and KNOBS["l0f_tr"]
                                and KNOBS["top2"] else 0):
                    t = half * 8 + tt
                    emit_top2(t, scbuf[:, tt * 8:(tt + 1) * 8])
                if half == 0:
                    DBG("tp", tp_sb[:, 0:64], (128, 64))
                xlnT_ready = sV.n
                tp_ready = sV.n

                for ei in range(KNOBS["l0f_experts"]):
                    def w12_load(i):
                        qG.wait(sT, l0f_state["w12_free"][i % 2])
                        qG.do(lambda e, ei=ei, i=i: e.dma_start(
                            out=w12b[i % 2], in_=dt["w12P"][ei, :, i]),
                            inc=sWG, k=16)
                        return sWG.n

                    w12_ld = [0] * 16
                    w12_ld[0] = w12_load(0)
                    w12_ld[1] = w12_load(1)
                    qG.wait(sT, l0f_state["w3_free"])
                    qG.do(lambda e, ei=ei: e.dma_start(
                        out=w3b, in_=dt["w3T"][ei]), inc=sWG, k=16)
                    w3_ld = sWG.n

                    qT.wait(sV, xlnT_ready)
                    for i in range(16 if KNOBS["l0f_h12"] else 0):
                        qT.wait(sWG, w12_ld[i])
                        for t2 in range(2):
                            c0 = t2 * 512
                            bh1 = bA[t2 * 2]
                            bh2 = bA[t2 * 2 + 1]
                            bh1.claim(qT)
                            bh2.claim(qT)
                            for k in range(KD):
                                qT.do(lambda e, i=i, k=k, c0=c0, b=bh1:
                                      e.matmul(b.t[:],
                                               w12b[i % 2][:, k * 256:k * 256 + 128],
                                               xlnT[k][:, c0:c0 + 512],
                                               start=(k == 0), stop=(k == 7)),
                                      inc=sT if k == 7 else None)
                            bh1.ready = sT.n
                            for k in range(KD):
                                qT.do(lambda e, i=i, k=k, c0=c0, b=bh2:
                                      e.matmul(b.t[:],
                                               w12b[i % 2][:, k * 256 + 128:k * 256 + 256],
                                               xlnT[k][:, c0:c0 + 512],
                                               start=(k == 0), stop=(k == 7)),
                                      inc=sT if k == 7 else None)
                            bh2.ready = sT.n
                            qS.wait(sT, bh1.ready)
                            qS.wait(sV, sil[0])
                            qS.do(lambda e, b=bh1: e.activation(
                                dump[:, 0:512], b.t[:], AF_.Silu), inc=sS)
                            sdone = sS.n
                            bh1.freed = (sS, sdone)
                            qV.wait(sS, sdone)
                            qV.wait(sT, bh2.ready)
                            qV.do(lambda e, b=bh2, i=i, t2=t2:
                                  e.tensor_tensor(
                                      hid[i][:, t2 * 512:(t2 + 1) * 512],
                                      dump[:, 0:512], b.t[:], ALU.mult),
                                  inc=sV)
                            bh2.freed = (sV, sV.n)
                            sil[0] = sV.n
                        l0f_state["w12_free"][i % 2] = sT.n
                        if i + 2 < 16:
                            w12_ld[i + 2] = w12_load(i + 2)
                    hid_ready = sV.n

                    qT.wait(sWG, w3_ld)
                    qT.wait(sV, hid_ready)
                    qT.wait(sV, tp_ready)
                    for tt in range(8 if KNOBS["l0f_down"] else 0):
                        t_abs = half * 8 + tt
                        t2, tc = tt // 4, (tt % 4) * 128
                        for n in range(2):
                            b = bA[4 + (tt * 2 + n) % 2]
                            b.claim(qT)
                            for kh in range(16):
                                qT.do(lambda e, kh=kh, t2=t2, tc=tc, n=n, b=b:
                                      e.matmul(
                                          b.t[:],
                                          hid[kh][:, t2 * 512 + tc:t2 * 512 + tc + 128],
                                          w3b[:, kh * 1024 + n * 512:kh * 1024 + (n + 1) * 512],
                                          start=(kh == 0), stop=(kh == 15)),
                                      inc=sT if kh == 15 else None)
                            b.ready = sT.n
                            qV.wait(sT, b.ready)
                            xa = x_sb[:, t_abs * D + n * 512:t_abs * D + (n + 1) * 512]
                            qV.do(lambda e, b=b, xa=xa, t_abs=t_abs, ei=ei:
                                  e.scalar_tensor_tensor(
                                      xa, b.t[:],
                                      tp_sb[:, t_abs * 8 + ei:t_abs * 8 + ei + 1],
                                      xa, ALU.mult, ALU.add), inc=sV)
                            b.freed = (sV, sV.n)
                    l0f_state["w3_free"] = sT.n
                    if half == 0 and ei == 0:
                        DBG("hidall", ABt[:, 8192:8192 + 16384],
                            (128, 16384), BF16)
                        DBG("xafter0", x_sb[:, 0:D], (128, D))

        l0a_state = {"wqk_free": [0]}

        def emit_l0_attn():
            AB.reset()
            AF.reset()
            phase_guard()
            xlnT = [AB.take(T) for _ in range(KD)]
            xlnb16 = [AB.take(D), AB.take(D)]
            xln_free = [0, 0]
            for t in range(NT):
                qV.wait(sT, xln_free[t % 2])
                emit_ln_tile(t, xlnb16[t % 2])
                emit_transpose_tile(t, xlnb16[t % 2], xlnT, identb, bT)
                xln_free[t % 2] = sT.n
            xlnT_ready = sV.n

            xoff = 16384
            pnb = AB.at(xoff, 512)
            ptb = [AB.at(xoff + 512 + i * 128, 128) for i in range(3)]
            wq_b = AB.at(xoff + 896, KD * 128)
            at = [AB.take(T) for _ in range(KD)]
            voff = AB.off
            v_c = [[AB.at(voff + b_ * 2048 + t_ * 128, 128) for t_ in range(NT)]
                   for b_ in range(2)]
            AB.off += 4096
            qt = AB.take(T)
            kt = AB.take(T)
            wk_b = AB.take(KD * 128)
            wv_b = [AB.take(KD * 128)]
            wo_b = AB.at(voff, KD * D)
            prow = AF.take(512)
            ln_done = sT.n
            l0a_state["pn_free"] = ln_done
            l0a_state["pt_free"] = ln_done
            l0a_state["wqk_free"][0] = ln_done

            v_free = [0, 0]
            wv_free = 0
            for c in range(KD):
                qG.wait(sT, l0a_state["wqk_free"][0])
                qG.do(lambda e, c=c: e.dma_start(out=wq_b,
                                                 in_=dt["wqT"][c]), inc=sWG, k=16)
                qG.do(lambda e, c=c: e.dma_start(out=wk_b,
                                                 in_=dt["wkT"][c]), inc=sWG, k=16)
                qG.wait(sT, wv_free)
                qG.do(lambda e, c=c: e.dma_start(
                    out=wv_b[0],
                    in_=dt["wvT"][:].rearrange("p (k f) -> p k f", k=KD)
                    [:, :, c * 128:(c + 1) * 128]), inc=sWG, k=16)
                wqk_ld = sWG.n
                qT.wait(sWG, wqk_ld)
                qT.wait(sV, xlnT_ready)
                for t in range(NT):
                    b = bA[4 + t % 2]
                    b.claim(qT)
                    for k in range(KD):
                        qT.do(lambda e, k=k, t=t, b=b, c=c: e.matmul(
                            b.t[:, 0:128], xlnT[k][:, t * 128:(t + 1) * 128],
                            wv_b[0][:, k * 128:(k + 1) * 128],
                            start=(k == 0), stop=False))
                    qT.do(lambda e, b=b, c=c: e.matmul(
                        b.t[:, 0:128], ones_r[:],
                        bvr[:, c * 128:(c + 1) * 128],
                        start=False, stop=True), inc=sT)
                    b.ready = sT.n
                    qS.wait(sT, b.ready)
                    qS.wait(sT, v_free[c % 2])
                    qS.do(lambda e, b=b, t=t, c=c: e.copy(
                        v_c[c % 2][t], b.t[:, 0:128]), inc=sS)
                    b.freed = (sS, sS.n)
                wv_free = sT.n
                for wbuf, dst, bias in ((wq_b, qt, bq_sb),
                                        (wk_b, kt, bk_sb)):
                    for t4 in range(4):
                        bA[t4].claim(qT)
                    for k in range(KD):
                        for t4 in range(4):
                            qT.do(lambda e, k=k, t4=t4, wbuf=wbuf: e.matmul(
                                bA[t4].t[:], wbuf[:, k * 128:(k + 1) * 128],
                                xlnT[k][:, t4 * 512:(t4 + 1) * 512],
                                start=(k == 0), stop=(k == 7)),
                                inc=sT if (k == 7 and t4 == 3) else None)
                    rdy = sT.n
                    qS.wait(sT, rdy)
                    for t4 in range(4):
                        qS.do(lambda e, t4=t4, dst=dst, bias=bias, c=c:
                              e.activation(dst[:, t4 * 512:(t4 + 1) * 512],
                                           bA[t4].t[:], AF_.Identity,
                                           bias=bias[:, c:c + 1]),
                              inc=sS if t4 == 3 else None)
                    for t4 in range(4):
                        bA[t4].freed = (sS, sS.n)
                l0a_state["wqk_free"][0] = sT.n
                qk_ready = sS.n

                for hh in range(2):
                    h = 2 * c + hh
                    r0 = 64 * hh
                    for a in range(NT):
                        W = min(a, 2) + 1
                        j0 = 128 * (a - (W - 1))
                        wc = 128 * W
                        bs = bA[a % 2]
                        bs.claim(qT)
                        qT.wait(sS, qk_ready)
                        pn_free = l0a_state.get("pn_free", 0)
                        pt_free = l0a_state.get("pt_free", 0)
                        qT.do(lambda e, bs=bs, r0=r0, a=a, j0=j0, wc=wc:
                              e.matmul(bs.t[:, 0:wc],
                                       qt[r0:r0 + 64, a * 128:(a + 1) * 128],
                                       kt[r0:r0 + 64, j0:j0 + wc],
                                       start=True, stop=True), inc=sT)
                        bs.ready = sT.n
                        qV.wait(sT, bs.ready)
                        qV.wait(sS, sS.n)
                        qV.do(lambda e, bs=bs, wc=wc: e.scalar_tensor_tensor(
                            dump[:, 0:wc], bs.t[:, 0:wc], 0.125, jrel[:, 0:wc],
                            ALU.mult, ALU.add), inc=sV)
                        bs.freed = (sV, sV.n)
                        qV.do(lambda e, wc=wc: e.tensor_tensor(
                            dump[:, wc - 128:wc], dump[:, wc - 128:wc],
                            madd[:], ALU.add), inc=sV)
                        qS.wait(sV, sV.n)
                        qS.do(lambda e, wc=wc, W=W: e.activation(
                            prow[:, 0:wc], dump[:, 0:wc], AF_.Exp,
                            bias=negp[:, W - 1:W], accum_out=col(C_D)), inc=sS)
                        qV.wait(sS, sS.n)
                        qV.do(lambda e: e.reciprocal(col(C_RD), col(C_D)), inc=sV, sw=True)
                        qV.wait(sT, pn_free)
                        qV.do(lambda e, wc=wc: e.tensor_scalar(
                            pnb[:, 0:wc], prow[:, 0:wc], col(C_RD), None,
                            ALU.mult), inc=sV)
                        pn_ready = sV.n
                        btr = bT[a % 2]
                        btr.claim(qT)
                        qT.wait(sV, pn_ready)
                        for w in range(W):
                            qT.do(lambda e, btr=btr, w=w, W=W: e.matmul(
                                btr.t[:, w * 128:(w + 1) * 128],
                                pnb[:, w * 128:(w + 1) * 128], identb[:],
                                is_transpose=True, start=(w == 0),
                                stop=(w == W - 1)),
                                inc=sT if w == W - 1 else None)
                        btr.ready = sT.n
                        l0a_state["pn_free"] = btr.ready
                        qS.wait(sT, btr.ready)
                        qS.wait(sT, pt_free)
                        for w in range(W):
                            qS.do(lambda e, btr=btr, w=w: e.copy(
                                ptb[w], btr.t[:, w * 128:(w + 1) * 128]),
                                inc=sS if w == W - 1 else None)
                        btr.freed = (sS, sS.n)
                        pt_ready = sS.n
                        bpv = bA[2 + a % 2]
                        bpv.claim(qT)
                        qT.wait(sS, pt_ready)
                        for w in range(W):
                            jt = a - (W - 1) + w
                            qT.do(lambda e, bpv=bpv, w=w, jt=jt, c=c, hh=hh,
                                  W=W: e.matmul(
                                      bpv.t[0:64, 0:128],
                                      v_c[c % 2][jt][:, hh * 64:hh * 64 + 64],
                                      ptb[w], start=(w == 0), stop=(w == W - 1)),
                                  inc=sT if w == W - 1 else None)
                        bpv.ready = sT.n
                        l0a_state["pt_free"] = bpv.ready
                        qV.wait(sT, bpv.ready)
                        qV.do(lambda e, bpv=bpv, c=c, r0=r0, a=a: e.tensor_copy(
                            at[c][r0:r0 + 64, a * 128:(a + 1) * 128],
                            bpv.t[0:64, 0:128]), inc=sV)
                        bpv.freed = (sV, sV.n)
                v_free[c % 2] = sT.n
            at_ready = sV.n

            qG.wait(sT, sT.n)
            qG.do(lambda e: e.dma_start(out=wo_b, in_=dt["woT"][:]),
                  inc=sWG, k=16)
            wo_ld = sWG.n
            qT.wait(sWG, wo_ld)
            qT.wait(sV, at_ready)
            for t in range(NT):
                for n in range(2):
                    b = bA[4 + (t * 2 + n) % 2]
                    b.claim(qT)
                    for k in range(KD):
                        qT.do(lambda e, k=k, t=t, n=n, b=b: e.matmul(
                            b.t[:], at[k][:, t * 128:(t + 1) * 128],
                            wo_b[:, k * D + n * 512:k * D + (n + 1) * 512],
                            start=(k == 0), stop=False))
                    qT.do(lambda e, n=n, b=b: e.matmul(
                        b.t[:], ones_r[:], bor[:, n * 512:(n + 1) * 512],
                        start=False, stop=True), inc=sT)
                    b.ready = sT.n
                    qV.wait(sT, b.ready)
                    xa = x_sb[:, t * D + n * 512:t * D + (n + 1) * 512]
                    qV.do(lambda e, b=b, xa=xa: e.tensor_tensor(
                        xa, xa, b.t[:], ALU.add), inc=sV)
                    b.freed = (sV, sV.n)

        def emit_l1_prep():
            AB.reset()
            AF.reset()
            phase_guard()
            stg = [AF.take(4096), AF.take(4096)]
            qa = AB.take(4096)
            qb = AB.take(4096)
            wqo = AB.take(4096)
            stg_free = [0, 0]
            store_done = sWS.n
            for wi, (name, nslices) in enumerate(
                    [("g", 8), ("u", 8), ("d", 8)]):
                raw = {"g": "wgT_raw", "u": "wuT_raw", "d": "wdT_raw"}[name]
                qV.do(lambda e, c=C_CM[name]: e.memset(col(c), 0.0), inc=sV, sw=True)
                for s in range(nslices):
                    qG.wait(sV, stg_free[s % 2])
                    qG.do(lambda e, s=s, raw=raw: e.dma_start(
                        out=stg[s % 2],
                        in_=dt[raw][:, s * 4096:(s + 1) * 4096]), inc=sWG, k=16)
                    ld = sWG.n
                    qV.wait(sWG, ld)
                    qV.do(lambda e, s=s: e.reduce_sum(
                        col(C_S1), stg[s % 2], AX.X, apply_absolute_value=True),
                        inc=sV, sw=True)
                    qV.do(lambda e, c=C_CM[name]: e.tensor_tensor(
                        col(c), col(c), col(C_S1), ALU.add), inc=sV, sw=True)
                    stg_free[s % 2] = sV.n
                b = bA[0]
                b.claim(qT)
                qT.wait(sV, sV.n)
                qT.do(lambda e, b=b, c=C_CM[name]: e.matmul(
                    b.t[0:1, 0:1], col(c), ones_c[:], start=True, stop=True),
                    inc=sT)
                b.ready = sT.n
                qV.wait(sT, b.ready)
                qV.do(lambda e, b=b: e.tensor_copy(dump[0:1, 0:1],
                                                   b.t[0:1, 0:1]), inc=sV, sw=True)
                b.freed = (sV, sV.n)
                b2 = bA[1]
                b2.claim(qT)
                qT.wait(sV, sV.n)
                qT.do(lambda e, b2=b2: e.matmul(
                    b2.t[:, 0:1], ones_r[:], dump[0:1, 0:1],
                    start=True, stop=True), inc=sT)
                b2.ready = sT.n
                qV.wait(sT, b2.ready)
                tot = 128.0 * nslices * 4096.0
                qV.do(lambda e, b2=b2, c=C_CM[name], tot=tot: e.tensor_scalar(
                    col(c), b2.t[:, 0:1], 1.0 / tot, None, ALU.mult), inc=sV, sw=True)
                b2.freed = (sV, sV.n)
                qV.do(lambda e, c=C_CM[name]: e.tensor_scalar(
                    col(c), col(c), 1e-5, None, ALU.max), inc=sV, sw=True)
                qV.do(lambda e, c=C_CM[name], p=C_TP[name]: e.tensor_scalar(
                    col(p), col(c), 0.5, None, ALU.mult), inc=sV, sw=True)
                qV.do(lambda e, c=C_CM[name], nn=C_TN[name]: e.tensor_scalar(
                    col(nn), col(c), -0.5, None, ALU.mult), inc=sV, sw=True)
                thr_ready = sV.n
                for s in range(nslices):
                    qG.wait(sV, stg_free[s % 2])
                    qG.do(lambda e, s=s, raw=raw: e.dma_start(
                        out=stg[s % 2],
                        in_=dt[raw][:, s * 4096:(s + 1) * 4096]), inc=sWG, k=16)
                    ld = sWG.n
                    qV.wait(sWG, ld)
                    qV.wait(sV, thr_ready)
                    qV.wait(sWS, store_done)
                    qV.do(lambda e, s=s, p=C_TP[name]: e.tensor_scalar(
                        qa, stg[s % 2], col(p), None, ALU.is_gt), inc=sV, sw=True)
                    qV.do(lambda e, s=s, nn=C_TN[name]: e.tensor_scalar(
                        qb, stg[s % 2], col(nn), None, ALU.is_lt), inc=sV, sw=True)
                    stg_free[s % 2] = sV.n
                    qV.do(lambda e: e.tensor_tensor(wqo, qa, qb, ALU.subtract),
                          inc=sV, sw=True)
                    wq_ready = sV.n
                    qY.wait(sV, wq_ready)
                    qY.do(lambda e, s=s, name=name: e.dma_start(
                        out=wQ[name][:, s * 4096:(s + 1) * 4096], in_=wqo),
                        inc=sWS, k=16)
                    store_done = sWS.n
            emit_l1_prep.store_done = store_done

        def emit_l1_ffn():
            AB.reset()
            AF.reset()
            phase_guard()
            xqT = [AB.take(512) for _ in range(KD)]
            xqb = [AB.take(D), AB.take(D)]
            offB = AB.off
            hid = [AB.at(offB + i * 4096, 4096) for i in range(4)]
            wdb = [AB.at(offB, 8 * 1024), AB.at(offB + 8 * 1024, 8 * 1024)]
            offC = offB + 4 * 4096
            wgb = [AB.at(offC, 4096), AB.at(offC + 4096, 4096)]
            wub = [AB.at(offC + 8192, 4096), AB.at(offC + 12288, 4096)]
            xq2T = [AB.at(offC + i * 512, 512) for i in range(32)]
            offX = offC + 4 * 4096
            xq2 = AB.at(offX, 4096)
            AB.hwm = max(AB.hwm, offX + 4096)

            xlnb = [AF.take(D), AF.take(D)]
            tst = AF.take(D)
            sgn = AF.take(D)
            y_acc = [AF.take(D) for _ in range(4)]

            st = emit_l1_ffn.state
            for sp in range(4):
                for ti in range(4):
                    t = sp * 4 + ti
                    qV.wait(sT, st["xln_free"][t % 2])
                    emit_ln_tile(t, xlnb[t % 2])
                    qV.do(lambda e, x=xlnb[t % 2]: e.reduce_max(
                        col(C_A), x, AX.X, apply_absolute_value=True), inc=sV, sw=True)
                    qV.do(lambda e: e.tensor_scalar(col(C_A), col(C_A), 1e-5,
                                                    None, ALU.max), inc=sV, sw=True)
                    qV.do(lambda e: e.reciprocal(col(C_SX), col(C_A)), inc=sV, sw=True)
                    qV.do(lambda e: e.tensor_scalar(col(C_SX), col(C_SX), 127.0,
                                                    None, ALU.mult), inc=sV, sw=True)
                    qV.do(lambda e, x=xlnb[t % 2]: e.tensor_scalar(
                        x, x, col(C_SX), None, ALU.mult), inc=sV)
                    qV.do(lambda e, ti=ti: e.tensor_scalar(
                        cols[:, C_SXC + ti:C_SXC + ti + 1], col(C_A),
                        1.0 / 127.0, None, ALU.mult), inc=sV, sw=True)
                    qV.wait(sV, st["xq_free_v"][t % 2])
                    emit_round_ladder(xlnb[t % 2], xqb[t % 2], tst, sgn, D)
                    emit_transpose_tile(ti, xqb[t % 2], xqT, identb, bT)
                    st["xln_free"][t % 2] = sT.n
                    st["xq_free_v"][t % 2] = 0
                xqT_ready = sV.n
                hid_ready = 0
                for s in range(8):
                    qG.wait(sT, st["wgu_free"][s % 2])
                    qG.wait(sT, st["wd_free"][1])
                    qG.wait(sWS, emit_l1_prep.store_done)
                    qG.do(lambda e, s=s: e.dma_start(
                        out=wgb[s % 2], in_=wQ["g"][:].rearrange(
                            "p (k o) -> p k o", k=KD)[:, :, s * 512:(s + 1) * 512]),
                        inc=sWG, k=16)
                    qG.do(lambda e, s=s: e.dma_start(
                        out=wub[s % 2], in_=wQ["u"][:].rearrange(
                            "p (k o) -> p k o", k=KD)[:, :, s * 512:(s + 1) * 512]),
                        inc=sWG, k=16)
                    wld = sWG.n
                    qT.wait(sWG, wld)
                    qT.wait(sV, xqT_ready)
                    for ti in range(4):
                        bg = bA[ti % 2]
                        bu = bA[2 + ti % 2]
                        bg.claim(qT)
                        bu.claim(qT)
                        for k in range(KD):
                            qT.do(lambda e, k=k, ti=ti, s=s, bg=bg: e.matmul(
                                bg.t[:], xqT[k][:, ti * 128:(ti + 1) * 128],
                                wgb[s % 2][:, k * 512:(k + 1) * 512],
                                start=(k == 0), stop=(k == 7)),
                                inc=sT if k == 7 else None)
                        bg.ready = sT.n
                        for k in range(KD):
                            qT.do(lambda e, k=k, ti=ti, s=s, bu=bu: e.matmul(
                                bu.t[:], xqT[k][:, ti * 128:(ti + 1) * 128],
                                wub[s % 2][:, k * 512:(k + 1) * 512],
                                start=(k == 0), stop=(k == 7)),
                                inc=sT if k == 7 else None)
                        bu.ready = sT.n
                        qV.do(lambda e, ti=ti: e.tensor_tensor(
                            col(C_C1), cols[:, C_SXC + ti:C_SXC + ti + 1],
                            col(C_CM["g"]), ALU.mult), inc=sV, sw=True)
                        qV.do(lambda e, ti=ti: e.tensor_tensor(
                            col(C_G1), cols[:, C_SXC + ti:C_SXC + ti + 1],
                            col(C_CM["u"]), ALU.mult), inc=sV, sw=True)
                        c_ready = sV.n
                        qS.wait(sT, bg.ready)
                        qS.wait(sV, c_ready)
                        qS.wait(sV, st["sil1"][0])
                        qS.do(lambda e, bg=bg: e.activation(
                            dump[:, 0:512], bg.t[:], AF_.Silu,
                            scale=col(C_C1)), inc=sS)
                        sdone = sS.n
                        bg.freed = (sS, sdone)
                        qV.wait(sS, sdone)
                        qV.wait(sT, bu.ready)
                        qV.do(lambda e, bu=bu, ti=ti, s=s:
                              e.scalar_tensor_tensor(
                                  hid[ti][:, s * 512:(s + 1) * 512], bu.t[:],
                                  col(C_G1), dump[:, 0:512],
                                  ALU.mult, ALU.mult), inc=sV)
                        bu.freed = (sV, sV.n)
                        st["sil1"][0] = sV.n
                    st["wgu_free"][s % 2] = sT.n
                hid_ready = sV.n
                gateup_done = sT.n
                for ti in range(4):
                    qV.wait(sV, hid_ready)
                    qV.do(lambda e, ti=ti: e.reduce_max(
                        col(C_A2), hid[ti], AX.X, apply_absolute_value=True),
                        inc=sV, sw=True)
                    qV.do(lambda e: e.tensor_scalar(col(C_A2), col(C_A2), 1e-5,
                                                    None, ALU.max), inc=sV, sw=True)
                    qV.do(lambda e: e.reciprocal(col(C_SX), col(C_A2)), inc=sV, sw=True)
                    qV.do(lambda e: e.tensor_scalar(col(C_SX), col(C_SX), 127.0,
                                                    None, ALU.mult), inc=sV, sw=True)
                    qV.do(lambda e, ti=ti: e.tensor_scalar(
                        col(C_C2), col(C_A2), 0.5 / 127.0, None, ALU.mult),
                        inc=sV, sw=True)
                    qV.do(lambda e, ti=ti: e.tensor_tensor(
                        cols[:, C_SXC + 8 + ti:C_SXC + 9 + ti], col(C_C2),
                        col(C_CM["d"]), ALU.mult), inc=sV, sw=True)
                    qw = 1024
                    for qq in range(4):
                        ha = hid[ti][:, qq * qw:(qq + 1) * qw]
                        qV.do(lambda e, ha=ha: e.tensor_scalar(
                            xlnb[0], ha, col(C_SX), None, ALU.mult), inc=sV)
                        emit_round_ladder(xlnb[0], xq2[:, qq * qw:(qq + 1) * qw],
                                          tst, sgn, qw)
                    xq2T_this = [xq2T[kh] for kh in range(32)]
                    qT.wait(sT, gateup_done)
                    for g in range(8):
                        bk = bT[g % 2]
                        bk.claim(qT)
                        qT.wait(sV, sV.n)
                        for j in range(4):
                            kh = g * 4 + j
                            qT.do(lambda e, bk=bk, j=j, kh=kh: e.matmul(
                                bk.t[:, j * 128:(j + 1) * 128],
                                xq2[:, kh * 128:(kh + 1) * 128], identb[:],
                                is_transpose=True, start=(j == 0), stop=(j == 3)),
                                inc=sT if j == 3 else None)
                        bk.ready = sT.n
                        qV.wait(sT, bk.ready)
                        for j in range(4):
                            kh = g * 4 + j
                            qV.do(lambda e, bk=bk, j=j, kh=kh, ti=ti:
                                  e.tensor_copy(
                                      xq2T[kh][:, ti * 128:(ti + 1) * 128],
                                      bk.t[:, j * 128:(j + 1) * 128]),
                                  inc=sV if j == 3 else None)
                        bk.freed = (sV, sV.n)
                xq2T_ready = sV.n
                for kg in range(4):
                    qG.wait(sV, xq2T_ready if kg == 0 else 0)
                    qG.wait(sT, st["wd_free"][kg % 2])
                    qG.do(lambda e, kg=kg: e.dma_start(
                        out=wdb[kg % 2],
                        in_=wQ["d"][:, kg * 8 * D:(kg + 1) * 8 * D]),
                        inc=sWG, k=16)
                    wld = sWG.n
                    qT.wait(sWG, wld)
                    qT.wait(sV, xq2T_ready)
                    for ti in range(4):
                        t = sp * 4 + ti
                        for n in range(2):
                            b = bA[4 + (ti * 2 + n) % 2]
                            b.claim(qT)
                            for kk in range(8):
                                kh = kg * 8 + kk
                                qT.do(lambda e, kk=kk, kh=kh, ti=ti, n=n, b=b, kg=kg:
                                      e.matmul(
                                          b.t[:],
                                          xq2T[kh][:, ti * 128:(ti + 1) * 128],
                                          wdb[kg % 2][:, kk * D + n * 512:kk * D + (n + 1) * 512],
                                          start=(kk == 0), stop=(kk == 7)),
                                      inc=sT if kk == 7 else None)
                            b.ready = sT.n
                            qV.wait(sT, b.ready)
                            if kg == 0:
                                qV.do(lambda e, b=b, ti=ti, n=n: e.tensor_copy(
                                    y_acc[ti][:, n * 512:(n + 1) * 512], b.t[:]),
                                    inc=sV)
                            elif kg < 3:
                                ya = y_acc[ti][:, n * 512:(n + 1) * 512]
                                qV.do(lambda e, b=b, ya=ya: e.tensor_tensor(
                                    ya, ya, b.t[:], ALU.add), inc=sV)
                            else:
                                xa = x_sb[:, t * D + n * 512:t * D + (n + 1) * 512]
                                qV.wait(sV, st["dn_dump"])
                                qV.do(lambda e, b=b, ti=ti, n=n: e.tensor_tensor(
                                    dump[:, 0:512], b.t[:],
                                    y_acc[ti][:, n * 512:(n + 1) * 512], ALU.add),
                                    inc=sV)
                                qV.do(lambda e, xa=xa, ti=ti:
                                      e.scalar_tensor_tensor(
                                          xa, dump[:, 0:512],
                                          cols[:, C_SXC + 8 + ti:C_SXC + 9 + ti],
                                          xa, ALU.mult, ALU.add), inc=sV)
                                st["dn_dump"] = sV.n
                            b.freed = (sV, sV.n)
                    st["wd_free"][kg % 2] = sT.n
                st["xq_free_v"][0] = 0

        emit_l1_ffn.state = {
            "xln_free": [0, 0], "xq_free_v": [0, 0], "wgu_free": [0, 0],
            "wd_free": [0, 0], "sil1": [0, 0], "dn_dump": 0}
        emit_l1_prep.store_done = 0

        def emit_l1_attn():
            AB.reset()
            AF.reset()
            phase_guard()
            xlnT = [AB.take(T) for _ in range(KD)]
            xlnb16 = [AB.take(D), AB.take(D)]
            xln_free = [0, 0]
            for t in range(NT):
                qV.wait(sT, xln_free[t % 2])
                emit_ln_tile(t, xlnb16[t % 2])
                emit_transpose_tile(t, xlnb16[t % 2], xlnT, identb, bT)
                xln_free[t % 2] = sT.n
            xlnT_ready = sV.n

            v_sb = [AB.take(D) for _ in range(NT)]
            qt1 = AB.take(T)
            kt1 = AB.take(T)
            wv1b = AB.take(KD * 512)
            attnb = AB.at(0, T)
            atT = [AB.at(T + i * 128, 128) for i in range(NT)]
            qkbuf = AF.take(512)
            ub = AF.take(512)
            gb = AF.take(512)
            sgb = AF.take(512)
            aub = AF.take(512)
            linb = AF.take(512)
            phb = AF.take(512)

            for t4 in range(4):
                bA[t4].claim(qT)
            qT.wait(sV, xlnT_ready)
            for k in range(KD):
                for t4 in range(4):
                    qT.do(lambda e, k=k, t4=t4: e.matmul(
                        bA[t4].t[:], wqk_sb[:, k * 128:(k + 1) * 128],
                        xlnT[k][:, t4 * 512:(t4 + 1) * 512],
                        start=(k == 0), stop=(k == 7)),
                        inc=sT if (k == 7 and t4 == 3) else None)
            rdy = sT.n
            for t4 in range(4):
                qS.wait(sT, rdy)
                qS.wait(sV, sV.n)
                qS.do(lambda e, t4=t4: e.activation(
                    qkbuf, bA[t4].t[:], AF_.Silu, bias=bqk_sb[:, 0:1]), inc=sS)
                sdone = sS.n
                qV.wait(sS, sdone)
                qV.do(lambda e, t4=t4: e.tensor_scalar(
                    qt1[:, t4 * 512:(t4 + 1) * 512], qkbuf,
                    gb_sb[:, 0:1], gb_sb[:, 2:3], ALU.mult, ALU.add), inc=sV)
                qV.do(lambda e, t4=t4: e.tensor_scalar(
                    kt1[:, t4 * 512:(t4 + 1) * 512], qkbuf,
                    gb_sb[:, 1:2], gb_sb[:, 3:4], ALU.mult, ALU.add), inc=sV)
                bA[t4].freed = (sS, sdone)
            qk_ready = sV.n

            wv1_free = 0
            for n in range(2):
                qG.wait(sT, wv1_free)
                qG.do(lambda e, n=n: e.dma_start(
                    out=wv1b, in_=dt["wv1T"][:].rearrange(
                        "p (k f) -> p k f", k=KD)[:, :, n * 512:(n + 1) * 512]),
                    inc=sWG, k=16)
                wv_ld = sWG.n
                qT.wait(sWG, wv_ld)
                for t in range(NT):
                    b = bA[t % 4]
                    b.claim(qT)
                    for k in range(KD):
                        qT.do(lambda e, k=k, t=t, n=n, b=b: e.matmul(
                            b.t[:], xlnT[k][:, t * 128:(t + 1) * 128],
                            wv1b[:, k * 512:(k + 1) * 512],
                            start=(k == 0), stop=False))
                    qT.do(lambda e, n=n, b=b: e.matmul(
                        b.t[:], ones_r[:], bv1r[:, n * 512:(n + 1) * 512],
                        start=False, stop=True), inc=sT)
                    b.ready = sT.n
                    qS.wait(sT, b.ready)
                    qS.do(lambda e, b=b, t=t, n=n: e.activation(
                        v_sb[t][:, n * 512:(n + 1) * 512], b.t[:], AF_.Silu),
                        inc=sS)
                    b.freed = (sS, sS.n)
                wv1_free = sT.n
            v_ready = sS.n

            vproj_done = sT.n
            l1a_state = {"attnb_free": vproj_done, "atT_free": vproj_done}
            for a in range(NT):
                nj = a + 1
                qV.wait(sT, l1a_state["attnb_free"])
                for bi in range((nj * 128 + 511) // 512):
                    bw = min(512, nj * 128 - bi * 512)
                    bs = bA[bi % 2]
                    bs.claim(qT)
                    qT.wait(sV, qk_ready)
                    qT.do(lambda e, bs=bs, a=a, bi=bi, bw=bw: e.matmul(
                        bs.t[:, 0:bw], qt1[:, a * 128:(a + 1) * 128],
                        kt1[:, bi * 512:bi * 512 + bw], start=True, stop=True),
                        inc=sT)
                    bs.ready = sT.n
                    for jj in range(bw // 128):
                        jti = bi * 4 + jj
                        r = a - jti
                        off = jj * 128
                        qV.wait(sT, bs.ready)
                        qV.wait(sS, sS.n)
                        if r >= 2:
                            qS.wait(sT, bs.ready)
                            qS.wait(sV, sV.n)
                            qS.do(lambda e, bs=bs, off=off: e.activation(
                                ub[:, 0:128], bs.t[:, off:off + 128], AF_.Identity,
                                scale=ALPHA_L1, bias=bn_sb[:, 128:129]), inc=sS)
                            qS.do(lambda e, bs=bs, off=off: e.activation(
                                gb[:, 0:128], bs.t[:, off:off + 128], AF_.Gelu,
                                scale=ALPHA_L1, bias=bn_sb[:, 128:129]),
                                inc=sS, sw=True)
                            qS.do(lambda e: e.activation(
                                sgb[:, 0:128], ub[:, 0:128], AF_.Sign), inc=sS)
                            u_from = sS.n
                            qV.wait(sS, u_from)
                        else:
                            bnr = bn_sb[:, r * 128:(r + 1) * 128]
                            qV.do(lambda e, bs=bs, off=off, bnr=bnr:
                                  e.scalar_tensor_tensor(
                                      ub[:, 0:128], bs.t[:, off:off + 128],
                                      ALPHA_L1, bnr, ALU.mult, ALU.add), inc=sV, sw=True)
                            qS.wait(sV, sV.n)
                            qS.do(lambda e: e.activation(
                                gb[:, 0:128], ub[:, 0:128], AF_.Gelu), inc=sS)
                            qS.do(lambda e: e.activation(
                                sgb[:, 0:128], ub[:, 0:128], AF_.Sign), inc=sS)
                            qV.wait(sS, sS.n)
                        qV.do(lambda e: e.tensor_tensor(
                            aub[:, 0:128], ub[:, 0:128], sgb[:, 0:128],
                            ALU.mult), inc=sV, sw=True)
                        qV.do(lambda e: e.tensor_scalar(
                            aub[:, 128:256], aub[:, 0:128], 1e-20, None,
                            ALU.max), inc=sV, sw=True)
                        qV.do(lambda e: e.reciprocal(aub[:, 256:384],
                                                     aub[:, 128:256]), inc=sV, sw=True)
                        qV.do(lambda e: e.tensor_tensor(
                            phb[:, 0:128], gb[:, 0:128], aub[:, 256:384],
                            ALU.mult), inc=sV, sw=True)
                        qV.do(lambda e: e.tensor_tensor(
                            phb[:, 0:128], phb[:, 0:128], sgb[:, 0:128],
                            ALU.mult), inc=sV, sw=True)
                        qV.do(lambda e: e.tensor_scalar(
                            linb[:, 0:128], ub[:, 0:128], C0, 0.5,
                            ALU.mult, ALU.add), inc=sV, sw=True)
                        qV.do(lambda e: e.tensor_scalar(
                            msku8[:, 0:128], aub[:, 0:128], 1e-3, None,
                            ALU.is_lt), inc=sV, sw=True)
                        if r == 0:
                            qV.do(lambda e: e.select(
                                linb[:, 128:256], msku8[:, 0:128],
                                linb[:, 0:128], phb[:, 0:128]), inc=sV, sw=True)
                            qV.do(lambda e, jti=jti: e.tensor_tensor(
                                attnb[:, jti * 128:(jti + 1) * 128],
                                linb[:, 128:256], m01[:], ALU.mult), inc=sV, sw=True)
                        else:
                            qV.do(lambda e, jti=jti: e.select(
                                attnb[:, jti * 128:(jti + 1) * 128],
                                msku8[:, 0:128], linb[:, 0:128],
                                phb[:, 0:128]), inc=sV, sw=True)
                    bs.freed = (sV, sV.n)
                attn_ready = sV.n
                for g in range((nj + 3) // 4):
                    bk = bT[g % 2]
                    bk.claim(qT)
                    qT.wait(sV, attn_ready)
                    lim = min(4, nj - g * 4)
                    for j in range(lim):
                        jti = g * 4 + j
                        qT.do(lambda e, bk=bk, j=j, jti=jti, lim=lim: e.matmul(
                            bk.t[:, j * 128:(j + 1) * 128],
                            attnb[:, jti * 128:(jti + 1) * 128], identb[:],
                            is_transpose=True, start=(j == 0),
                            stop=(j == lim - 1)),
                            inc=sT if j == lim - 1 else None)
                    bk.ready = sT.n
                    l1a_state["attnb_free"] = bk.ready
                    qS.wait(sT, bk.ready)
                    qS.wait(sT, l1a_state["atT_free"])
                    for j in range(lim):
                        jti = g * 4 + j
                        qS.do(lambda e, bk=bk, j=j, jti=jti: e.copy(
                            atT[jti], bk.t[:, j * 128:(j + 1) * 128]),
                            inc=sS if j == lim - 1 else None)
                    bk.freed = (sS, sS.n)
                at_ready = sS.n
                qT.wait(sS, at_ready)
                qT.wait(sS, v_ready)
                b0 = bA[2]
                b1 = bA[3]
                b0.claim(qT)
                b1.claim(qT)
                for jti in range(nj):
                    for n, b in ((0, b0), (1, b1)):
                        qT.do(lambda e, jti=jti, n=n, b=b, nj=nj: e.matmul(
                            b.t[:], atT[jti],
                            v_sb[jti][:, n * 512:(n + 1) * 512],
                            start=(jti == 0), stop=(jti == nj - 1)),
                            inc=sT if (jti == nj - 1 and n == 1) else None)
                rdy = sT.n
                l1a_state["atT_free"] = rdy
                for n, b in ((0, b0), (1, b1)):
                    b.ready = rdy
                    qV.wait(sT, rdy)
                    xa = x_sb[:, a * D + n * 512:a * D + (n + 1) * 512]
                    qV.do(lambda e, b=b, xa=xa: e.tensor_tensor(
                        xa, xa, b.t[:], ALU.add), inc=sV)
                    b.freed = (sV, sV.n)

        STAGE_FN = {"l0f_a": emit_l0_ffn, "l0a": emit_l0_attn,
                    "l0f_b": emit_l0_ffn, "l1prep": emit_l1_prep,
                    "l1f_a": emit_l1_ffn, "l1a": emit_l1_attn,
                    "l1f_b": emit_l1_ffn}
        for st_name in stages:
            STAGE_FN[st_name]()

        qY.wait(sV, sV.n)
        qY.wait(sS, sS.n)
        qY.wait(sT, sT.n)
        for t in range(NT):
            qY.do(lambda e, t=t: e.dma_start(
                out=y_out[t], in_=x_sb[:, t * D:(t + 1) * D]), inc=sST, k=16)
        qY.wait(sST, sST.n)
        qY.wait(sWS, sWS.n)
        qG.wait(sWG, sWG.n)

        blk = ctx.enter_context(nc.Block())

        @blk.tensor
        def _(e):
            for f in qT.ops:
                f(e)

        @blk.vector
        def _(e):
            for f in qV.ops:
                f(e)

        @blk.scalar
        def _(e):
            for f in qS.ops:
                f(e)

        @blk.sync
        def _(e):
            for f in qY.ops:
                f(e)

        @blk.gpsimd
        def _(e):
            for f in qG.ops:
                f(e)

    return nc


def _host_consts():
    f32 = np.float32
    L = {}
    p = np.arange(128)[:, None]
    fq = np.arange(128)[None, :]
    L["identb"] = np.eye(128, dtype=ml_dtypes.bfloat16)
    L["identf"] = np.eye(128, dtype=f32)
    L["ones_row"] = np.ones((1, 128), f32)
    L["ones_col"] = np.ones((128, 1), f32)
    L["jrel"] = np.ascontiguousarray(
        np.broadcast_to(np.arange(384, dtype=f32)[None, :], (128, 384)))
    L["madd"] = np.where(fq <= p, 0.0, -1e30).astype(f32)
    L["m01"] = (fq <= p).astype(f32)
    L["negp"] = (-(np.arange(128, dtype=f32)[:, None]
                   + 128.0 * np.arange(3, dtype=f32)[None, :]))
    return L


def _host_weights(inputs):
    f32 = np.float32
    L = {}
    wq = np.asarray(inputs["L0_attn_Wq"], f32)
    wk = np.asarray(inputs["L0_attn_Wk"], f32)
    wv = np.asarray(inputs["L0_attn_Wv"], f32)
    wo = np.asarray(inputs["L0_attn_Wo"], f32)
    L["wqT"] = np.ascontiguousarray(
        wq.reshape(KD, 128, KD, 128).transpose(0, 3, 2, 1)).reshape(KD, 128, KD * 128)
    L["wkT"] = np.ascontiguousarray(
        wk.reshape(KD, 128, KD, 128).transpose(0, 3, 2, 1)).reshape(KD, 128, KD * 128)
    L["wvT"] = np.ascontiguousarray(
        wv.reshape(D, KD, 128).transpose(2, 1, 0)).reshape(128, KD * D)
    L["woT"] = np.ascontiguousarray(
        wo.reshape(D, KD, 128).transpose(2, 1, 0)).reshape(128, KD * D)
    L["bq_c"] = np.ascontiguousarray(
        np.asarray(inputs["L0_attn_bq"], f32).reshape(KD, 128, 1))
    L["bk_c"] = np.ascontiguousarray(
        np.asarray(inputs["L0_attn_bk"], f32).reshape(KD, 128, 1))
    L["bv_row"] = np.asarray(inputs["L0_attn_bv"], f32).reshape(1, D)
    L["bo_row"] = np.asarray(inputs["L0_attn_bo"], f32).reshape(1, D)

    w12 = np.asarray(inputs["L0_ffn_w12"], f32)
    t = w12.reshape(E, 2, 16, 128, KD, 128)
    L["w12P"] = np.ascontiguousarray(
        t.transpose(0, 5, 2, 4, 1, 3).reshape(E, 128, 16, 2048))
    w3 = np.asarray(inputs["L0_ffn_w3"], f32)
    L["w3T"] = np.ascontiguousarray(
        w3.reshape(E, D, 16, 128).transpose(0, 3, 2, 1)).reshape(E, 128, 16 * 1024)
    gwv = np.asarray(inputs["L0_ffn_gate_w"], f32)
    L["gwT"] = np.ascontiguousarray(
        gwv.reshape(E, KD, 128).transpose(2, 1, 0)).reshape(128, KD * E)

    wqk = np.asarray(inputs["L1_attn_wqk"], f32)
    L["wqk1"] = np.ascontiguousarray(
        wqk.reshape(KD, 128, 128).transpose(1, 0, 2)).reshape(128, KD * 128)
    L["bqk_c"] = np.asarray(inputs["L1_attn_bqk"], f32).reshape(128, 1)
    gm = np.asarray(inputs["L1_attn_gamma"], f32)
    bt = np.asarray(inputs["L1_attn_beta"], f32)
    L["gb_c"] = np.ascontiguousarray(
        np.stack([gm[0], gm[1], bt[0], bt[1]])[:, :, None])
    wv1 = np.asarray(inputs["L1_attn_wv"], f32)
    L["wv1T"] = np.ascontiguousarray(
        wv1.reshape(KD, 128, D).transpose(1, 0, 2)).reshape(128, KD * D)
    L["bv1_row"] = np.asarray(inputs["L1_attn_bv"], f32).reshape(1, D)

    tbl = np.asarray(inputs["L1_attn_rel_bias"], f32)
    pp = np.arange(128)[:, None]
    ff = np.arange(128)[None, :]
    braw = np.zeros((2, 128, 128), f32)
    for r in range(2):
        dist = 128 * r + pp - ff
        n = np.maximum(dist, 0)
        with np.errstate(divide="ignore", invalid="ignore"):
            vl = 16 + (np.log(np.maximum(n, 1).astype(np.float64) / 16.0)
                       / np.log(8.0) * 16).astype(np.int64)
        vl = np.minimum(vl, 31)
        bucket = np.where(n < 16, n, vl).astype(np.int64)
        braw[r] = tbl[bucket]
    L["braw"] = braw

    wg = np.asarray(inputs["L1_ffn_w_gate"], f32)
    wu = np.asarray(inputs["L1_ffn_w_up"], f32)
    wd = np.asarray(inputs["L1_ffn_w_down"], f32)
    L["wgT_raw"] = np.ascontiguousarray(
        wg.reshape(INNER, KD, 128).transpose(2, 1, 0)).reshape(128, KD * INNER)
    L["wuT_raw"] = np.ascontiguousarray(
        wu.reshape(INNER, KD, 128).transpose(2, 1, 0)).reshape(128, KD * INNER)
    L["wdT_raw"] = np.ascontiguousarray(
        wd.reshape(D, 32, 128).transpose(2, 1, 0)).reshape(128, 32 * D)
    L.update(_host_consts())
    return L


_NC_CACHE = {}


def kernel(**inputs):
    key = "all"
    if key not in _NC_CACHE:
        _NC_CACHE[key] = build_nc(ALL_STAGES)
    nc = _NC_CACHE[key]
    shared = _host_weights(inputs)
    x = np.asarray(inputs["L0_attn_x"], np.float32)
    in_maps = []
    for c in range(8):
        m = dict(shared)
        m["x_in"] = np.ascontiguousarray(x[c].reshape(NT, 128, D))
        in_maps.append(m)
    res = run_bass_kernel_spmd(nc, in_maps, core_ids=list(range(8)))
    out = np.stack([r["y"].reshape(T, D) for r in res.results])
    return out.astype(np.float32)


# version 03
# speedup vs baseline: 1.1002x; 1.1002x over previous
import contextlib
import numpy as np
import ml_dtypes

import concourse.bass as bass
import concourse.mybir as mybir
from concourse.bass_utils import run_bass_kernel_spmd

F32 = mybir.dt.float32
BF16 = mybir.dt.bfloat16
U8 = mybir.dt.uint8
AF_ = mybir.ActivationFunctionType
ALU = mybir.AluOpType
AX = mybir.AxisListType

T = 2048
D = 1024
NT = 16
KD = 8
E = 8
INNER = 4096
LN_EPS = 1e-5

MU = float(np.sqrt(0.5))
STD = float(np.sqrt(0.25 * np.pi))
RELSCALE = float(128 ** 0.5)
ALPHA_L1 = 1.0 / (2048.0 * STD)
C0 = float(1.0 / np.sqrt(2.0 * np.pi))

ALL_STAGES = ("l0f_a", "l0a", "l0f_b", "l1prep", "l1f_a", "l1a", "l1f_b")
KNOBS = {"l0f_halves": 2, "l0f_experts": 8, "l0f_gate": True,
         "l0f_h12": True, "l0f_down": True, "l0f_tr": True, "l0f_ln": True,
         "top2": True, "scmm": True}


class Sem:
    _next = 0

    def __init__(self, h):
        self.h = h
        self.n = 0
        self.i = Sem._next
        Sem._next += 1


class Q:
    def __init__(self, name):
        self.name = name
        self.ops = []
        self.lastwait = {}

    def wait(self, sem, val):
        if val <= 0:
            return
        if self.lastwait.get(sem.i, -1) >= val:
            return
        self.lastwait[sem.i] = val
        self.ops.append(lambda e, s=sem, v=val: e.wait_ge(s.h, v))

    def do(self, fn, inc=None, k=1, sw=False):
        if inc is None:
            self.ops.append(fn)
            return None
        inc.n += k
        self.ops.append(lambda e, f=fn, s=inc, kk=k: f(e).then_inc(s.h, kk))
        if sw:
            self.wait(inc, inc.n)
        return inc.n


class Arena:
    def __init__(self, tensor, n):
        self.t = tensor
        self.n = n
        self.off = 0
        self.hwm = 0

    def take(self, cols):
        o = self.off
        self.off += cols
        self.hwm = max(self.hwm, self.off)
        assert self.off <= self.n, f"arena overflow {self.off} > {self.n}"
        return self.t[:, o:o + cols]

    def at(self, off, cols):
        assert off + cols <= self.n
        self.hwm = max(self.hwm, off + cols)
        return self.t[:, off:off + cols]

    def reset(self):
        self.off = 0


class Bank:
    def __init__(self, tensor):
        self.t = tensor
        self.ready = 0
        self.freed = None

    def claim(self, qT):
        if self.freed is not None:
            qT.wait(self.freed[0], self.freed[1])


def build_nc(stages=ALL_STAGES, dbg=False):
    nc = bass.Bass("TRN2", target_bir_lowering=False,
                   detect_race_conditions=False)
    dt = {}

    def din(name, shape, dtype=F32):
        dt[name] = nc.dram_tensor(name, list(shape), dtype, kind="ExternalInput")
        return dt[name]

    din("x_in", (NT, 128, D))
    din("wqT", (KD, 128, KD * 128))
    din("wkT", (KD, 128, KD * 128))
    din("wvT", (128, KD * D))
    din("woT", (128, KD * D))
    din("bq_c", (KD, 128, 1))
    din("bk_c", (KD, 128, 1))
    din("bv_row", (1, D))
    din("bo_row", (1, D))
    din("w12P", (E, 128, 16, 2048))
    din("w3T", (E, 128, 16 * 1024))
    din("gwT", (128, KD * E))
    din("wqk1", (128, KD * 128))
    din("bqk_c", (128, 1))
    din("gb_c", (4, 128, 1))
    din("wv1T", (128, KD * D))
    din("bv1_row", (1, D))
    din("braw", (2, 128, 128))
    din("wgT_raw", (128, KD * INNER))
    din("wuT_raw", (128, KD * INNER))
    din("wdT_raw", (128, 32 * D))
    din("identb", (128, 128), BF16)
    din("identf", (128, 128))
    din("ones_row", (1, 128))
    din("ones_col", (128, 1))
    din("jrel", (128, 384))
    din("madd", (128, 128))
    din("m01", (128, 128))
    din("negp", (128, 3))

    y_out = nc.dram_tensor("y", [NT, 128, D], F32, kind="ExternalOutput")
    wQ = {
        "g": nc.dram_tensor("wgQ", [128, KD * INNER], BF16),
        "u": nc.dram_tensor("wuQ", [128, KD * INNER], BF16),
        "d": nc.dram_tensor("wdQ", [128, 32 * D], BF16),
    }

    with contextlib.ExitStack() as ctx:
        def sb(name, shape, dtype=F32):
            return ctx.enter_context(nc.sbuf_tensor(name, shape, dtype))

        x_sb = sb("x_sb", [128, NT * D])
        ABt = sb("AB", [128, 45568], BF16)
        AFt = sb("AF", [128, 8192])
        identb = sb("identb_s", [128, 128], BF16)
        identf = sb("identf_s", [128, 128])
        ones_r = sb("ones_s", [1, 128])
        ones_c = sb("onesc_s", [128, 1])
        jrel = sb("jrel_s", [128, 384])
        madd = sb("madd_s", [128, 128])
        m01 = sb("m01_s", [128, 128])
        negp = sb("negp_s", [128, 3])
        gw_sb = sb("gw_sb", [128, KD * E])
        bq_sb = sb("bq_sb", [128, KD])
        bk_sb = sb("bk_sb", [128, KD])
        bvr = sb("bvr_s", [1, D])
        bor = sb("bor_s", [1, D])
        bqk_sb = sb("bqk_sb", [128, 1])
        gb_sb = sb("gb_sb", [128, 4])
        bn_sb = sb("bn_sb", [128, 256])
        bv1r = sb("bv1r_s", [1, D])
        wqk_sb = sb("wqk_sb", [128, KD * 128], BF16)
        cols = sb("cols", [128, 64])
        tp_sb = sb("tp_sb", [128, NT * E])
        dump = sb("dump", [128, 512])
        msku8 = sb("msku8", [128, 128], U8)

        AB = Arena(ABt, 45568)
        AF = Arena(AFt, 8192)

        pA = [ctx.enter_context(nc.psum_tensor(f"pA{i}", [128, 512], F32))
              for i in range(6)]
        pTb = [ctx.enter_context(nc.psum_tensor(f"pT{i}", [128, 1024], BF16))
               for i in range(2)]
        bA = [Bank(p) for p in pA]
        bT = [Bank(p) for p in pTb]

        sWS = Sem(ctx.enter_context(nc.semaphore("sWS")))
        sWG = Sem(ctx.enter_context(nc.semaphore("sWG")))
        sT = Sem(ctx.enter_context(nc.semaphore("sT")))
        sV = Sem(ctx.enter_context(nc.semaphore("sV")))
        sS = Sem(ctx.enter_context(nc.semaphore("sS")))
        sST = Sem(ctx.enter_context(nc.semaphore("sST")))

        qT, qV, qS, qY, qG = Q("t"), Q("v"), Q("s"), Q("y"), Q("g")

        def DBG(name, ap, shape, dtype=F32):
            if not dbg:
                return
            if not isinstance(ap, bass.AP):
                ap = ap[:]
            dten = nc.dram_tensor(f"dbg_{name}", list(shape), dtype,
                                  kind="ExternalOutput")
            qY.wait(sV, sV.n)
            qY.wait(sS, sS.n)
            qY.wait(sT, sT.n)
            qY.do(lambda e, d=dten, a=ap: e.dma_start(out=d[:], in_=a),
                  inc=sST, k=16)
            barrier = sST.n
            for q in (qT, qV, qS, qG):
                q.wait(sST, barrier)

        for t in range(NT):
            qY.do(lambda e, t=t: e.dma_start(out=x_sb[:, t * D:(t + 1) * D],
                                             in_=dt["x_in"][t]), inc=sWS, k=16)
        for dst, src in [(identb, "identb"), (identf, "identf"),
                         (ones_r, "ones_row"), (ones_c, "ones_col"),
                         (jrel, "jrel"), (madd, "madd"), (m01, "m01"),
                         (negp, "negp"), (bvr, "bv_row"), (bor, "bo_row"),
                         (bqk_sb, "bqk_c"), (bv1r, "bv1_row"),
                         (gw_sb, "gwT")]:
            qY.do(lambda e, d=dst, s=src: e.dma_start(out=d[:], in_=dt[s][:]),
                  inc=sWS, k=16)
        for k in range(KD):
            qY.do(lambda e, k=k: e.dma_start(out=bq_sb[:, k:k + 1],
                                             in_=dt["bq_c"][k]), inc=sWS, k=16)
            qY.do(lambda e, k=k: e.dma_start(out=bk_sb[:, k:k + 1],
                                             in_=dt["bk_c"][k]), inc=sWS, k=16)
        for j in range(4):
            qY.do(lambda e, j=j: e.dma_start(out=gb_sb[:, j:j + 1],
                                             in_=dt["gb_c"][j]), inc=sWS, k=16)
        for r in range(2):
            qY.do(lambda e, r=r: e.dma_start(out=bn_sb[:, r * 128:(r + 1) * 128],
                                             in_=dt["braw"][r]), inc=sWS, k=16)
        qG.do(lambda e: e.dma_start(out=wqk_sb[:], in_=dt["wqk1"][:]),
              inc=sWG, k=16)
        LWS, LWG = sWS.n, sWG.n
        for q in (qT, qV, qS, qG, qY):
            q.wait(sWS, LWS)
        qT.wait(sWG, LWG)
        qV.wait(sWG, LWG)

        qV.do(lambda e: e.memset(cols[:, 29:30], -128.0), inc=sV)
        for r in range(2):
            qV.do(lambda e, r=r: e.tensor_scalar(
                bn_sb[:, r * 128:(r + 1) * 128], bn_sb[:, r * 128:(r + 1) * 128],
                RELSCALE / STD, -MU / STD, ALU.mult, ALU.add), inc=sV)

        C_M, C_V, C_R, C_SS, C_S1, C_A, C_SX, C_G1 = 0, 1, 2, 3, 4, 5, 6, 7
        C_D, C_RD, C_M1, C_M2, C_P2 = 8, 9, 10, 11, 12
        C_CM = {"g": 16, "u": 17, "d": 18}
        C_TP = {"g": 20, "u": 21, "d": 22}
        C_TN = {"g": 23, "u": 24, "d": 25}
        C_C1, C_C2, C_A2, C_NEGK = 26, 27, 28, 29
        C_SXC = 32

        def col(i):
            return cols[:, i:i + 1]

        def emit_ln_tile(t, dest):
            xt = x_sb[:, t * D:(t + 1) * D]
            qS.wait(sV, sV.n)
            qS.do(lambda e, xt=xt: e.activation(dump[:, 0:512], xt[:, 0:512],
                                                AF_.Square,
                                                accum_out=col(C_SS)), inc=sS)
            qS.do(lambda e, xt=xt: e.activation(dump[:, 0:512], xt[:, 512:1024],
                                                AF_.Square,
                                                accum_out=col(C_G1)), inc=sS)
            qV.do(lambda e, xt=xt: e.reduce_sum(col(C_S1), xt, AX.X), inc=sV, sw=True)
            qV.do(lambda e: e.tensor_scalar(col(C_M), col(C_S1), 1.0 / D, None,
                                            ALU.mult), inc=sV, sw=True)
            qV.wait(sS, sS.n)
            qV.do(lambda e: e.tensor_tensor(col(C_SS), col(C_SS), col(C_G1),
                                            ALU.add), inc=sV, sw=True)
            qV.do(lambda e: e.tensor_tensor(col(C_G1), col(C_M), col(C_M),
                                            ALU.mult), inc=sV, sw=True)
            qV.do(lambda e: e.tensor_scalar(col(C_V), col(C_SS), 1.0 / D, None,
                                            ALU.mult), inc=sV, sw=True)
            qV.do(lambda e: e.tensor_tensor(col(C_V), col(C_V), col(C_G1),
                                            ALU.subtract), inc=sV, sw=True)
            qV.do(lambda e: e.tensor_scalar(col(C_V), col(C_V), LN_EPS, None,
                                            ALU.add), inc=sV, sw=True)
            qS.wait(sV, sV.n)
            qS.do(lambda e: e.activation(col(C_G1), col(C_V), AF_.Sqrt), inc=sS)
            qV.wait(sS, sS.n)
            qV.do(lambda e: e.reciprocal(col(C_R), col(C_G1)), inc=sV, sw=True)
            qV.do(lambda e, xt=xt, dest=dest: e.tensor_scalar(
                dest, xt, col(C_M), col(C_R), ALU.subtract, ALU.mult), inc=sV)

        tr_free = {}

        def emit_transpose_tile(t, src, xlnT, ident, banks, out_cast_q=None):
            qT.wait(sV, sV.n)
            for g in range(2):
                bk = banks[g % 2]
                bk.claim(qT)
                for j in range(4):
                    k = g * 4 + j
                    qT.do(lambda e, bk=bk, j=j, k=k, s=src, ident=ident:
                          e.matmul(bk.t[:, j * 128:(j + 1) * 128],
                                   s[:, k * 128:(k + 1) * 128], ident[:],
                                   is_transpose=True, start=(j == 0),
                                   stop=(j == 3)),
                          inc=sT if j == 3 else None)
                bk.ready = sT.n
                qV.wait(sT, bk.ready)
                for j in range(4):
                    k = g * 4 + j
                    qV.do(lambda e, bk=bk, j=j, k=k, t=t, xlnT=xlnT:
                          e.tensor_copy(xlnT[k][:, t * 128:(t + 1) * 128],
                                        bk.t[:, j * 128:(j + 1) * 128]),
                          inc=sV if j == 3 else None)
                bk.freed = (sV, sV.n)

        gate_state = {"x32_free": 0}

        def emit_transpose_gate_tile(t, src, xlnT, x32buf, scbank):
            qT.wait(sV, sV.n)
            for g in range(2):
                bk = bA[4 + (g % 2)]
                bk.claim(qT)
                for j in range(4):
                    k = g * 4 + j
                    qT.do(lambda e, bk=bk, j=j, k=k, s=src:
                          e.matmul(bk.t[:, j * 128:(j + 1) * 128],
                                   s[:, k * 128:(k + 1) * 128], identf[:],
                                   is_transpose=True, start=(j == 0),
                                   stop=(j == 3)),
                          inc=sT if j == 3 else None)
                bk.ready = sT.n
                qV.wait(sT, bk.ready)
                for j in range(4):
                    k = g * 4 + j
                    qV.do(lambda e, bk=bk, j=j, k=k, t=t, xlnT=xlnT:
                          e.tensor_copy(xlnT[k][:, t * 128:(t + 1) * 128],
                                        bk.t[:, j * 128:(j + 1) * 128]),
                          inc=sV if j == 3 else None)
                qS.wait(sV, sV.n)
                qS.wait(sT, gate_state["x32_free"])
                for j in range(4):
                    qS.do(lambda e, bk=bk, j=j, g=g, x32buf=x32buf:
                          e.copy(x32buf[g][:, j * 128:(j + 1) * 128],
                                 bk.t[:, j * 128:(j + 1) * 128]),
                          inc=sS if j == 3 else None)
                bk.freed = (sS, sS.n)
                sdone = sS.n
                qT.wait(sS, sdone)
                for j in range(4 if KNOBS["scmm"] else 0):
                    k = g * 4 + j
                    qT.do(lambda e, j=j, k=k, g=g, x32buf=x32buf, scbank=scbank:
                          e.matmul(scbank.t[:, 0:8],
                                   x32buf[g][:, j * 128:(j + 1) * 128],
                                   gw_sb[:, k * 8:(k + 1) * 8],
                                   start=(k == 0), stop=(k == 7)),
                          inc=sT if k == 7 else None)
            scbank.ready = sT.n
            gate_state["x32_free"] = sT.n

        def emit_top2(t, sc8):
            qV.do(lambda e, s8=sc8: e.reduce_max(col(C_M1), s8, AX.X), inc=sV, sw=True)
            qV.do(lambda e, s8=sc8: e.tensor_scalar(
                dump[:, 0:8], s8, col(C_M1), None, ALU.is_equal), inc=sV, sw=True)
            qV.do(lambda e, s8=sc8: e.scalar_tensor_tensor(
                dump[:, 8:16], dump[:, 0:8], -1e30, s8, ALU.mult, ALU.add),
                inc=sV, sw=True)
            qV.do(lambda e: e.reduce_max(col(C_M2), dump[:, 8:16], AX.X), inc=sV, sw=True)
            qV.do(lambda e: e.tensor_tensor(col(C_P2), col(C_M2), col(C_M1),
                                            ALU.subtract), inc=sV, sw=True)
            qS.wait(sV, sV.n)
            qS.do(lambda e: e.activation(col(C_P2), col(C_P2), AF_.Sigmoid),
                  inc=sS)
            qV.wait(sS, sS.n)
            qV.do(lambda e: e.tensor_scalar(col(C_G1), col(C_P2), -0.5, 0.5,
                                            ALU.mult, ALU.add), inc=sV, sw=True)
            qV.do(lambda e: e.tensor_scalar(col(C_P2), col(C_P2), 0.5, None,
                                            ALU.mult), inc=sV, sw=True)
            qV.do(lambda e: e.tensor_scalar(dump[:, 8:16], dump[:, 8:16],
                                            col(C_M2), None, ALU.is_equal),
                  inc=sV, sw=True)
            qV.do(lambda e: e.tensor_scalar(dump[:, 0:8], dump[:, 0:8],
                                            col(C_G1), None, ALU.mult), inc=sV, sw=True)
            qV.do(lambda e, t=t: e.scalar_tensor_tensor(
                tp_sb[:, t * 8:(t + 1) * 8], dump[:, 8:16], col(C_P2),
                dump[:, 0:8], ALU.mult, ALU.add), inc=sV, sw=True)

        def emit_ladder2(jobs):
            snapV = []
            for (y, q, t, s) in jobs:
                qV.do(lambda e, y=y, t=t: e.tensor_scalar(
                    t, y, 128.5, None, ALU.add), inc=sV)
                snapV.append(sV.n)
            Cacc = 0.0
            for k in (128, 64, 32, 16, 8, 4, 2, 1):
                snapS = []
                for j, (y, q, t, s) in enumerate(jobs):
                    qS.wait(sV, snapV[j])
                    qS.do(lambda e, t=t, s=s:
                          e.activation(s, t, AF_.Sign, bias=col(C_NEGK)),
                          inc=sS)
                    snapS.append(sS.n)
                for j, (y, q, t, s) in enumerate(jobs):
                    qV.wait(sS, snapS[j])
                    qV.do(lambda e, t=t, s=s, k=k:
                          e.scalar_tensor_tensor(t, s, -float(k) / 2.0, t,
                                                 ALU.mult, ALU.add), inc=sV)
                    snapV[j] = sV.n
                Cacc += float(k) / 2.0
            for (y, q, t, s) in jobs:
                qV.do(lambda e, y=y, t=t: e.scalar_tensor_tensor(
                    t, t, -1.0, y, ALU.mult, ALU.add), inc=sV)
                qV.do(lambda e, q=q, t=t, C=Cacc: e.tensor_scalar(
                    q, t, 0.5 + C, None, ALU.add), inc=sV)

        def phase_guard():
            qG.wait(sT, sT.n)
            qG.wait(sV, sV.n)
            qG.wait(sS, sS.n)
            qY.wait(sT, sT.n)
            qY.wait(sV, sV.n)
            qY.wait(sS, sS.n)

        l0f_state = {"w3_free": 0, "w12_free": [0, 0]}

        def emit_l0_ffn():
            AB.reset()
            AF.reset()
            phase_guard()
            xlnT = [AB.take(T // 2) for _ in range(KD)]
            hid = [AB.take(1024) for _ in range(16)]
            w12b = [AB.take(2048), AB.take(2048)]
            w3b = AB.take(16 * 1024)
            xlnb = [AF.take(D), AF.take(D)]
            x32buf = [AF.take(512), AF.take(512)]
            scbuf = AF.take(128)
            xln_free = [0, 0]

            sil = [0, 0]
            for half in range(KNOBS["l0f_halves"]):
                for tt in range(8):
                    t = half * 8 + tt
                    qV.wait(sT, xln_free[t % 2])
                    if KNOBS["l0f_ln"]:
                        emit_ln_tile(t, xlnb[t % 2])
                    if not KNOBS["l0f_tr"]:
                        continue
                    if not KNOBS["l0f_gate"]:
                        emit_transpose_tile(tt, xlnb[t % 2], xlnT, identf,
                                            [bA[4], bA[5]])
                        xln_free[t % 2] = sT.n
                        continue
                    scb = bA[2 + t % 2]
                    scb.claim(qT)
                    emit_transpose_gate_tile(tt, xlnb[t % 2], xlnT, x32buf, scb)
                    xln_free[t % 2] = sT.n
                    qV.wait(sT, scb.ready)
                    qV.do(lambda e, t=t, scb=scb: e.tensor_copy(
                        scbuf[:, t % 8 * 8:(t % 8 + 1) * 8], scb.t[:, 0:8]),
                        inc=sV, sw=True)
                    scb.freed = (sV, sV.n)
                if half == 0:
                    DBG("xln0", xlnb[0], (128, D))
                    DBG("xlnT0", xlnT[0], (128, T // 2), BF16)
                    DBG("scbuf", scbuf[:, 0:64], (128, 64))
                for tt in range(8 if KNOBS["l0f_gate"] and KNOBS["l0f_tr"]
                                and KNOBS["top2"] else 0):
                    t = half * 8 + tt
                    emit_top2(t, scbuf[:, tt * 8:(tt + 1) * 8])
                if half == 0:
                    DBG("tp", tp_sb[:, 0:64], (128, 64))
                xlnT_ready = sV.n
                tp_ready = sV.n

                for ei in range(KNOBS["l0f_experts"]):
                    def w12_load(i):
                        qG.wait(sT, l0f_state["w12_free"][i % 2])
                        qG.do(lambda e, ei=ei, i=i: e.dma_start(
                            out=w12b[i % 2], in_=dt["w12P"][ei, :, i]),
                            inc=sWG, k=16)
                        return sWG.n

                    w12_ld = [0] * 16
                    w12_ld[0] = w12_load(0)
                    w12_ld[1] = w12_load(1)
                    qG.wait(sT, l0f_state["w3_free"])
                    qG.do(lambda e, ei=ei: e.dma_start(
                        out=w3b, in_=dt["w3T"][ei]), inc=sWG, k=16)
                    w3_ld = sWG.n

                    qT.wait(sV, xlnT_ready)
                    for i in range(16 if KNOBS["l0f_h12"] else 0):
                        qT.wait(sWG, w12_ld[i])
                        for t2 in range(2):
                            c0 = t2 * 512
                            bh1 = bA[t2 * 2]
                            bh2 = bA[t2 * 2 + 1]
                            bh1.claim(qT)
                            bh2.claim(qT)
                            for k in range(KD):
                                qT.do(lambda e, i=i, k=k, c0=c0, b=bh1:
                                      e.matmul(b.t[:],
                                               w12b[i % 2][:, k * 256:k * 256 + 128],
                                               xlnT[k][:, c0:c0 + 512],
                                               start=(k == 0), stop=(k == 7)),
                                      inc=sT if k == 7 else None)
                            bh1.ready = sT.n
                            for k in range(KD):
                                qT.do(lambda e, i=i, k=k, c0=c0, b=bh2:
                                      e.matmul(b.t[:],
                                               w12b[i % 2][:, k * 256 + 128:k * 256 + 256],
                                               xlnT[k][:, c0:c0 + 512],
                                               start=(k == 0), stop=(k == 7)),
                                      inc=sT if k == 7 else None)
                            bh2.ready = sT.n
                            qS.wait(sT, bh1.ready)
                            qS.wait(sV, sil[0])
                            qS.do(lambda e, b=bh1: e.activation(
                                dump[:, 0:512], b.t[:], AF_.Silu), inc=sS)
                            sdone = sS.n
                            bh1.freed = (sS, sdone)
                            qV.wait(sS, sdone)
                            qV.wait(sT, bh2.ready)
                            qV.do(lambda e, b=bh2, i=i, t2=t2:
                                  e.tensor_tensor(
                                      hid[i][:, t2 * 512:(t2 + 1) * 512],
                                      dump[:, 0:512], b.t[:], ALU.mult),
                                  inc=sV)
                            bh2.freed = (sV, sV.n)
                            sil[0] = sV.n
                        l0f_state["w12_free"][i % 2] = sT.n
                        if i + 2 < 16:
                            w12_ld[i + 2] = w12_load(i + 2)
                    hid_ready = sV.n

                    qT.wait(sWG, w3_ld)
                    qT.wait(sV, hid_ready)
                    qT.wait(sV, tp_ready)
                    for tt in range(8 if KNOBS["l0f_down"] else 0):
                        t_abs = half * 8 + tt
                        t2, tc = tt // 4, (tt % 4) * 128
                        for n in range(2):
                            b = bA[2 + (tt * 2 + n) % 4]
                            b.claim(qT)
                            for kh in range(16):
                                qT.do(lambda e, kh=kh, t2=t2, tc=tc, n=n, b=b:
                                      e.matmul(
                                          b.t[:],
                                          hid[kh][:, t2 * 512 + tc:t2 * 512 + tc + 128],
                                          w3b[:, kh * 1024 + n * 512:kh * 1024 + (n + 1) * 512],
                                          start=(kh == 0), stop=(kh == 15)),
                                      inc=sT if kh == 15 else None)
                            b.ready = sT.n
                            qV.wait(sT, b.ready)
                            xa = x_sb[:, t_abs * D + n * 512:t_abs * D + (n + 1) * 512]
                            qV.do(lambda e, b=b, xa=xa, t_abs=t_abs, ei=ei:
                                  e.scalar_tensor_tensor(
                                      xa, b.t[:],
                                      tp_sb[:, t_abs * 8 + ei:t_abs * 8 + ei + 1],
                                      xa, ALU.mult, ALU.add), inc=sV)
                            b.freed = (sV, sV.n)
                    l0f_state["w3_free"] = sT.n
                    if half == 0 and ei == 0:
                        DBG("hidall", ABt[:, 8192:8192 + 16384],
                            (128, 16384), BF16)
                        DBG("xafter0", x_sb[:, 0:D], (128, D))

        l0a_state = {"wqk_free": [0]}

        def emit_l0_attn():
            AB.reset()
            AF.reset()
            phase_guard()
            xlnT = [AB.take(T) for _ in range(KD)]
            xlnb16 = [AB.take(D), AB.take(D)]
            xln_free = [0, 0]
            for t in range(NT):
                qV.wait(sT, xln_free[t % 2])
                emit_ln_tile(t, xlnb16[t % 2])
                emit_transpose_tile(t, xlnb16[t % 2], xlnT, identb, bT)
                xln_free[t % 2] = sT.n
            xlnT_ready = sV.n

            xoff = 16384
            pnb = AB.at(xoff, 512)
            ptb = [AB.at(xoff + 512 + i * 128, 128) for i in range(3)]
            wq_b = AB.at(xoff + 896, KD * 128)
            at = [AB.take(T) for _ in range(KD)]
            voff = AB.off
            v_c = [[AB.at(voff + b_ * 2048 + t_ * 128, 128) for t_ in range(NT)]
                   for b_ in range(2)]
            AB.off += 4096
            qt = AB.take(T)
            kt = AB.take(T)
            wk_b = AB.take(KD * 128)
            wv_b = [AB.take(KD * 128)]
            wo_b = AB.at(voff, KD * D)
            prow = [AF.take(512), AF.take(512)]
            srow = [AF.take(384), AF.take(384)]
            ln_done = sT.n
            l0a_state["pn_free"] = ln_done
            l0a_state["pt_free"] = ln_done
            l0a_state["wqk_free"][0] = ln_done

            v_free = [0, 0]
            wv_free = 0
            for c in range(KD):
                qG.wait(sT, l0a_state["wqk_free"][0])
                qG.do(lambda e, c=c: e.dma_start(out=wq_b,
                                                 in_=dt["wqT"][c]), inc=sWG, k=16)
                qG.do(lambda e, c=c: e.dma_start(out=wk_b,
                                                 in_=dt["wkT"][c]), inc=sWG, k=16)
                qG.wait(sT, wv_free)
                qG.do(lambda e, c=c: e.dma_start(
                    out=wv_b[0],
                    in_=dt["wvT"][:].rearrange("p (k f) -> p k f", k=KD)
                    [:, :, c * 128:(c + 1) * 128]), inc=sWG, k=16)
                wqk_ld = sWG.n
                qT.wait(sWG, wqk_ld)
                qT.wait(sV, xlnT_ready)
                for t in range(NT):
                    b = bA[4 + t % 2]
                    b.claim(qT)
                    for k in range(KD):
                        qT.do(lambda e, k=k, t=t, b=b, c=c: e.matmul(
                            b.t[:, 0:128], xlnT[k][:, t * 128:(t + 1) * 128],
                            wv_b[0][:, k * 128:(k + 1) * 128],
                            start=(k == 0), stop=False))
                    qT.do(lambda e, b=b, c=c: e.matmul(
                        b.t[:, 0:128], ones_r[:],
                        bvr[:, c * 128:(c + 1) * 128],
                        start=False, stop=True), inc=sT)
                    b.ready = sT.n
                    qS.wait(sT, b.ready)
                    qS.wait(sT, v_free[c % 2])
                    qS.do(lambda e, b=b, t=t, c=c: e.copy(
                        v_c[c % 2][t], b.t[:, 0:128]), inc=sS)
                    b.freed = (sS, sS.n)
                wv_free = sT.n
                for wbuf, dst, bias in ((wq_b, qt, bq_sb),
                                        (wk_b, kt, bk_sb)):
                    for t4 in range(4):
                        bA[t4].claim(qT)
                    for k in range(KD):
                        for t4 in range(4):
                            qT.do(lambda e, k=k, t4=t4, wbuf=wbuf: e.matmul(
                                bA[t4].t[:], wbuf[:, k * 128:(k + 1) * 128],
                                xlnT[k][:, t4 * 512:(t4 + 1) * 512],
                                start=(k == 0), stop=(k == 7)),
                                inc=sT if (k == 7 and t4 == 3) else None)
                    rdy = sT.n
                    qS.wait(sT, rdy)
                    for t4 in range(4):
                        qS.do(lambda e, t4=t4, dst=dst, bias=bias, c=c:
                              e.activation(dst[:, t4 * 512:(t4 + 1) * 512],
                                           bA[t4].t[:], AF_.Identity,
                                           bias=bias[:, c:c + 1]),
                              inc=sS if t4 == 3 else None)
                    for t4 in range(4):
                        bA[t4].freed = (sS, sS.n)
                l0a_state["wqk_free"][0] = sT.n
                qk_ready = sS.n

                for hh in range(2):
                    h = 2 * c + hh
                    r0 = 64 * hh
                    for a in range(NT):
                        W = min(a, 2) + 1
                        j0 = 128 * (a - (W - 1))
                        wc = 128 * W
                        par = a % 2
                        cD = 8 + 2 * par
                        cRD = 9 + 2 * par
                        bs = bA[par]
                        bs.claim(qT)
                        qT.wait(sS, qk_ready)
                        pn_free = l0a_state.get("pn_free", 0)
                        pt_free = l0a_state.get("pt_free", 0)
                        qT.do(lambda e, bs=bs, r0=r0, a=a, j0=j0, wc=wc:
                              e.matmul(bs.t[:, 0:wc],
                                       qt[r0:r0 + 64, a * 128:(a + 1) * 128],
                                       kt[r0:r0 + 64, j0:j0 + wc],
                                       start=True, stop=True), inc=sT)
                        bs.ready = sT.n
                        qV.wait(sT, bs.ready)
                        qV.wait(sS, l0a_state.get(("exp", par), 0))
                        qV.do(lambda e, bs=bs, wc=wc, par=par:
                              e.scalar_tensor_tensor(
                                  srow[par][:, 0:wc], bs.t[:, 0:wc], 0.125,
                                  jrel[:, 0:wc], ALU.mult, ALU.add), inc=sV)
                        bs.freed = (sV, sV.n)
                        qV.do(lambda e, wc=wc, par=par: e.tensor_tensor(
                            srow[par][:, wc - 128:wc], srow[par][:, wc - 128:wc],
                            madd[:], ALU.add), inc=sV)
                        qS.wait(sV, sV.n)
                        qS.wait(sV, l0a_state.get(("pn", par), 0))
                        qS.do(lambda e, wc=wc, W=W, par=par, cD=cD:
                              e.activation(
                                  prow[par][:, 0:wc], srow[par][:, 0:wc],
                                  AF_.Exp, bias=negp[:, W - 1:W],
                                  accum_out=col(cD)), inc=sS)
                        l0a_state[("exp", par)] = sS.n
                        qV.wait(sS, sS.n)
                        qV.do(lambda e, cD=cD, cRD=cRD: e.reciprocal(
                            col(cRD), col(cD)), inc=sV, sw=True)
                        qV.wait(sT, pn_free)
                        qV.do(lambda e, wc=wc, par=par, cRD=cRD: e.tensor_scalar(
                            pnb[:, 0:wc], prow[par][:, 0:wc], col(cRD), None,
                            ALU.mult), inc=sV)
                        l0a_state[("pn", par)] = sV.n
                        pn_ready = sV.n
                        btr = bT[a % 2]
                        btr.claim(qT)
                        qT.wait(sV, pn_ready)
                        for w in range(W):
                            qT.do(lambda e, btr=btr, w=w, W=W: e.matmul(
                                btr.t[:, w * 128:(w + 1) * 128],
                                pnb[:, w * 128:(w + 1) * 128], identb[:],
                                is_transpose=True, start=(w == 0),
                                stop=(w == W - 1)),
                                inc=sT if w == W - 1 else None)
                        btr.ready = sT.n
                        l0a_state["pn_free"] = btr.ready
                        qS.wait(sT, btr.ready)
                        qS.wait(sT, pt_free)
                        for w in range(W):
                            qS.do(lambda e, btr=btr, w=w: e.copy(
                                ptb[w], btr.t[:, w * 128:(w + 1) * 128]),
                                inc=sS if w == W - 1 else None)
                        btr.freed = (sS, sS.n)
                        pt_ready = sS.n
                        bpv = bA[2 + a % 2]
                        bpv.claim(qT)
                        qT.wait(sS, pt_ready)
                        for w in range(W):
                            jt = a - (W - 1) + w
                            qT.do(lambda e, bpv=bpv, w=w, jt=jt, c=c, hh=hh,
                                  W=W: e.matmul(
                                      bpv.t[0:64, 0:128],
                                      v_c[c % 2][jt][:, hh * 64:hh * 64 + 64],
                                      ptb[w], start=(w == 0), stop=(w == W - 1)),
                                  inc=sT if w == W - 1 else None)
                        bpv.ready = sT.n
                        l0a_state["pt_free"] = bpv.ready
                        qV.wait(sT, bpv.ready)
                        qV.do(lambda e, bpv=bpv, c=c, r0=r0, a=a: e.tensor_copy(
                            at[c][r0:r0 + 64, a * 128:(a + 1) * 128],
                            bpv.t[0:64, 0:128]), inc=sV)
                        bpv.freed = (sV, sV.n)
                v_free[c % 2] = sT.n
            at_ready = sV.n

            qG.wait(sT, sT.n)
            qG.do(lambda e: e.dma_start(out=wo_b, in_=dt["woT"][:]),
                  inc=sWG, k=16)
            wo_ld = sWG.n
            qT.wait(sWG, wo_ld)
            qT.wait(sV, at_ready)
            for t in range(NT):
                for n in range(2):
                    b = bA[4 + (t * 2 + n) % 2]
                    b.claim(qT)
                    for k in range(KD):
                        qT.do(lambda e, k=k, t=t, n=n, b=b: e.matmul(
                            b.t[:], at[k][:, t * 128:(t + 1) * 128],
                            wo_b[:, k * D + n * 512:k * D + (n + 1) * 512],
                            start=(k == 0), stop=False))
                    qT.do(lambda e, n=n, b=b: e.matmul(
                        b.t[:], ones_r[:], bor[:, n * 512:(n + 1) * 512],
                        start=False, stop=True), inc=sT)
                    b.ready = sT.n
                    qV.wait(sT, b.ready)
                    xa = x_sb[:, t * D + n * 512:t * D + (n + 1) * 512]
                    qV.do(lambda e, b=b, xa=xa: e.tensor_tensor(
                        xa, xa, b.t[:], ALU.add), inc=sV)
                    b.freed = (sV, sV.n)

        def emit_l1_prep():
            AB.reset()
            AF.reset()
            phase_guard()
            stg = [AF.take(4096), AF.take(4096)]
            qa = AB.take(4096)
            qb = AB.take(4096)
            wqo = AB.take(4096)
            stg_free = [0, 0]
            store_done = sWS.n
            for wi, (name, nslices) in enumerate(
                    [("g", 8), ("u", 8), ("d", 8)]):
                raw = {"g": "wgT_raw", "u": "wuT_raw", "d": "wdT_raw"}[name]
                qV.do(lambda e, c=C_CM[name]: e.memset(col(c), 0.0), inc=sV, sw=True)
                for s in range(nslices):
                    qG.wait(sV, stg_free[s % 2])
                    qG.do(lambda e, s=s, raw=raw: e.dma_start(
                        out=stg[s % 2],
                        in_=dt[raw][:, s * 4096:(s + 1) * 4096]), inc=sWG, k=16)
                    ld = sWG.n
                    qV.wait(sWG, ld)
                    qV.do(lambda e, s=s: e.reduce_sum(
                        col(C_S1), stg[s % 2], AX.X, apply_absolute_value=True),
                        inc=sV, sw=True)
                    qV.do(lambda e, c=C_CM[name]: e.tensor_tensor(
                        col(c), col(c), col(C_S1), ALU.add), inc=sV, sw=True)
                    stg_free[s % 2] = sV.n
                b = bA[0]
                b.claim(qT)
                qT.wait(sV, sV.n)
                qT.do(lambda e, b=b, c=C_CM[name]: e.matmul(
                    b.t[0:1, 0:1], col(c), ones_c[:], start=True, stop=True),
                    inc=sT)
                b.ready = sT.n
                qV.wait(sT, b.ready)
                qV.do(lambda e, b=b: e.tensor_copy(dump[0:1, 0:1],
                                                   b.t[0:1, 0:1]), inc=sV, sw=True)
                b.freed = (sV, sV.n)
                b2 = bA[1]
                b2.claim(qT)
                qT.wait(sV, sV.n)
                qT.do(lambda e, b2=b2: e.matmul(
                    b2.t[:, 0:1], ones_r[:], dump[0:1, 0:1],
                    start=True, stop=True), inc=sT)
                b2.ready = sT.n
                qV.wait(sT, b2.ready)
                tot = 128.0 * nslices * 4096.0
                qV.do(lambda e, b2=b2, c=C_CM[name], tot=tot: e.tensor_scalar(
                    col(c), b2.t[:, 0:1], 1.0 / tot, None, ALU.mult), inc=sV, sw=True)
                b2.freed = (sV, sV.n)
                qV.do(lambda e, c=C_CM[name]: e.tensor_scalar(
                    col(c), col(c), 1e-5, None, ALU.max), inc=sV, sw=True)
                qV.do(lambda e, c=C_CM[name], p=C_TP[name]: e.tensor_scalar(
                    col(p), col(c), 0.5, None, ALU.mult), inc=sV, sw=True)
                qV.do(lambda e, c=C_CM[name], nn=C_TN[name]: e.tensor_scalar(
                    col(nn), col(c), -0.5, None, ALU.mult), inc=sV, sw=True)
                thr_ready = sV.n
                for s in range(nslices):
                    qG.wait(sV, stg_free[s % 2])
                    qG.do(lambda e, s=s, raw=raw: e.dma_start(
                        out=stg[s % 2],
                        in_=dt[raw][:, s * 4096:(s + 1) * 4096]), inc=sWG, k=16)
                    ld = sWG.n
                    qV.wait(sWG, ld)
                    qV.wait(sV, thr_ready)
                    qV.wait(sWS, store_done)
                    qV.do(lambda e, s=s, p=C_TP[name]: e.tensor_scalar(
                        qa, stg[s % 2], col(p), None, ALU.is_gt), inc=sV, sw=True)
                    qV.do(lambda e, s=s, nn=C_TN[name]: e.tensor_scalar(
                        qb, stg[s % 2], col(nn), None, ALU.is_lt), inc=sV, sw=True)
                    stg_free[s % 2] = sV.n
                    qV.do(lambda e: e.tensor_tensor(wqo, qa, qb, ALU.subtract),
                          inc=sV, sw=True)
                    wq_ready = sV.n
                    qY.wait(sV, wq_ready)
                    qY.do(lambda e, s=s, name=name: e.dma_start(
                        out=wQ[name][:, s * 4096:(s + 1) * 4096], in_=wqo),
                        inc=sWS, k=16)
                    store_done = sWS.n
            emit_l1_prep.store_done = store_done

        def emit_l1_ffn():
            AB.reset()
            AF.reset()
            phase_guard()
            xqT = [AB.take(512) for _ in range(KD)]
            xqb = [AB.take(D), AB.take(D)]
            offB = AB.off
            hid = [AB.at(offB + i * 4096, 4096) for i in range(4)]
            wdb = [AB.at(offB, 8 * 1024), AB.at(offB + 8 * 1024, 8 * 1024)]
            offC = offB + 4 * 4096
            wgb = [AB.at(offC, 4096), AB.at(offC + 4096, 4096)]
            wub = [AB.at(offC + 8192, 4096), AB.at(offC + 12288, 4096)]
            xq2T = [AB.at(offC + i * 512, 512) for i in range(32)]
            offX = offC + 4 * 4096
            xq2 = AB.at(offX, 4096)
            AB.hwm = max(AB.hwm, offX + 4096)

            xlnb = [AF.take(D), AF.take(D)]
            tst2 = [AF.take(D), AF.take(D)]
            y_acc = [AF.take(D) for _ in range(4)]
            sgn2 = [AB.at(offX + 4096, D), AB.at(offX + 5120, D)]
            AB.hwm = max(AB.hwm, offX + 6144)

            st = emit_l1_ffn.state
            for sp in range(4):
                for ti0 in (0, 2):
                    for jj in (0, 1):
                        ti = ti0 + jj
                        t = sp * 4 + ti
                        qV.wait(sT, st["xln_free"][jj])
                        emit_ln_tile(t, xlnb[jj])
                        qV.do(lambda e, x=xlnb[jj]: e.reduce_max(
                            col(C_A), x, AX.X, apply_absolute_value=True),
                            inc=sV, sw=True)
                        qV.do(lambda e: e.tensor_scalar(
                            col(C_A), col(C_A), 1e-5, None, ALU.max),
                            inc=sV, sw=True)
                        qV.do(lambda e: e.reciprocal(col(C_SX), col(C_A)),
                              inc=sV, sw=True)
                        qV.do(lambda e: e.tensor_scalar(
                            col(C_SX), col(C_SX), 127.0, None, ALU.mult),
                            inc=sV, sw=True)
                        qV.do(lambda e, x=xlnb[jj]: e.tensor_scalar(
                            x, x, col(C_SX), None, ALU.mult), inc=sV)
                        qV.do(lambda e, ti=ti: e.tensor_scalar(
                            cols[:, C_SXC + ti:C_SXC + ti + 1], col(C_A),
                            1.0 / 127.0, None, ALU.mult), inc=sV, sw=True)
                    emit_ladder2([(xlnb[0], xqb[0], tst2[0], sgn2[0]),
                                  (xlnb[1], xqb[1], tst2[1], sgn2[1])])
                    for jj in (0, 1):
                        emit_transpose_tile(ti0 + jj, xqb[jj], xqT, identb, bT)
                        st["xln_free"][jj] = sT.n
                xqT_ready = sV.n
                hid_ready = 0
                for s in range(8):
                    qG.wait(sT, st["wgu_free"][s % 2])
                    qG.wait(sT, st["wd_free"][1])
                    qG.wait(sWS, emit_l1_prep.store_done)
                    qG.do(lambda e, s=s: e.dma_start(
                        out=wgb[s % 2], in_=wQ["g"][:].rearrange(
                            "p (k o) -> p k o", k=KD)[:, :, s * 512:(s + 1) * 512]),
                        inc=sWG, k=16)
                    qG.do(lambda e, s=s: e.dma_start(
                        out=wub[s % 2], in_=wQ["u"][:].rearrange(
                            "p (k o) -> p k o", k=KD)[:, :, s * 512:(s + 1) * 512]),
                        inc=sWG, k=16)
                    wld = sWG.n
                    qT.wait(sWG, wld)
                    qT.wait(sV, xqT_ready)
                    for ti in range(4):
                        bg = bA[ti % 2]
                        bu = bA[2 + ti % 2]
                        bg.claim(qT)
                        bu.claim(qT)
                        for k in range(KD):
                            qT.do(lambda e, k=k, ti=ti, s=s, bg=bg: e.matmul(
                                bg.t[:], xqT[k][:, ti * 128:(ti + 1) * 128],
                                wgb[s % 2][:, k * 512:(k + 1) * 512],
                                start=(k == 0), stop=(k == 7)),
                                inc=sT if k == 7 else None)
                        bg.ready = sT.n
                        for k in range(KD):
                            qT.do(lambda e, k=k, ti=ti, s=s, bu=bu: e.matmul(
                                bu.t[:], xqT[k][:, ti * 128:(ti + 1) * 128],
                                wub[s % 2][:, k * 512:(k + 1) * 512],
                                start=(k == 0), stop=(k == 7)),
                                inc=sT if k == 7 else None)
                        bu.ready = sT.n
                        qV.do(lambda e, ti=ti: e.tensor_tensor(
                            col(C_C1), cols[:, C_SXC + ti:C_SXC + ti + 1],
                            col(C_CM["g"]), ALU.mult), inc=sV, sw=True)
                        qV.do(lambda e, ti=ti: e.tensor_tensor(
                            col(C_G1), cols[:, C_SXC + ti:C_SXC + ti + 1],
                            col(C_CM["u"]), ALU.mult), inc=sV, sw=True)
                        c_ready = sV.n
                        qS.wait(sT, bg.ready)
                        qS.wait(sV, c_ready)
                        qS.wait(sV, st["sil1"][0])
                        qS.do(lambda e, bg=bg: e.activation(
                            dump[:, 0:512], bg.t[:], AF_.Silu,
                            scale=col(C_C1)), inc=sS)
                        sdone = sS.n
                        bg.freed = (sS, sdone)
                        qV.wait(sS, sdone)
                        qV.wait(sT, bu.ready)
                        qV.do(lambda e, bu=bu, ti=ti, s=s:
                              e.scalar_tensor_tensor(
                                  hid[ti][:, s * 512:(s + 1) * 512], bu.t[:],
                                  col(C_G1), dump[:, 0:512],
                                  ALU.mult, ALU.mult), inc=sV)
                        bu.freed = (sV, sV.n)
                        st["sil1"][0] = sV.n
                    st["wgu_free"][s % 2] = sT.n
                hid_ready = sV.n
                gateup_done = sT.n
                for ti in range(4):
                    qV.wait(sV, hid_ready)
                    qV.do(lambda e, ti=ti: e.reduce_max(
                        col(C_A2), hid[ti], AX.X, apply_absolute_value=True),
                        inc=sV, sw=True)
                    qV.do(lambda e: e.tensor_scalar(col(C_A2), col(C_A2), 1e-5,
                                                    None, ALU.max), inc=sV, sw=True)
                    qV.do(lambda e: e.reciprocal(col(C_SX), col(C_A2)), inc=sV, sw=True)
                    qV.do(lambda e: e.tensor_scalar(col(C_SX), col(C_SX), 127.0,
                                                    None, ALU.mult), inc=sV, sw=True)
                    qV.do(lambda e, ti=ti: e.tensor_scalar(
                        col(C_C2), col(C_A2), 0.5 / 127.0, None, ALU.mult),
                        inc=sV, sw=True)
                    qV.do(lambda e, ti=ti: e.tensor_tensor(
                        cols[:, C_SXC + 8 + ti:C_SXC + 9 + ti], col(C_C2),
                        col(C_CM["d"]), ALU.mult), inc=sV, sw=True)
                    qw = 1024
                    for q0 in (0, 2):
                        for jj in (0, 1):
                            qq = q0 + jj
                            ha = hid[ti][:, qq * qw:(qq + 1) * qw]
                            qV.do(lambda e, ha=ha, jj=jj: e.tensor_scalar(
                                xlnb[jj], ha, col(C_SX), None, ALU.mult),
                                inc=sV)
                        emit_ladder2([
                            (xlnb[0], xq2[:, q0 * qw:(q0 + 1) * qw],
                             tst2[0], sgn2[0]),
                            (xlnb[1], xq2[:, (q0 + 1) * qw:(q0 + 2) * qw],
                             tst2[1], sgn2[1])])
                    xq2T_this = [xq2T[kh] for kh in range(32)]
                    qT.wait(sT, gateup_done)
                    for g in range(8):
                        bk = bT[g % 2]
                        bk.claim(qT)
                        qT.wait(sV, sV.n)
                        for j in range(4):
                            kh = g * 4 + j
                            qT.do(lambda e, bk=bk, j=j, kh=kh: e.matmul(
                                bk.t[:, j * 128:(j + 1) * 128],
                                xq2[:, kh * 128:(kh + 1) * 128], identb[:],
                                is_transpose=True, start=(j == 0), stop=(j == 3)),
                                inc=sT if j == 3 else None)
                        bk.ready = sT.n
                        qV.wait(sT, bk.ready)
                        for j in range(4):
                            kh = g * 4 + j
                            qV.do(lambda e, bk=bk, j=j, kh=kh, ti=ti:
                                  e.tensor_copy(
                                      xq2T[kh][:, ti * 128:(ti + 1) * 128],
                                      bk.t[:, j * 128:(j + 1) * 128]),
                                  inc=sV if j == 3 else None)
                        bk.freed = (sV, sV.n)
                xq2T_ready = sV.n
                for kg in range(4):
                    qG.wait(sV, xq2T_ready if kg == 0 else 0)
                    qG.wait(sT, st["wd_free"][kg % 2])
                    qG.do(lambda e, kg=kg: e.dma_start(
                        out=wdb[kg % 2],
                        in_=wQ["d"][:, kg * 8 * D:(kg + 1) * 8 * D]),
                        inc=sWG, k=16)
                    wld = sWG.n
                    qT.wait(sWG, wld)
                    qT.wait(sV, xq2T_ready)
                    for ti in range(4):
                        t = sp * 4 + ti
                        for n in range(2):
                            b = bA[4 + (ti * 2 + n) % 2]
                            b.claim(qT)
                            for kk in range(8):
                                kh = kg * 8 + kk
                                qT.do(lambda e, kk=kk, kh=kh, ti=ti, n=n, b=b, kg=kg:
                                      e.matmul(
                                          b.t[:],
                                          xq2T[kh][:, ti * 128:(ti + 1) * 128],
                                          wdb[kg % 2][:, kk * D + n * 512:kk * D + (n + 1) * 512],
                                          start=(kk == 0), stop=(kk == 7)),
                                      inc=sT if kk == 7 else None)
                            b.ready = sT.n
                            qV.wait(sT, b.ready)
                            if kg == 0:
                                qV.do(lambda e, b=b, ti=ti, n=n: e.tensor_copy(
                                    y_acc[ti][:, n * 512:(n + 1) * 512], b.t[:]),
                                    inc=sV)
                            elif kg < 3:
                                ya = y_acc[ti][:, n * 512:(n + 1) * 512]
                                qV.do(lambda e, b=b, ya=ya: e.tensor_tensor(
                                    ya, ya, b.t[:], ALU.add), inc=sV)
                            else:
                                xa = x_sb[:, t * D + n * 512:t * D + (n + 1) * 512]
                                qV.wait(sV, st["dn_dump"])
                                qV.do(lambda e, b=b, ti=ti, n=n: e.tensor_tensor(
                                    dump[:, 0:512], b.t[:],
                                    y_acc[ti][:, n * 512:(n + 1) * 512], ALU.add),
                                    inc=sV)
                                qV.do(lambda e, xa=xa, ti=ti:
                                      e.scalar_tensor_tensor(
                                          xa, dump[:, 0:512],
                                          cols[:, C_SXC + 8 + ti:C_SXC + 9 + ti],
                                          xa, ALU.mult, ALU.add), inc=sV)
                                st["dn_dump"] = sV.n
                            b.freed = (sV, sV.n)
                    st["wd_free"][kg % 2] = sT.n
                st["xq_free_v"][0] = 0

        emit_l1_ffn.state = {
            "xln_free": [0, 0], "xq_free_v": [0, 0], "wgu_free": [0, 0],
            "wd_free": [0, 0], "sil1": [0, 0], "dn_dump": 0}
        emit_l1_prep.store_done = 0

        def emit_l1_attn():
            AB.reset()
            AF.reset()
            phase_guard()
            xlnT = [AB.take(T) for _ in range(KD)]
            xlnb16 = [AB.take(D), AB.take(D)]
            xln_free = [0, 0]
            for t in range(NT):
                qV.wait(sT, xln_free[t % 2])
                emit_ln_tile(t, xlnb16[t % 2])
                emit_transpose_tile(t, xlnb16[t % 2], xlnT, identb, bT)
                xln_free[t % 2] = sT.n
            xlnT_ready = sV.n

            v_sb = [AB.take(D) for _ in range(NT)]
            qt1 = AB.take(T)
            kt1 = AB.take(T)
            wv1b = AB.take(KD * 512)
            attnb = AB.at(0, T)
            atT = [AB.at(T + i * 128, 128) for i in range(NT)]
            qkbuf = AF.take(512)
            ub = AF.take(512)
            gb = AF.take(512)
            sgb = AF.take(512)
            aub = AF.take(512)
            linb = AF.take(512)
            phb = AF.take(512)

            for t4 in range(4):
                bA[t4].claim(qT)
            qT.wait(sV, xlnT_ready)
            for k in range(KD):
                for t4 in range(4):
                    qT.do(lambda e, k=k, t4=t4: e.matmul(
                        bA[t4].t[:], wqk_sb[:, k * 128:(k + 1) * 128],
                        xlnT[k][:, t4 * 512:(t4 + 1) * 512],
                        start=(k == 0), stop=(k == 7)),
                        inc=sT if (k == 7 and t4 == 3) else None)
            rdy = sT.n
            for t4 in range(4):
                qS.wait(sT, rdy)
                qS.wait(sV, sV.n)
                qS.do(lambda e, t4=t4: e.activation(
                    qkbuf, bA[t4].t[:], AF_.Silu, bias=bqk_sb[:, 0:1]), inc=sS)
                sdone = sS.n
                qV.wait(sS, sdone)
                qV.do(lambda e, t4=t4: e.tensor_scalar(
                    qt1[:, t4 * 512:(t4 + 1) * 512], qkbuf,
                    gb_sb[:, 0:1], gb_sb[:, 2:3], ALU.mult, ALU.add), inc=sV)
                qV.do(lambda e, t4=t4: e.tensor_scalar(
                    kt1[:, t4 * 512:(t4 + 1) * 512], qkbuf,
                    gb_sb[:, 1:2], gb_sb[:, 3:4], ALU.mult, ALU.add), inc=sV)
                bA[t4].freed = (sS, sdone)
            qk_ready = sV.n

            wv1_free = 0
            for n in range(2):
                qG.wait(sT, wv1_free)
                qG.do(lambda e, n=n: e.dma_start(
                    out=wv1b, in_=dt["wv1T"][:].rearrange(
                        "p (k f) -> p k f", k=KD)[:, :, n * 512:(n + 1) * 512]),
                    inc=sWG, k=16)
                wv_ld = sWG.n
                qT.wait(sWG, wv_ld)
                for t in range(NT):
                    b = bA[t % 4]
                    b.claim(qT)
                    for k in range(KD):
                        qT.do(lambda e, k=k, t=t, n=n, b=b: e.matmul(
                            b.t[:], xlnT[k][:, t * 128:(t + 1) * 128],
                            wv1b[:, k * 512:(k + 1) * 512],
                            start=(k == 0), stop=False))
                    qT.do(lambda e, n=n, b=b: e.matmul(
                        b.t[:], ones_r[:], bv1r[:, n * 512:(n + 1) * 512],
                        start=False, stop=True), inc=sT)
                    b.ready = sT.n
                    qS.wait(sT, b.ready)
                    qS.do(lambda e, b=b, t=t, n=n: e.activation(
                        v_sb[t][:, n * 512:(n + 1) * 512], b.t[:], AF_.Silu),
                        inc=sS)
                    b.freed = (sS, sS.n)
                wv1_free = sT.n
            v_ready = sS.n

            vproj_done = sT.n
            l1a_state = {"attnb_free": vproj_done, "atT_free": vproj_done}
            for a in range(NT):
                nj = a + 1
                qV.wait(sT, l1a_state["attnb_free"])
                for bi in range((nj * 128 + 511) // 512):
                    bw = min(512, nj * 128 - bi * 512)
                    bs = bA[bi % 2]
                    bs.claim(qT)
                    qT.wait(sV, qk_ready)
                    qT.do(lambda e, bs=bs, a=a, bi=bi, bw=bw: e.matmul(
                        bs.t[:, 0:bw], qt1[:, a * 128:(a + 1) * 128],
                        kt1[:, bi * 512:bi * 512 + bw], start=True, stop=True),
                        inc=sT)
                    bs.ready = sT.n
                    for jj in range(bw // 128):
                        jti = bi * 4 + jj
                        r = a - jti
                        off = jj * 128
                        qV.wait(sT, bs.ready)
                        qV.wait(sS, sS.n)
                        if r >= 2:
                            qS.wait(sT, bs.ready)
                            qS.wait(sV, sV.n)
                            qS.do(lambda e, bs=bs, off=off: e.activation(
                                ub[:, 0:128], bs.t[:, off:off + 128], AF_.Identity,
                                scale=ALPHA_L1, bias=bn_sb[:, 128:129]), inc=sS)
                            qS.do(lambda e, bs=bs, off=off: e.activation(
                                gb[:, 0:128], bs.t[:, off:off + 128], AF_.Gelu,
                                scale=ALPHA_L1, bias=bn_sb[:, 128:129]),
                                inc=sS, sw=True)
                            qS.do(lambda e: e.activation(
                                sgb[:, 0:128], ub[:, 0:128], AF_.Sign), inc=sS)
                            u_from = sS.n
                            qV.wait(sS, u_from)
                        else:
                            bnr = bn_sb[:, r * 128:(r + 1) * 128]
                            qV.do(lambda e, bs=bs, off=off, bnr=bnr:
                                  e.scalar_tensor_tensor(
                                      ub[:, 0:128], bs.t[:, off:off + 128],
                                      ALPHA_L1, bnr, ALU.mult, ALU.add), inc=sV, sw=True)
                            qS.wait(sV, sV.n)
                            qS.do(lambda e: e.activation(
                                gb[:, 0:128], ub[:, 0:128], AF_.Gelu), inc=sS)
                            qS.do(lambda e: e.activation(
                                sgb[:, 0:128], ub[:, 0:128], AF_.Sign), inc=sS)
                            qV.wait(sS, sS.n)
                        qV.do(lambda e: e.tensor_tensor(
                            aub[:, 0:128], ub[:, 0:128], sgb[:, 0:128],
                            ALU.mult), inc=sV, sw=True)
                        qV.do(lambda e: e.tensor_scalar(
                            aub[:, 128:256], aub[:, 0:128], 1e-20, None,
                            ALU.max), inc=sV, sw=True)
                        qV.do(lambda e: e.reciprocal(aub[:, 256:384],
                                                     aub[:, 128:256]), inc=sV, sw=True)
                        qV.do(lambda e: e.tensor_tensor(
                            phb[:, 0:128], gb[:, 0:128], aub[:, 256:384],
                            ALU.mult), inc=sV, sw=True)
                        qV.do(lambda e: e.tensor_tensor(
                            phb[:, 0:128], phb[:, 0:128], sgb[:, 0:128],
                            ALU.mult), inc=sV, sw=True)
                        qV.do(lambda e: e.tensor_scalar(
                            linb[:, 0:128], ub[:, 0:128], C0, 0.5,
                            ALU.mult, ALU.add), inc=sV, sw=True)
                        qV.do(lambda e: e.tensor_scalar(
                            msku8[:, 0:128], aub[:, 0:128], 1e-3, None,
                            ALU.is_lt), inc=sV, sw=True)
                        if r == 0:
                            qV.do(lambda e: e.select(
                                linb[:, 128:256], msku8[:, 0:128],
                                linb[:, 0:128], phb[:, 0:128]), inc=sV, sw=True)
                            qV.do(lambda e, jti=jti: e.tensor_tensor(
                                attnb[:, jti * 128:(jti + 1) * 128],
                                linb[:, 128:256], m01[:], ALU.mult), inc=sV, sw=True)
                        else:
                            qV.do(lambda e, jti=jti: e.select(
                                attnb[:, jti * 128:(jti + 1) * 128],
                                msku8[:, 0:128], linb[:, 0:128],
                                phb[:, 0:128]), inc=sV, sw=True)
                    bs.freed = (sV, sV.n)
                attn_ready = sV.n
                for g in range((nj + 3) // 4):
                    bk = bT[g % 2]
                    bk.claim(qT)
                    qT.wait(sV, attn_ready)
                    lim = min(4, nj - g * 4)
                    for j in range(lim):
                        jti = g * 4 + j
                        qT.do(lambda e, bk=bk, j=j, jti=jti, lim=lim: e.matmul(
                            bk.t[:, j * 128:(j + 1) * 128],
                            attnb[:, jti * 128:(jti + 1) * 128], identb[:],
                            is_transpose=True, start=(j == 0),
                            stop=(j == lim - 1)),
                            inc=sT if j == lim - 1 else None)
                    bk.ready = sT.n
                    l1a_state["attnb_free"] = bk.ready
                    qS.wait(sT, bk.ready)
                    qS.wait(sT, l1a_state["atT_free"])
                    for j in range(lim):
                        jti = g * 4 + j
                        qS.do(lambda e, bk=bk, j=j, jti=jti: e.copy(
                            atT[jti], bk.t[:, j * 128:(j + 1) * 128]),
                            inc=sS if j == lim - 1 else None)
                    bk.freed = (sS, sS.n)
                at_ready = sS.n
                qT.wait(sS, at_ready)
                qT.wait(sS, v_ready)
                b0 = bA[2]
                b1 = bA[3]
                b0.claim(qT)
                b1.claim(qT)
                for jti in range(nj):
                    for n, b in ((0, b0), (1, b1)):
                        qT.do(lambda e, jti=jti, n=n, b=b, nj=nj: e.matmul(
                            b.t[:], atT[jti],
                            v_sb[jti][:, n * 512:(n + 1) * 512],
                            start=(jti == 0), stop=(jti == nj - 1)),
                            inc=sT if (jti == nj - 1 and n == 1) else None)
                rdy = sT.n
                l1a_state["atT_free"] = rdy
                for n, b in ((0, b0), (1, b1)):
                    b.ready = rdy
                    qV.wait(sT, rdy)
                    xa = x_sb[:, a * D + n * 512:a * D + (n + 1) * 512]
                    qV.do(lambda e, b=b, xa=xa: e.tensor_tensor(
                        xa, xa, b.t[:], ALU.add), inc=sV)
                    b.freed = (sV, sV.n)

        STAGE_FN = {"l0f_a": emit_l0_ffn, "l0a": emit_l0_attn,
                    "l0f_b": emit_l0_ffn, "l1prep": emit_l1_prep,
                    "l1f_a": emit_l1_ffn, "l1a": emit_l1_attn,
                    "l1f_b": emit_l1_ffn}
        for st_name in stages:
            STAGE_FN[st_name]()

        qY.wait(sV, sV.n)
        qY.wait(sS, sS.n)
        qY.wait(sT, sT.n)
        for t in range(NT):
            qY.do(lambda e, t=t: e.dma_start(
                out=y_out[t], in_=x_sb[:, t * D:(t + 1) * D]), inc=sST, k=16)
        qY.wait(sST, sST.n)
        qY.wait(sWS, sWS.n)
        qG.wait(sWG, sWG.n)

        blk = ctx.enter_context(nc.Block())

        @blk.tensor
        def _(e):
            for f in qT.ops:
                f(e)

        @blk.vector
        def _(e):
            for f in qV.ops:
                f(e)

        @blk.scalar
        def _(e):
            for f in qS.ops:
                f(e)

        @blk.sync
        def _(e):
            for f in qY.ops:
                f(e)

        @blk.gpsimd
        def _(e):
            for f in qG.ops:
                f(e)

    return nc


def _host_consts():
    f32 = np.float32
    L = {}
    p = np.arange(128)[:, None]
    fq = np.arange(128)[None, :]
    L["identb"] = np.eye(128, dtype=ml_dtypes.bfloat16)
    L["identf"] = np.eye(128, dtype=f32)
    L["ones_row"] = np.ones((1, 128), f32)
    L["ones_col"] = np.ones((128, 1), f32)
    L["jrel"] = np.ascontiguousarray(
        np.broadcast_to(np.arange(384, dtype=f32)[None, :], (128, 384)))
    L["madd"] = np.where(fq <= p, 0.0, -1e30).astype(f32)
    L["m01"] = (fq <= p).astype(f32)
    L["negp"] = (-(np.arange(128, dtype=f32)[:, None]
                   + 128.0 * np.arange(3, dtype=f32)[None, :]))
    return L


def _host_weights(inputs):
    f32 = np.float32
    L = {}
    wq = np.asarray(inputs["L0_attn_Wq"], f32)
    wk = np.asarray(inputs["L0_attn_Wk"], f32)
    wv = np.asarray(inputs["L0_attn_Wv"], f32)
    wo = np.asarray(inputs["L0_attn_Wo"], f32)
    L["wqT"] = np.ascontiguousarray(
        wq.reshape(KD, 128, KD, 128).transpose(0, 3, 2, 1)).reshape(KD, 128, KD * 128)
    L["wkT"] = np.ascontiguousarray(
        wk.reshape(KD, 128, KD, 128).transpose(0, 3, 2, 1)).reshape(KD, 128, KD * 128)
    L["wvT"] = np.ascontiguousarray(
        wv.reshape(D, KD, 128).transpose(2, 1, 0)).reshape(128, KD * D)
    L["woT"] = np.ascontiguousarray(
        wo.reshape(D, KD, 128).transpose(2, 1, 0)).reshape(128, KD * D)
    L["bq_c"] = np.ascontiguousarray(
        np.asarray(inputs["L0_attn_bq"], f32).reshape(KD, 128, 1))
    L["bk_c"] = np.ascontiguousarray(
        np.asarray(inputs["L0_attn_bk"], f32).reshape(KD, 128, 1))
    L["bv_row"] = np.asarray(inputs["L0_attn_bv"], f32).reshape(1, D)
    L["bo_row"] = np.asarray(inputs["L0_attn_bo"], f32).reshape(1, D)

    w12 = np.asarray(inputs["L0_ffn_w12"], f32)
    t = w12.reshape(E, 2, 16, 128, KD, 128)
    L["w12P"] = np.ascontiguousarray(
        t.transpose(0, 5, 2, 4, 1, 3).reshape(E, 128, 16, 2048))
    w3 = np.asarray(inputs["L0_ffn_w3"], f32)
    L["w3T"] = np.ascontiguousarray(
        w3.reshape(E, D, 16, 128).transpose(0, 3, 2, 1)).reshape(E, 128, 16 * 1024)
    gwv = np.asarray(inputs["L0_ffn_gate_w"], f32)
    L["gwT"] = np.ascontiguousarray(
        gwv.reshape(E, KD, 128).transpose(2, 1, 0)).reshape(128, KD * E)

    wqk = np.asarray(inputs["L1_attn_wqk"], f32)
    L["wqk1"] = np.ascontiguousarray(
        wqk.reshape(KD, 128, 128).transpose(1, 0, 2)).reshape(128, KD * 128)
    L["bqk_c"] = np.asarray(inputs["L1_attn_bqk"], f32).reshape(128, 1)
    gm = np.asarray(inputs["L1_attn_gamma"], f32)
    bt = np.asarray(inputs["L1_attn_beta"], f32)
    L["gb_c"] = np.ascontiguousarray(
        np.stack([gm[0], gm[1], bt[0], bt[1]])[:, :, None])
    wv1 = np.asarray(inputs["L1_attn_wv"], f32)
    L["wv1T"] = np.ascontiguousarray(
        wv1.reshape(KD, 128, D).transpose(1, 0, 2)).reshape(128, KD * D)
    L["bv1_row"] = np.asarray(inputs["L1_attn_bv"], f32).reshape(1, D)

    tbl = np.asarray(inputs["L1_attn_rel_bias"], f32)
    pp = np.arange(128)[:, None]
    ff = np.arange(128)[None, :]
    braw = np.zeros((2, 128, 128), f32)
    for r in range(2):
        dist = 128 * r + pp - ff
        n = np.maximum(dist, 0)
        with np.errstate(divide="ignore", invalid="ignore"):
            vl = 16 + (np.log(np.maximum(n, 1).astype(np.float64) / 16.0)
                       / np.log(8.0) * 16).astype(np.int64)
        vl = np.minimum(vl, 31)
        bucket = np.where(n < 16, n, vl).astype(np.int64)
        braw[r] = tbl[bucket]
    L["braw"] = braw

    wg = np.asarray(inputs["L1_ffn_w_gate"], f32)
    wu = np.asarray(inputs["L1_ffn_w_up"], f32)
    wd = np.asarray(inputs["L1_ffn_w_down"], f32)
    L["wgT_raw"] = np.ascontiguousarray(
        wg.reshape(INNER, KD, 128).transpose(2, 1, 0)).reshape(128, KD * INNER)
    L["wuT_raw"] = np.ascontiguousarray(
        wu.reshape(INNER, KD, 128).transpose(2, 1, 0)).reshape(128, KD * INNER)
    L["wdT_raw"] = np.ascontiguousarray(
        wd.reshape(D, 32, 128).transpose(2, 1, 0)).reshape(128, 32 * D)
    L.update(_host_consts())
    return L


_NC_CACHE = {}


def kernel(**inputs):
    key = "all"
    if key not in _NC_CACHE:
        _NC_CACHE[key] = build_nc(ALL_STAGES)
    nc = _NC_CACHE[key]
    shared = _host_weights(inputs)
    x = np.asarray(inputs["L0_attn_x"], np.float32)
    in_maps = []
    for c in range(8):
        m = dict(shared)
        m["x_in"] = np.ascontiguousarray(x[c].reshape(NT, 128, D))
        in_maps.append(m)
    res = run_bass_kernel_spmd(nc, in_maps, core_ids=list(range(8)))
    out = np.stack([r["y"].reshape(T, D) for r in res.results])
    return out.astype(np.float32)


# version 04
# speedup vs baseline: 1.1085x; 1.0076x over previous
import contextlib
import numpy as np
import ml_dtypes

import concourse.bass as bass
import concourse.mybir as mybir
from concourse.bass_utils import run_bass_kernel_spmd

F32 = mybir.dt.float32
BF16 = mybir.dt.bfloat16
U8 = mybir.dt.uint8
AF_ = mybir.ActivationFunctionType
ALU = mybir.AluOpType
AX = mybir.AxisListType

T = 2048
D = 1024
NT = 16
KD = 8
E = 8
INNER = 4096
LN_EPS = 1e-5

MU = float(np.sqrt(0.5))
STD = float(np.sqrt(0.25 * np.pi))
RELSCALE = float(128 ** 0.5)
ALPHA_L1 = 1.0 / (2048.0 * STD)
C0 = float(1.0 / np.sqrt(2.0 * np.pi))

ALL_STAGES = ("l0f_a", "l0a", "l0f_b", "l1prep", "l1f_a", "l1a", "l1f_b")
KNOBS = {"l0f_halves": 2, "l0f_experts": 8, "l0f_gate": True,
         "l0f_h12": True, "l0f_down": True, "l0f_tr": True, "l0f_ln": True,
         "top2": True, "scmm": True}


class Sem:
    _next = 0

    def __init__(self, h):
        self.h = h
        self.n = 0
        self.i = Sem._next
        Sem._next += 1


class Q:
    def __init__(self, name):
        self.name = name
        self.ops = []
        self.lastwait = {}

    def wait(self, sem, val):
        if val <= 0:
            return
        if self.lastwait.get(sem.i, -1) >= val:
            return
        self.lastwait[sem.i] = val
        self.ops.append(lambda e, s=sem, v=val: e.wait_ge(s.h, v))

    def do(self, fn, inc=None, k=1, sw=False):
        if inc is None:
            self.ops.append(fn)
            return None
        inc.n += k
        self.ops.append(lambda e, f=fn, s=inc, kk=k: f(e).then_inc(s.h, kk))
        if sw:
            self.wait(inc, inc.n)
        return inc.n


class Arena:
    def __init__(self, tensor, n):
        self.t = tensor
        self.n = n
        self.off = 0
        self.hwm = 0

    def take(self, cols):
        o = self.off
        self.off += cols
        self.hwm = max(self.hwm, self.off)
        assert self.off <= self.n, f"arena overflow {self.off} > {self.n}"
        return self.t[:, o:o + cols]

    def at(self, off, cols):
        assert off + cols <= self.n
        self.hwm = max(self.hwm, off + cols)
        return self.t[:, off:off + cols]

    def reset(self):
        self.off = 0


class Bank:
    def __init__(self, tensor):
        self.t = tensor
        self.ready = 0
        self.freed = None

    def claim(self, qT):
        if self.freed is not None:
            qT.wait(self.freed[0], self.freed[1])


def build_nc(stages=ALL_STAGES, dbg=False):
    nc = bass.Bass("TRN2", target_bir_lowering=False,
                   detect_race_conditions=False)
    dt = {}

    def din(name, shape, dtype=F32):
        dt[name] = nc.dram_tensor(name, list(shape), dtype, kind="ExternalInput")
        return dt[name]

    din("x_in", (NT, 128, D))
    din("wqT", (KD, 128, KD * 128))
    din("wkT", (KD, 128, KD * 128))
    din("wvT", (128, KD * D))
    din("woT", (128, KD * D))
    din("bq_c", (KD, 128, 1))
    din("bk_c", (KD, 128, 1))
    din("bv_row", (1, D))
    din("bo_row", (1, D))
    din("w12P", (E, 128, 16, 2048))
    din("w3T", (E, 128, 16 * 1024))
    din("gwT", (128, KD * E))
    din("wqk1", (128, KD * 128))
    din("bqk_c", (128, 1))
    din("gb_c", (4, 128, 1))
    din("wv1T", (128, KD * D))
    din("bv1_row", (1, D))
    din("braw", (2, 128, 128))
    din("wgT_raw", (128, KD * INNER))
    din("wuT_raw", (128, KD * INNER))
    din("wdT_raw", (128, 32 * D))
    din("identb", (128, 128), BF16)
    din("identf", (128, 128))
    din("ones_row", (1, 128))
    din("ones_col", (128, 1))
    din("jrel", (128, 384))
    din("madd", (128, 128))
    din("m01", (128, 128))
    din("negp", (128, 3))

    y_out = nc.dram_tensor("y", [NT, 128, D], F32, kind="ExternalOutput")
    wQ = {
        "g": nc.dram_tensor("wgQ", [128, KD * INNER], BF16),
        "u": nc.dram_tensor("wuQ", [128, KD * INNER], BF16),
        "d": nc.dram_tensor("wdQ", [128, 32 * D], BF16),
    }

    with contextlib.ExitStack() as ctx:
        def sb(name, shape, dtype=F32):
            return ctx.enter_context(nc.sbuf_tensor(name, shape, dtype))

        x_sb = sb("x_sb", [128, NT * D])
        ABt = sb("AB", [128, 45568], BF16)
        AFt = sb("AF", [128, 8192])
        identb = sb("identb_s", [128, 128], BF16)
        identf = sb("identf_s", [128, 128])
        ones_r = sb("ones_s", [1, 128])
        ones_c = sb("onesc_s", [128, 1])
        jrel = sb("jrel_s", [128, 384])
        madd = sb("madd_s", [128, 128])
        m01 = sb("m01_s", [128, 128])
        negp = sb("negp_s", [128, 3])
        gw_sb = sb("gw_sb", [128, KD * E])
        bq_sb = sb("bq_sb", [128, KD])
        bk_sb = sb("bk_sb", [128, KD])
        bvr = sb("bvr_s", [1, D])
        bor = sb("bor_s", [1, D])
        bqk_sb = sb("bqk_sb", [128, 1])
        gb_sb = sb("gb_sb", [128, 4])
        bn_sb = sb("bn_sb", [128, 256])
        bv1r = sb("bv1r_s", [1, D])
        wqk_sb = sb("wqk_sb", [128, KD * 128], BF16)
        cols = sb("cols", [128, 64])
        tp_sb = sb("tp_sb", [128, NT * E])
        dump = sb("dump", [128, 512])
        msku8 = sb("msku8", [128, 128], U8)

        AB = Arena(ABt, 45568)
        AF = Arena(AFt, 8192)

        pA = [ctx.enter_context(nc.psum_tensor(f"pA{i}", [128, 512], F32))
              for i in range(6)]
        pTb = [ctx.enter_context(nc.psum_tensor(f"pT{i}", [128, 1024], BF16))
               for i in range(2)]
        bA = [Bank(p) for p in pA]
        bT = [Bank(p) for p in pTb]

        sWS = Sem(ctx.enter_context(nc.semaphore("sWS")))
        sWG = Sem(ctx.enter_context(nc.semaphore("sWG")))
        sT = Sem(ctx.enter_context(nc.semaphore("sT")))
        sV = Sem(ctx.enter_context(nc.semaphore("sV")))
        sS = Sem(ctx.enter_context(nc.semaphore("sS")))
        sST = Sem(ctx.enter_context(nc.semaphore("sST")))

        qT, qV, qS, qY, qG = Q("t"), Q("v"), Q("s"), Q("y"), Q("g")

        def DBG(name, ap, shape, dtype=F32):
            if not dbg:
                return
            if not isinstance(ap, bass.AP):
                ap = ap[:]
            dten = nc.dram_tensor(f"dbg_{name}", list(shape), dtype,
                                  kind="ExternalOutput")
            qY.wait(sV, sV.n)
            qY.wait(sS, sS.n)
            qY.wait(sT, sT.n)
            qY.do(lambda e, d=dten, a=ap: e.dma_start(out=d[:], in_=a),
                  inc=sST, k=16)
            barrier = sST.n
            for q in (qT, qV, qS, qG):
                q.wait(sST, barrier)

        for t in range(NT):
            qY.do(lambda e, t=t: e.dma_start(out=x_sb[:, t * D:(t + 1) * D],
                                             in_=dt["x_in"][t]), inc=sWS, k=16)
        for dst, src in [(identb, "identb"), (identf, "identf"),
                         (ones_r, "ones_row"), (ones_c, "ones_col"),
                         (jrel, "jrel"), (madd, "madd"), (m01, "m01"),
                         (negp, "negp"), (bvr, "bv_row"), (bor, "bo_row"),
                         (bqk_sb, "bqk_c"), (bv1r, "bv1_row"),
                         (gw_sb, "gwT")]:
            qY.do(lambda e, d=dst, s=src: e.dma_start(out=d[:], in_=dt[s][:]),
                  inc=sWS, k=16)
        for k in range(KD):
            qY.do(lambda e, k=k: e.dma_start(out=bq_sb[:, k:k + 1],
                                             in_=dt["bq_c"][k]), inc=sWS, k=16)
            qY.do(lambda e, k=k: e.dma_start(out=bk_sb[:, k:k + 1],
                                             in_=dt["bk_c"][k]), inc=sWS, k=16)
        for j in range(4):
            qY.do(lambda e, j=j: e.dma_start(out=gb_sb[:, j:j + 1],
                                             in_=dt["gb_c"][j]), inc=sWS, k=16)
        for r in range(2):
            qY.do(lambda e, r=r: e.dma_start(out=bn_sb[:, r * 128:(r + 1) * 128],
                                             in_=dt["braw"][r]), inc=sWS, k=16)
        qG.do(lambda e: e.dma_start(out=wqk_sb[:], in_=dt["wqk1"][:]),
              inc=sWG, k=16)
        LWS, LWG = sWS.n, sWG.n
        for q in (qT, qV, qS, qG, qY):
            q.wait(sWS, LWS)
        qT.wait(sWG, LWG)
        qV.wait(sWG, LWG)

        qV.do(lambda e: e.memset(cols[:, 29:30], -128.0), inc=sV)
        for r in range(2):
            qV.do(lambda e, r=r: e.tensor_scalar(
                bn_sb[:, r * 128:(r + 1) * 128], bn_sb[:, r * 128:(r + 1) * 128],
                RELSCALE / STD, -MU / STD, ALU.mult, ALU.add), inc=sV)

        C_M, C_V, C_R, C_SS, C_S1, C_A, C_SX, C_G1 = 0, 1, 2, 3, 4, 5, 6, 7
        C_D, C_RD, C_M1, C_M2, C_P2 = 8, 9, 10, 11, 12
        C_CM = {"g": 16, "u": 17, "d": 18}
        C_TP = {"g": 20, "u": 21, "d": 22}
        C_TN = {"g": 23, "u": 24, "d": 25}
        C_C1, C_C2, C_A2, C_NEGK = 26, 27, 28, 29
        C_SXC = 32

        def col(i):
            return cols[:, i:i + 1]

        def emit_ln_tile(t, dest):
            xt = x_sb[:, t * D:(t + 1) * D]
            qS.wait(sV, sV.n)
            qS.do(lambda e, xt=xt: e.activation(dump[:, 0:512], xt[:, 0:512],
                                                AF_.Square,
                                                accum_out=col(C_SS)), inc=sS)
            qS.do(lambda e, xt=xt: e.activation(dump[:, 0:512], xt[:, 512:1024],
                                                AF_.Square,
                                                accum_out=col(C_G1)), inc=sS)
            qV.do(lambda e, xt=xt: e.reduce_sum(col(C_S1), xt, AX.X), inc=sV, sw=True)
            qV.do(lambda e: e.tensor_scalar(col(C_M), col(C_S1), 1.0 / D, None,
                                            ALU.mult), inc=sV, sw=True)
            qV.wait(sS, sS.n)
            qV.do(lambda e: e.tensor_tensor(col(C_SS), col(C_SS), col(C_G1),
                                            ALU.add), inc=sV, sw=True)
            qV.do(lambda e: e.tensor_tensor(col(C_G1), col(C_M), col(C_M),
                                            ALU.mult), inc=sV, sw=True)
            qV.do(lambda e: e.tensor_scalar(col(C_V), col(C_SS), 1.0 / D, None,
                                            ALU.mult), inc=sV, sw=True)
            qV.do(lambda e: e.tensor_tensor(col(C_V), col(C_V), col(C_G1),
                                            ALU.subtract), inc=sV, sw=True)
            qV.do(lambda e: e.tensor_scalar(col(C_V), col(C_V), LN_EPS, None,
                                            ALU.add), inc=sV, sw=True)
            qS.wait(sV, sV.n)
            qS.do(lambda e: e.activation(col(C_G1), col(C_V), AF_.Sqrt), inc=sS)
            qV.wait(sS, sS.n)
            qV.do(lambda e: e.reciprocal(col(C_R), col(C_G1)), inc=sV, sw=True)
            qV.do(lambda e, xt=xt, dest=dest: e.tensor_scalar(
                dest, xt, col(C_M), col(C_R), ALU.subtract, ALU.mult), inc=sV)

        tr_free = {}

        def emit_transpose_tile(t, src, xlnT, ident, banks, out_cast_q=None):
            qT.wait(sV, sV.n)
            for g in range(2):
                bk = banks[g % 2]
                bk.claim(qT)
                for j in range(4):
                    k = g * 4 + j
                    qT.do(lambda e, bk=bk, j=j, k=k, s=src, ident=ident:
                          e.matmul(bk.t[:, j * 128:(j + 1) * 128],
                                   s[:, k * 128:(k + 1) * 128], ident[:],
                                   is_transpose=True, start=(j == 0),
                                   stop=(j == 3)),
                          inc=sT if j == 3 else None)
                bk.ready = sT.n
                qV.wait(sT, bk.ready)
                for j in range(4):
                    k = g * 4 + j
                    qV.do(lambda e, bk=bk, j=j, k=k, t=t, xlnT=xlnT:
                          e.tensor_copy(xlnT[k][:, t * 128:(t + 1) * 128],
                                        bk.t[:, j * 128:(j + 1) * 128]),
                          inc=sV if j == 3 else None)
                bk.freed = (sV, sV.n)

        gate_state = {"x32_free": 0}

        def emit_transpose_gate_tile(t, src, xlnT, x32buf, scbank):
            qT.wait(sV, sV.n)
            for g in range(2):
                bk = bA[4 + (g % 2)]
                bk.claim(qT)
                for j in range(4):
                    k = g * 4 + j
                    qT.do(lambda e, bk=bk, j=j, k=k, s=src:
                          e.matmul(bk.t[:, j * 128:(j + 1) * 128],
                                   s[:, k * 128:(k + 1) * 128], identf[:],
                                   is_transpose=True, start=(j == 0),
                                   stop=(j == 3)),
                          inc=sT if j == 3 else None)
                bk.ready = sT.n
                qV.wait(sT, bk.ready)
                for j in range(4):
                    k = g * 4 + j
                    qV.do(lambda e, bk=bk, j=j, k=k, t=t, xlnT=xlnT:
                          e.tensor_copy(xlnT[k][:, t * 128:(t + 1) * 128],
                                        bk.t[:, j * 128:(j + 1) * 128]),
                          inc=sV if j == 3 else None)
                qS.wait(sV, sV.n)
                qS.wait(sT, gate_state["x32_free"])
                for j in range(4):
                    qS.do(lambda e, bk=bk, j=j, g=g, x32buf=x32buf:
                          e.copy(x32buf[g][:, j * 128:(j + 1) * 128],
                                 bk.t[:, j * 128:(j + 1) * 128]),
                          inc=sS if j == 3 else None)
                bk.freed = (sS, sS.n)
                sdone = sS.n
                qT.wait(sS, sdone)
                for j in range(4 if KNOBS["scmm"] else 0):
                    k = g * 4 + j
                    qT.do(lambda e, j=j, k=k, g=g, x32buf=x32buf, scbank=scbank:
                          e.matmul(scbank.t[:, 0:8],
                                   x32buf[g][:, j * 128:(j + 1) * 128],
                                   gw_sb[:, k * 8:(k + 1) * 8],
                                   start=(k == 0), stop=(k == 7)),
                          inc=sT if k == 7 else None)
            scbank.ready = sT.n
            gate_state["x32_free"] = sT.n

        def emit_top2(t, sc8):
            qV.do(lambda e, s8=sc8: e.reduce_max(col(C_M1), s8, AX.X), inc=sV, sw=True)
            qV.do(lambda e, s8=sc8: e.tensor_scalar(
                dump[:, 0:8], s8, col(C_M1), None, ALU.is_equal), inc=sV, sw=True)
            qV.do(lambda e, s8=sc8: e.scalar_tensor_tensor(
                dump[:, 8:16], dump[:, 0:8], -1e30, s8, ALU.mult, ALU.add),
                inc=sV, sw=True)
            qV.do(lambda e: e.reduce_max(col(C_M2), dump[:, 8:16], AX.X), inc=sV, sw=True)
            qV.do(lambda e: e.tensor_tensor(col(C_P2), col(C_M2), col(C_M1),
                                            ALU.subtract), inc=sV, sw=True)
            qS.wait(sV, sV.n)
            qS.do(lambda e: e.activation(col(C_P2), col(C_P2), AF_.Sigmoid),
                  inc=sS)
            qV.wait(sS, sS.n)
            qV.do(lambda e: e.tensor_scalar(col(C_G1), col(C_P2), -0.5, 0.5,
                                            ALU.mult, ALU.add), inc=sV, sw=True)
            qV.do(lambda e: e.tensor_scalar(col(C_P2), col(C_P2), 0.5, None,
                                            ALU.mult), inc=sV, sw=True)
            qV.do(lambda e: e.tensor_scalar(dump[:, 8:16], dump[:, 8:16],
                                            col(C_M2), None, ALU.is_equal),
                  inc=sV, sw=True)
            qV.do(lambda e: e.tensor_scalar(dump[:, 0:8], dump[:, 0:8],
                                            col(C_G1), None, ALU.mult), inc=sV, sw=True)
            qV.do(lambda e, t=t: e.scalar_tensor_tensor(
                tp_sb[:, t * 8:(t + 1) * 8], dump[:, 8:16], col(C_P2),
                dump[:, 0:8], ALU.mult, ALU.add), inc=sV, sw=True)

        def emit_ladder2(jobs):
            snapV = []
            for (y, q, t, s) in jobs:
                qV.do(lambda e, y=y, t=t: e.tensor_scalar(
                    t, y, 128.5, None, ALU.add), inc=sV)
                snapV.append(sV.n)
            Cacc = 0.0
            for k in (128, 64, 32, 16, 8, 4, 2, 1):
                snapS = []
                for j, (y, q, t, s) in enumerate(jobs):
                    qS.wait(sV, snapV[j])
                    qS.do(lambda e, t=t, s=s:
                          e.activation(s, t, AF_.Sign, bias=col(C_NEGK)),
                          inc=sS)
                    snapS.append(sS.n)
                for j, (y, q, t, s) in enumerate(jobs):
                    qV.wait(sS, snapS[j])
                    qV.do(lambda e, t=t, s=s, k=k:
                          e.scalar_tensor_tensor(t, s, -float(k) / 2.0, t,
                                                 ALU.mult, ALU.add), inc=sV)
                    snapV[j] = sV.n
                Cacc += float(k) / 2.0
            for (y, q, t, s) in jobs:
                qV.do(lambda e, y=y, t=t: e.scalar_tensor_tensor(
                    t, t, -1.0, y, ALU.mult, ALU.add), inc=sV)
                qV.do(lambda e, q=q, t=t, C=Cacc: e.tensor_scalar(
                    q, t, 0.5 + C, None, ALU.add), inc=sV)

        def phase_guard():
            qG.wait(sT, sT.n)
            qG.wait(sV, sV.n)
            qG.wait(sS, sS.n)
            qY.wait(sT, sT.n)
            qY.wait(sV, sV.n)
            qY.wait(sS, sS.n)

        l0f_state = {"w3_free": 0, "w12_free": [0, 0]}

        def emit_l0_ffn():
            AB.reset()
            AF.reset()
            phase_guard()
            xlnT = [AB.take(T // 2) for _ in range(KD)]
            hid = [AB.take(1024) for _ in range(16)]
            w12b = [AB.take(2048), AB.take(2048)]
            w3b = AB.take(16 * 1024)
            xlnb = [AF.take(D), AF.take(D)]
            x32buf = [AF.take(512), AF.take(512)]
            scbuf = AF.take(128)
            xln_free = [0, 0]

            sil = [0, 0]
            for half in range(KNOBS["l0f_halves"]):
                for tt in range(8):
                    t = half * 8 + tt
                    qV.wait(sT, xln_free[t % 2])
                    if KNOBS["l0f_ln"]:
                        emit_ln_tile(t, xlnb[t % 2])
                    if not KNOBS["l0f_tr"]:
                        continue
                    if not KNOBS["l0f_gate"]:
                        emit_transpose_tile(tt, xlnb[t % 2], xlnT, identf,
                                            [bA[4], bA[5]])
                        xln_free[t % 2] = sT.n
                        continue
                    scb = bA[2 + t % 2]
                    scb.claim(qT)
                    emit_transpose_gate_tile(tt, xlnb[t % 2], xlnT, x32buf, scb)
                    xln_free[t % 2] = sT.n
                    qV.wait(sT, scb.ready)
                    qV.do(lambda e, t=t, scb=scb: e.tensor_copy(
                        scbuf[:, t % 8 * 8:(t % 8 + 1) * 8], scb.t[:, 0:8]),
                        inc=sV, sw=True)
                    scb.freed = (sV, sV.n)
                if half == 0:
                    DBG("xln0", xlnb[0], (128, D))
                    DBG("xlnT0", xlnT[0], (128, T // 2), BF16)
                    DBG("scbuf", scbuf[:, 0:64], (128, 64))
                for tt in range(8 if KNOBS["l0f_gate"] and KNOBS["l0f_tr"]
                                and KNOBS["top2"] else 0):
                    t = half * 8 + tt
                    emit_top2(t, scbuf[:, tt * 8:(tt + 1) * 8])
                if half == 0:
                    DBG("tp", tp_sb[:, 0:64], (128, 64))
                xlnT_ready = sV.n
                tp_ready = sV.n

                for ei in range(KNOBS["l0f_experts"]):
                    def w12_load(i):
                        qG.wait(sT, l0f_state["w12_free"][i % 2])
                        qG.do(lambda e, ei=ei, i=i: e.dma_start(
                            out=w12b[i % 2], in_=dt["w12P"][ei, :, i]),
                            inc=sWG, k=16)
                        return sWG.n

                    w12_ld = [0] * 16
                    w12_ld[0] = w12_load(0)
                    w12_ld[1] = w12_load(1)

                    qT.wait(sV, xlnT_ready)
                    for i in range(16 if KNOBS["l0f_h12"] else 0):
                        qT.wait(sWG, w12_ld[i])
                        for t2 in range(2):
                            c0 = t2 * 512
                            bh1 = bA[t2 * 2]
                            bh2 = bA[t2 * 2 + 1]
                            bh1.claim(qT)
                            bh2.claim(qT)
                            for k in range(KD):
                                qT.do(lambda e, i=i, k=k, c0=c0, b=bh1:
                                      e.matmul(b.t[:],
                                               w12b[i % 2][:, k * 256:k * 256 + 128],
                                               xlnT[k][:, c0:c0 + 512],
                                               start=(k == 0), stop=(k == 7)),
                                      inc=sT if k == 7 else None)
                            bh1.ready = sT.n
                            for k in range(KD):
                                qT.do(lambda e, i=i, k=k, c0=c0, b=bh2:
                                      e.matmul(b.t[:],
                                               w12b[i % 2][:, k * 256 + 128:k * 256 + 256],
                                               xlnT[k][:, c0:c0 + 512],
                                               start=(k == 0), stop=(k == 7)),
                                      inc=sT if k == 7 else None)
                            bh2.ready = sT.n
                            qS.wait(sT, bh1.ready)
                            qS.wait(sV, sil[0])
                            qS.do(lambda e, b=bh1: e.activation(
                                dump[:, 0:512], b.t[:], AF_.Silu), inc=sS)
                            sdone = sS.n
                            bh1.freed = (sS, sdone)
                            qV.wait(sS, sdone)
                            qV.wait(sT, bh2.ready)
                            qV.do(lambda e, b=bh2, i=i, t2=t2:
                                  e.tensor_tensor(
                                      hid[i][:, t2 * 512:(t2 + 1) * 512],
                                      dump[:, 0:512], b.t[:], ALU.mult),
                                  inc=sV)
                            bh2.freed = (sV, sV.n)
                            sil[0] = sV.n
                        l0f_state["w12_free"][i % 2] = sT.n
                        if i + 2 < 16:
                            w12_ld[i + 2] = w12_load(i + 2)
                        if i in (4, 8):
                            half_i = 0 if i == 4 else 1
                            qG.wait(sT, l0f_state["w3_free"])
                            qG.do(lambda e, ei=ei, h=half_i: e.dma_start(
                                out=w3b[:, h * 8192:(h + 1) * 8192],
                                in_=dt["w3T"][ei][:, h * 8192:(h + 1) * 8192]),
                                inc=sWG, k=16)
                            w3_ld = sWG.n
                    hid_ready = sV.n

                    qT.wait(sWG, w3_ld)
                    qT.wait(sV, hid_ready)
                    qT.wait(sV, tp_ready)
                    for tt in range(8 if KNOBS["l0f_down"] else 0):
                        t_abs = half * 8 + tt
                        t2, tc = tt // 4, (tt % 4) * 128
                        for n in range(2):
                            b = bA[2 + (tt * 2 + n) % 4]
                            b.claim(qT)
                            for kh in range(16):
                                qT.do(lambda e, kh=kh, t2=t2, tc=tc, n=n, b=b:
                                      e.matmul(
                                          b.t[:],
                                          hid[kh][:, t2 * 512 + tc:t2 * 512 + tc + 128],
                                          w3b[:, kh * 1024 + n * 512:kh * 1024 + (n + 1) * 512],
                                          start=(kh == 0), stop=(kh == 15)),
                                      inc=sT if kh == 15 else None)
                            b.ready = sT.n
                            qV.wait(sT, b.ready)
                            xa = x_sb[:, t_abs * D + n * 512:t_abs * D + (n + 1) * 512]
                            qV.do(lambda e, b=b, xa=xa, t_abs=t_abs, ei=ei:
                                  e.scalar_tensor_tensor(
                                      xa, b.t[:],
                                      tp_sb[:, t_abs * 8 + ei:t_abs * 8 + ei + 1],
                                      xa, ALU.mult, ALU.add), inc=sV)
                            b.freed = (sV, sV.n)
                    l0f_state["w3_free"] = sT.n
                    if half == 0 and ei == 0:
                        DBG("hidall", ABt[:, 8192:8192 + 16384],
                            (128, 16384), BF16)
                        DBG("xafter0", x_sb[:, 0:D], (128, D))

        l0a_state = {"wqk_free": [0]}

        def emit_l0_attn():
            AB.reset()
            AF.reset()
            phase_guard()
            xlnT = [AB.take(T) for _ in range(KD)]
            xlnb16 = [AB.take(D), AB.take(D)]
            xln_free = [0, 0]
            for t in range(NT):
                qV.wait(sT, xln_free[t % 2])
                emit_ln_tile(t, xlnb16[t % 2])
                emit_transpose_tile(t, xlnb16[t % 2], xlnT, identb, bT)
                xln_free[t % 2] = sT.n
            xlnT_ready = sV.n

            xoff = 16384
            pnb = AB.at(xoff, 512)
            ptb = [AB.at(xoff + 512 + i * 128, 128) for i in range(3)]
            wq_b = AB.at(xoff + 896, KD * 128)
            at = [AB.take(T) for _ in range(KD)]
            voff = AB.off
            v_c = [[AB.at(voff + b_ * 2048 + t_ * 128, 128) for t_ in range(NT)]
                   for b_ in range(2)]
            AB.off += 4096
            qt = AB.take(T)
            kt = AB.take(T)
            wk_b = AB.take(KD * 128)
            wv_b = [AB.take(KD * 128)]
            wo_b = AB.at(voff, KD * D)
            prow = [AF.take(512), AF.take(512)]
            srow = [AF.take(384), AF.take(384)]
            ln_done = sT.n
            l0a_state["pn_free"] = ln_done
            l0a_state["pt_free"] = ln_done
            l0a_state["wqk_free"][0] = ln_done

            v_free = [0, 0]
            wv_free = 0
            for c in range(KD):
                qG.wait(sT, l0a_state["wqk_free"][0])
                qG.do(lambda e, c=c: e.dma_start(out=wq_b,
                                                 in_=dt["wqT"][c]), inc=sWG, k=16)
                qG.do(lambda e, c=c: e.dma_start(out=wk_b,
                                                 in_=dt["wkT"][c]), inc=sWG, k=16)
                qG.wait(sT, wv_free)
                qG.do(lambda e, c=c: e.dma_start(
                    out=wv_b[0],
                    in_=dt["wvT"][:].rearrange("p (k f) -> p k f", k=KD)
                    [:, :, c * 128:(c + 1) * 128]), inc=sWG, k=16)
                wqk_ld = sWG.n
                qT.wait(sWG, wqk_ld)
                qT.wait(sV, xlnT_ready)
                for t in range(NT):
                    b = bA[4 + t % 2]
                    b.claim(qT)
                    for k in range(KD):
                        qT.do(lambda e, k=k, t=t, b=b, c=c: e.matmul(
                            b.t[:, 0:128], xlnT[k][:, t * 128:(t + 1) * 128],
                            wv_b[0][:, k * 128:(k + 1) * 128],
                            start=(k == 0), stop=False))
                    qT.do(lambda e, b=b, c=c: e.matmul(
                        b.t[:, 0:128], ones_r[:],
                        bvr[:, c * 128:(c + 1) * 128],
                        start=False, stop=True), inc=sT)
                    b.ready = sT.n
                    qS.wait(sT, b.ready)
                    qS.wait(sT, v_free[c % 2])
                    qS.do(lambda e, b=b, t=t, c=c: e.copy(
                        v_c[c % 2][t], b.t[:, 0:128]), inc=sS)
                    b.freed = (sS, sS.n)
                wv_free = sT.n
                for wbuf, dst, bias in ((wq_b, qt, bq_sb),
                                        (wk_b, kt, bk_sb)):
                    for t4 in range(4):
                        bA[t4].claim(qT)
                    for k in range(KD):
                        for t4 in range(4):
                            qT.do(lambda e, k=k, t4=t4, wbuf=wbuf: e.matmul(
                                bA[t4].t[:], wbuf[:, k * 128:(k + 1) * 128],
                                xlnT[k][:, t4 * 512:(t4 + 1) * 512],
                                start=(k == 0), stop=(k == 7)),
                                inc=sT if (k == 7 and t4 == 3) else None)
                    rdy = sT.n
                    qS.wait(sT, rdy)
                    for t4 in range(4):
                        qS.do(lambda e, t4=t4, dst=dst, bias=bias, c=c:
                              e.activation(dst[:, t4 * 512:(t4 + 1) * 512],
                                           bA[t4].t[:], AF_.Identity,
                                           bias=bias[:, c:c + 1]),
                              inc=sS if t4 == 3 else None)
                    for t4 in range(4):
                        bA[t4].freed = (sS, sS.n)
                l0a_state["wqk_free"][0] = sT.n
                qk_ready = sS.n

                for hh in range(2):
                    h = 2 * c + hh
                    r0 = 64 * hh
                    for a in range(NT):
                        W = min(a, 2) + 1
                        j0 = 128 * (a - (W - 1))
                        wc = 128 * W
                        par = a % 2
                        cD = 8 + 2 * par
                        cRD = 9 + 2 * par
                        bs = bA[par]
                        bs.claim(qT)
                        qT.wait(sS, qk_ready)
                        pn_free = l0a_state.get("pn_free", 0)
                        pt_free = l0a_state.get("pt_free", 0)
                        qT.do(lambda e, bs=bs, r0=r0, a=a, j0=j0, wc=wc:
                              e.matmul(bs.t[:, 0:wc],
                                       qt[r0:r0 + 64, a * 128:(a + 1) * 128],
                                       kt[r0:r0 + 64, j0:j0 + wc],
                                       start=True, stop=True), inc=sT)
                        bs.ready = sT.n
                        qV.wait(sT, bs.ready)
                        qV.wait(sS, l0a_state.get(("exp", par), 0))
                        qV.do(lambda e, bs=bs, wc=wc, par=par:
                              e.scalar_tensor_tensor(
                                  srow[par][:, 0:wc], bs.t[:, 0:wc], 0.125,
                                  jrel[:, 0:wc], ALU.mult, ALU.add), inc=sV)
                        bs.freed = (sV, sV.n)
                        qV.do(lambda e, wc=wc, par=par: e.tensor_tensor(
                            srow[par][:, wc - 128:wc], srow[par][:, wc - 128:wc],
                            madd[:], ALU.add), inc=sV)
                        qS.wait(sV, sV.n)
                        qS.wait(sV, l0a_state.get(("pn", par), 0))
                        qS.do(lambda e, wc=wc, W=W, par=par, cD=cD:
                              e.activation(
                                  prow[par][:, 0:wc], srow[par][:, 0:wc],
                                  AF_.Exp, bias=negp[:, W - 1:W],
                                  accum_out=col(cD)), inc=sS)
                        l0a_state[("exp", par)] = sS.n
                        qV.wait(sS, sS.n)
                        qV.do(lambda e, cD=cD, cRD=cRD: e.reciprocal(
                            col(cRD), col(cD)), inc=sV, sw=True)
                        qV.wait(sT, pn_free)
                        qV.do(lambda e, wc=wc, par=par, cRD=cRD: e.tensor_scalar(
                            pnb[:, 0:wc], prow[par][:, 0:wc], col(cRD), None,
                            ALU.mult), inc=sV)
                        l0a_state[("pn", par)] = sV.n
                        pn_ready = sV.n
                        btr = bT[a % 2]
                        btr.claim(qT)
                        qT.wait(sV, pn_ready)
                        for w in range(W):
                            qT.do(lambda e, btr=btr, w=w, W=W: e.matmul(
                                btr.t[:, w * 128:(w + 1) * 128],
                                pnb[:, w * 128:(w + 1) * 128], identb[:],
                                is_transpose=True, start=(w == 0),
                                stop=(w == W - 1)),
                                inc=sT if w == W - 1 else None)
                        btr.ready = sT.n
                        l0a_state["pn_free"] = btr.ready
                        qS.wait(sT, btr.ready)
                        qS.wait(sT, pt_free)
                        for w in range(W):
                            qS.do(lambda e, btr=btr, w=w: e.copy(
                                ptb[w], btr.t[:, w * 128:(w + 1) * 128]),
                                inc=sS if w == W - 1 else None)
                        btr.freed = (sS, sS.n)
                        pt_ready = sS.n
                        bpv = bA[2 + a % 2]
                        bpv.claim(qT)
                        qT.wait(sS, pt_ready)
                        for w in range(W):
                            jt = a - (W - 1) + w
                            qT.do(lambda e, bpv=bpv, w=w, jt=jt, c=c, hh=hh,
                                  W=W: e.matmul(
                                      bpv.t[0:64, 0:128],
                                      v_c[c % 2][jt][:, hh * 64:hh * 64 + 64],
                                      ptb[w], start=(w == 0), stop=(w == W - 1)),
                                  inc=sT if w == W - 1 else None)
                        bpv.ready = sT.n
                        l0a_state["pt_free"] = bpv.ready
                        qV.wait(sT, bpv.ready)
                        qV.do(lambda e, bpv=bpv, c=c, r0=r0, a=a: e.tensor_copy(
                            at[c][r0:r0 + 64, a * 128:(a + 1) * 128],
                            bpv.t[0:64, 0:128]), inc=sV)
                        bpv.freed = (sV, sV.n)
                v_free[c % 2] = sT.n
            at_ready = sV.n

            qG.wait(sT, sT.n)
            qG.do(lambda e: e.dma_start(out=wo_b, in_=dt["woT"][:]),
                  inc=sWG, k=16)
            wo_ld = sWG.n
            qT.wait(sWG, wo_ld)
            qT.wait(sV, at_ready)
            for t in range(NT):
                for n in range(2):
                    b = bA[4 + (t * 2 + n) % 2]
                    b.claim(qT)
                    for k in range(KD):
                        qT.do(lambda e, k=k, t=t, n=n, b=b: e.matmul(
                            b.t[:], at[k][:, t * 128:(t + 1) * 128],
                            wo_b[:, k * D + n * 512:k * D + (n + 1) * 512],
                            start=(k == 0), stop=False))
                    qT.do(lambda e, n=n, b=b: e.matmul(
                        b.t[:], ones_r[:], bor[:, n * 512:(n + 1) * 512],
                        start=False, stop=True), inc=sT)
                    b.ready = sT.n
                    qV.wait(sT, b.ready)
                    xa = x_sb[:, t * D + n * 512:t * D + (n + 1) * 512]
                    qV.do(lambda e, b=b, xa=xa: e.tensor_tensor(
                        xa, xa, b.t[:], ALU.add), inc=sV)
                    b.freed = (sV, sV.n)

        def emit_l1_prep():
            AB.reset()
            AF.reset()
            phase_guard()
            stg = [AF.take(4096), AF.take(4096)]
            qa = AB.take(4096)
            qb = AB.take(4096)
            wqo = AB.take(4096)
            stg_free = [0, 0]
            store_done = sWS.n
            for wi, (name, nslices) in enumerate(
                    [("g", 8), ("u", 8), ("d", 8)]):
                raw = {"g": "wgT_raw", "u": "wuT_raw", "d": "wdT_raw"}[name]
                qV.do(lambda e, c=C_CM[name]: e.memset(col(c), 0.0), inc=sV, sw=True)
                for s in range(nslices):
                    qG.wait(sV, stg_free[s % 2])
                    qG.do(lambda e, s=s, raw=raw: e.dma_start(
                        out=stg[s % 2],
                        in_=dt[raw][:, s * 4096:(s + 1) * 4096]), inc=sWG, k=16)
                    ld = sWG.n
                    qV.wait(sWG, ld)
                    qV.do(lambda e, s=s: e.reduce_sum(
                        col(C_S1), stg[s % 2], AX.X, apply_absolute_value=True),
                        inc=sV, sw=True)
                    qV.do(lambda e, c=C_CM[name]: e.tensor_tensor(
                        col(c), col(c), col(C_S1), ALU.add), inc=sV, sw=True)
                    stg_free[s % 2] = sV.n
                b = bA[0]
                b.claim(qT)
                qT.wait(sV, sV.n)
                qT.do(lambda e, b=b, c=C_CM[name]: e.matmul(
                    b.t[0:1, 0:1], col(c), ones_c[:], start=True, stop=True),
                    inc=sT)
                b.ready = sT.n
                qV.wait(sT, b.ready)
                qV.do(lambda e, b=b: e.tensor_copy(dump[0:1, 0:1],
                                                   b.t[0:1, 0:1]), inc=sV, sw=True)
                b.freed = (sV, sV.n)
                b2 = bA[1]
                b2.claim(qT)
                qT.wait(sV, sV.n)
                qT.do(lambda e, b2=b2: e.matmul(
                    b2.t[:, 0:1], ones_r[:], dump[0:1, 0:1],
                    start=True, stop=True), inc=sT)
                b2.ready = sT.n
                qV.wait(sT, b2.ready)
                tot = 128.0 * nslices * 4096.0
                qV.do(lambda e, b2=b2, c=C_CM[name], tot=tot: e.tensor_scalar(
                    col(c), b2.t[:, 0:1], 1.0 / tot, None, ALU.mult), inc=sV, sw=True)
                b2.freed = (sV, sV.n)
                qV.do(lambda e, c=C_CM[name]: e.tensor_scalar(
                    col(c), col(c), 1e-5, None, ALU.max), inc=sV, sw=True)
                qV.do(lambda e, c=C_CM[name], p=C_TP[name]: e.tensor_scalar(
                    col(p), col(c), 0.5, None, ALU.mult), inc=sV, sw=True)
                qV.do(lambda e, c=C_CM[name], nn=C_TN[name]: e.tensor_scalar(
                    col(nn), col(c), -0.5, None, ALU.mult), inc=sV, sw=True)
                thr_ready = sV.n
                for s in range(nslices):
                    qG.wait(sV, stg_free[s % 2])
                    qG.do(lambda e, s=s, raw=raw: e.dma_start(
                        out=stg[s % 2],
                        in_=dt[raw][:, s * 4096:(s + 1) * 4096]), inc=sWG, k=16)
                    ld = sWG.n
                    qV.wait(sWG, ld)
                    qV.wait(sV, thr_ready)
                    qV.wait(sWS, store_done)
                    qV.do(lambda e, s=s, p=C_TP[name]: e.tensor_scalar(
                        qa, stg[s % 2], col(p), None, ALU.is_gt), inc=sV, sw=True)
                    qV.do(lambda e, s=s, nn=C_TN[name]: e.tensor_scalar(
                        qb, stg[s % 2], col(nn), None, ALU.is_lt), inc=sV, sw=True)
                    stg_free[s % 2] = sV.n
                    qV.do(lambda e: e.tensor_tensor(wqo, qa, qb, ALU.subtract),
                          inc=sV, sw=True)
                    wq_ready = sV.n
                    qY.wait(sV, wq_ready)
                    qY.do(lambda e, s=s, name=name: e.dma_start(
                        out=wQ[name][:, s * 4096:(s + 1) * 4096], in_=wqo),
                        inc=sWS, k=16)
                    store_done = sWS.n
            emit_l1_prep.store_done = store_done

        def emit_l1_ffn():
            AB.reset()
            AF.reset()
            phase_guard()
            xqT = [AB.take(512) for _ in range(KD)]
            xqb = [AB.take(D), AB.take(D)]
            offB = AB.off
            hid = [AB.at(offB + i * 4096, 4096) for i in range(4)]
            wdb = [AB.at(offB, 8 * 1024), AB.at(offB + 8 * 1024, 8 * 1024)]
            offC = offB + 4 * 4096
            wgb = [AB.at(offC, 4096), AB.at(offC + 4096, 4096)]
            wub = [AB.at(offC + 8192, 4096), AB.at(offC + 12288, 4096)]
            xq2T = [AB.at(offC + i * 512, 512) for i in range(32)]
            offX = offC + 4 * 4096
            xq2 = AB.at(offX, 4096)
            AB.hwm = max(AB.hwm, offX + 4096)

            xlnb = [AF.take(D), AF.take(D)]
            tst2 = [AF.take(D), AF.take(D)]
            y_acc = [AF.take(D) for _ in range(4)]
            sgn2 = [AB.at(offX + 4096, D), AB.at(offX + 5120, D)]
            AB.hwm = max(AB.hwm, offX + 6144)

            st = emit_l1_ffn.state
            for sp in range(4):
                for ti0 in (0, 2):
                    for jj in (0, 1):
                        ti = ti0 + jj
                        t = sp * 4 + ti
                        qV.wait(sT, st["xln_free"][jj])
                        emit_ln_tile(t, xlnb[jj])
                        qV.do(lambda e, x=xlnb[jj]: e.reduce_max(
                            col(C_A), x, AX.X, apply_absolute_value=True),
                            inc=sV, sw=True)
                        qV.do(lambda e: e.tensor_scalar(
                            col(C_A), col(C_A), 1e-5, None, ALU.max),
                            inc=sV, sw=True)
                        qV.do(lambda e: e.reciprocal(col(C_SX), col(C_A)),
                              inc=sV, sw=True)
                        qV.do(lambda e: e.tensor_scalar(
                            col(C_SX), col(C_SX), 127.0, None, ALU.mult),
                            inc=sV, sw=True)
                        qV.do(lambda e, x=xlnb[jj]: e.tensor_scalar(
                            x, x, col(C_SX), None, ALU.mult), inc=sV)
                        qV.do(lambda e, ti=ti: e.tensor_scalar(
                            cols[:, C_SXC + ti:C_SXC + ti + 1], col(C_A),
                            1.0 / 127.0, None, ALU.mult), inc=sV, sw=True)
                    emit_ladder2([(xlnb[0], xqb[0], tst2[0], sgn2[0]),
                                  (xlnb[1], xqb[1], tst2[1], sgn2[1])])
                    for jj in (0, 1):
                        emit_transpose_tile(ti0 + jj, xqb[jj], xqT, identb, bT)
                        st["xln_free"][jj] = sT.n
                xqT_ready = sV.n
                hid_ready = 0
                for s in range(8):
                    qG.wait(sT, st["wgu_free"][s % 2])
                    qG.wait(sT, st["wd_free"][1])
                    qG.wait(sWS, emit_l1_prep.store_done)
                    qG.do(lambda e, s=s: e.dma_start(
                        out=wgb[s % 2], in_=wQ["g"][:].rearrange(
                            "p (k o) -> p k o", k=KD)[:, :, s * 512:(s + 1) * 512]),
                        inc=sWG, k=16)
                    qG.do(lambda e, s=s: e.dma_start(
                        out=wub[s % 2], in_=wQ["u"][:].rearrange(
                            "p (k o) -> p k o", k=KD)[:, :, s * 512:(s + 1) * 512]),
                        inc=sWG, k=16)
                    wld = sWG.n
                    qT.wait(sWG, wld)
                    qT.wait(sV, xqT_ready)
                    for ti in range(4):
                        bg = bA[ti % 2]
                        bu = bA[2 + ti % 2]
                        bg.claim(qT)
                        bu.claim(qT)
                        for k in range(KD):
                            qT.do(lambda e, k=k, ti=ti, s=s, bg=bg: e.matmul(
                                bg.t[:], xqT[k][:, ti * 128:(ti + 1) * 128],
                                wgb[s % 2][:, k * 512:(k + 1) * 512],
                                start=(k == 0), stop=(k == 7)),
                                inc=sT if k == 7 else None)
                        bg.ready = sT.n
                        for k in range(KD):
                            qT.do(lambda e, k=k, ti=ti, s=s, bu=bu: e.matmul(
                                bu.t[:], xqT[k][:, ti * 128:(ti + 1) * 128],
                                wub[s % 2][:, k * 512:(k + 1) * 512],
                                start=(k == 0), stop=(k == 7)),
                                inc=sT if k == 7 else None)
                        bu.ready = sT.n
                        qV.do(lambda e, ti=ti: e.tensor_tensor(
                            col(C_C1), cols[:, C_SXC + ti:C_SXC + ti + 1],
                            col(C_CM["g"]), ALU.mult), inc=sV, sw=True)
                        qV.do(lambda e, ti=ti: e.tensor_tensor(
                            col(C_G1), cols[:, C_SXC + ti:C_SXC + ti + 1],
                            col(C_CM["u"]), ALU.mult), inc=sV, sw=True)
                        c_ready = sV.n
                        qS.wait(sT, bg.ready)
                        qS.wait(sV, c_ready)
                        qS.wait(sV, st["sil1"][0])
                        qS.do(lambda e, bg=bg: e.activation(
                            dump[:, 0:512], bg.t[:], AF_.Silu,
                            scale=col(C_C1)), inc=sS)
                        sdone = sS.n
                        bg.freed = (sS, sdone)
                        qV.wait(sS, sdone)
                        qV.wait(sT, bu.ready)
                        qV.do(lambda e, bu=bu, ti=ti, s=s:
                              e.scalar_tensor_tensor(
                                  hid[ti][:, s * 512:(s + 1) * 512], bu.t[:],
                                  col(C_G1), dump[:, 0:512],
                                  ALU.mult, ALU.mult), inc=sV)
                        bu.freed = (sV, sV.n)
                        st["sil1"][0] = sV.n
                    st["wgu_free"][s % 2] = sT.n
                hid_ready = sV.n
                gateup_done = sT.n
                for ti in range(4):
                    qV.wait(sV, hid_ready)
                    qV.do(lambda e, ti=ti: e.reduce_max(
                        col(C_A2), hid[ti], AX.X, apply_absolute_value=True),
                        inc=sV, sw=True)
                    qV.do(lambda e: e.tensor_scalar(col(C_A2), col(C_A2), 1e-5,
                                                    None, ALU.max), inc=sV, sw=True)
                    qV.do(lambda e: e.reciprocal(col(C_SX), col(C_A2)), inc=sV, sw=True)
                    qV.do(lambda e: e.tensor_scalar(col(C_SX), col(C_SX), 127.0,
                                                    None, ALU.mult), inc=sV, sw=True)
                    qV.do(lambda e, ti=ti: e.tensor_scalar(
                        col(C_C2), col(C_A2), 0.5 / 127.0, None, ALU.mult),
                        inc=sV, sw=True)
                    qV.do(lambda e, ti=ti: e.tensor_tensor(
                        cols[:, C_SXC + 8 + ti:C_SXC + 9 + ti], col(C_C2),
                        col(C_CM["d"]), ALU.mult), inc=sV, sw=True)
                    qw = 1024
                    qT.wait(sT, gateup_done)
                    for q0 in (0, 2):
                        for jj in (0, 1):
                            qq = q0 + jj
                            ha = hid[ti][:, qq * qw:(qq + 1) * qw]
                            qV.do(lambda e, ha=ha, jj=jj: e.tensor_scalar(
                                xlnb[jj], ha, col(C_SX), None, ALU.mult),
                                inc=sV)
                        emit_ladder2([
                            (xlnb[0], xq2[:, q0 * qw:(q0 + 1) * qw],
                             tst2[0], sgn2[0]),
                            (xlnb[1], xq2[:, (q0 + 1) * qw:(q0 + 2) * qw],
                             tst2[1], sgn2[1])])
                        lad_done = sV.n
                        for g in range(q0 * 2, q0 * 2 + 4):
                            bk = bT[g % 2]
                            bk.claim(qT)
                            qT.wait(sV, lad_done)
                            for j in range(4):
                                kh = g * 4 + j
                                qT.do(lambda e, bk=bk, j=j, kh=kh: e.matmul(
                                    bk.t[:, j * 128:(j + 1) * 128],
                                    xq2[:, kh * 128:(kh + 1) * 128], identb[:],
                                    is_transpose=True, start=(j == 0),
                                    stop=(j == 3)),
                                    inc=sT if j == 3 else None)
                            bk.ready = sT.n
                            qV.wait(sT, bk.ready)
                            for j in range(4):
                                kh = g * 4 + j
                                qV.do(lambda e, bk=bk, j=j, kh=kh, ti=ti:
                                      e.tensor_copy(
                                          xq2T[kh][:, ti * 128:(ti + 1) * 128],
                                          bk.t[:, j * 128:(j + 1) * 128]),
                                      inc=sV if j == 3 else None)
                            bk.freed = (sV, sV.n)
                xq2T_ready = sV.n
                for kg in range(4):
                    qG.wait(sV, xq2T_ready if kg == 0 else 0)
                    qG.wait(sT, st["wd_free"][kg % 2])
                    qG.do(lambda e, kg=kg: e.dma_start(
                        out=wdb[kg % 2],
                        in_=wQ["d"][:, kg * 8 * D:(kg + 1) * 8 * D]),
                        inc=sWG, k=16)
                    wld = sWG.n
                    qT.wait(sWG, wld)
                    qT.wait(sV, xq2T_ready)
                    for ti in range(4):
                        t = sp * 4 + ti
                        for n in range(2):
                            b = bA[4 + (ti * 2 + n) % 2]
                            b.claim(qT)
                            for kk in range(8):
                                kh = kg * 8 + kk
                                qT.do(lambda e, kk=kk, kh=kh, ti=ti, n=n, b=b, kg=kg:
                                      e.matmul(
                                          b.t[:],
                                          xq2T[kh][:, ti * 128:(ti + 1) * 128],
                                          wdb[kg % 2][:, kk * D + n * 512:kk * D + (n + 1) * 512],
                                          start=(kk == 0), stop=(kk == 7)),
                                      inc=sT if kk == 7 else None)
                            b.ready = sT.n
                            qV.wait(sT, b.ready)
                            if kg == 0:
                                qV.do(lambda e, b=b, ti=ti, n=n: e.tensor_copy(
                                    y_acc[ti][:, n * 512:(n + 1) * 512], b.t[:]),
                                    inc=sV)
                            elif kg < 3:
                                ya = y_acc[ti][:, n * 512:(n + 1) * 512]
                                qV.do(lambda e, b=b, ya=ya: e.tensor_tensor(
                                    ya, ya, b.t[:], ALU.add), inc=sV)
                            else:
                                xa = x_sb[:, t * D + n * 512:t * D + (n + 1) * 512]
                                qV.wait(sV, st["dn_dump"])
                                qV.do(lambda e, b=b, ti=ti, n=n: e.tensor_tensor(
                                    dump[:, 0:512], b.t[:],
                                    y_acc[ti][:, n * 512:(n + 1) * 512], ALU.add),
                                    inc=sV)
                                qV.do(lambda e, xa=xa, ti=ti:
                                      e.scalar_tensor_tensor(
                                          xa, dump[:, 0:512],
                                          cols[:, C_SXC + 8 + ti:C_SXC + 9 + ti],
                                          xa, ALU.mult, ALU.add), inc=sV)
                                st["dn_dump"] = sV.n
                            b.freed = (sV, sV.n)
                    st["wd_free"][kg % 2] = sT.n
                st["xq_free_v"][0] = 0

        emit_l1_ffn.state = {
            "xln_free": [0, 0], "xq_free_v": [0, 0], "wgu_free": [0, 0],
            "wd_free": [0, 0], "sil1": [0, 0], "dn_dump": 0}
        emit_l1_prep.store_done = 0

        def emit_l1_attn():
            AB.reset()
            AF.reset()
            phase_guard()
            xlnT = [AB.take(T) for _ in range(KD)]
            xlnb16 = [AB.take(D), AB.take(D)]
            xln_free = [0, 0]
            for t in range(NT):
                qV.wait(sT, xln_free[t % 2])
                emit_ln_tile(t, xlnb16[t % 2])
                emit_transpose_tile(t, xlnb16[t % 2], xlnT, identb, bT)
                xln_free[t % 2] = sT.n
            xlnT_ready = sV.n

            v_sb = [AB.take(D) for _ in range(NT)]
            qt1 = AB.take(T)
            kt1 = AB.take(T)
            wv1b = AB.take(KD * 512)
            attnb = AB.at(0, T)
            atT = [AB.at(T + i * 128, 128) for i in range(NT)]
            qkbuf = AF.take(512)
            ub = AF.take(512)
            gb = AF.take(512)
            sgb = AF.take(512)
            aub = AF.take(512)
            linb = AF.take(512)
            phb = AF.take(512)

            for t4 in range(4):
                bA[t4].claim(qT)
            qT.wait(sV, xlnT_ready)
            for k in range(KD):
                for t4 in range(4):
                    qT.do(lambda e, k=k, t4=t4: e.matmul(
                        bA[t4].t[:], wqk_sb[:, k * 128:(k + 1) * 128],
                        xlnT[k][:, t4 * 512:(t4 + 1) * 512],
                        start=(k == 0), stop=(k == 7)),
                        inc=sT if (k == 7 and t4 == 3) else None)
            rdy = sT.n
            for t4 in range(4):
                qS.wait(sT, rdy)
                qS.wait(sV, sV.n)
                qS.do(lambda e, t4=t4: e.activation(
                    qkbuf, bA[t4].t[:], AF_.Silu, bias=bqk_sb[:, 0:1]), inc=sS)
                sdone = sS.n
                qV.wait(sS, sdone)
                qV.do(lambda e, t4=t4: e.tensor_scalar(
                    qt1[:, t4 * 512:(t4 + 1) * 512], qkbuf,
                    gb_sb[:, 0:1], gb_sb[:, 2:3], ALU.mult, ALU.add), inc=sV)
                qV.do(lambda e, t4=t4: e.tensor_scalar(
                    kt1[:, t4 * 512:(t4 + 1) * 512], qkbuf,
                    gb_sb[:, 1:2], gb_sb[:, 3:4], ALU.mult, ALU.add), inc=sV)
                bA[t4].freed = (sS, sdone)
            qk_ready = sV.n

            wv1_free = 0
            for n in range(2):
                qG.wait(sT, wv1_free)
                qG.do(lambda e, n=n: e.dma_start(
                    out=wv1b, in_=dt["wv1T"][:].rearrange(
                        "p (k f) -> p k f", k=KD)[:, :, n * 512:(n + 1) * 512]),
                    inc=sWG, k=16)
                wv_ld = sWG.n
                qT.wait(sWG, wv_ld)
                for t in range(NT):
                    b = bA[t % 4]
                    b.claim(qT)
                    for k in range(KD):
                        qT.do(lambda e, k=k, t=t, n=n, b=b: e.matmul(
                            b.t[:], xlnT[k][:, t * 128:(t + 1) * 128],
                            wv1b[:, k * 512:(k + 1) * 512],
                            start=(k == 0), stop=False))
                    qT.do(lambda e, n=n, b=b: e.matmul(
                        b.t[:], ones_r[:], bv1r[:, n * 512:(n + 1) * 512],
                        start=False, stop=True), inc=sT)
                    b.ready = sT.n
                    qS.wait(sT, b.ready)
                    qS.do(lambda e, b=b, t=t, n=n: e.activation(
                        v_sb[t][:, n * 512:(n + 1) * 512], b.t[:], AF_.Silu),
                        inc=sS)
                    b.freed = (sS, sS.n)
                wv1_free = sT.n
            v_ready = sS.n

            vproj_done = sT.n
            l1a_state = {"attnb_free": vproj_done, "atT_free": vproj_done}
            for a in range(NT):
                nj = a + 1
                qV.wait(sT, l1a_state["attnb_free"])
                for bi in range((nj * 128 + 511) // 512):
                    bw = min(512, nj * 128 - bi * 512)
                    bs = bA[bi % 2]
                    bs.claim(qT)
                    qT.wait(sV, qk_ready)
                    qT.do(lambda e, bs=bs, a=a, bi=bi, bw=bw: e.matmul(
                        bs.t[:, 0:bw], qt1[:, a * 128:(a + 1) * 128],
                        kt1[:, bi * 512:bi * 512 + bw], start=True, stop=True),
                        inc=sT)
                    bs.ready = sT.n
                    for jj in range(bw // 128):
                        jti = bi * 4 + jj
                        r = a - jti
                        off = jj * 128
                        qV.wait(sT, bs.ready)
                        qV.wait(sS, sS.n)
                        if r >= 2:
                            qS.wait(sT, bs.ready)
                            qS.wait(sV, sV.n)
                            qS.do(lambda e, bs=bs, off=off: e.activation(
                                ub[:, 0:128], bs.t[:, off:off + 128], AF_.Identity,
                                scale=ALPHA_L1, bias=bn_sb[:, 128:129]), inc=sS)
                            qS.do(lambda e, bs=bs, off=off: e.activation(
                                gb[:, 0:128], bs.t[:, off:off + 128], AF_.Gelu,
                                scale=ALPHA_L1, bias=bn_sb[:, 128:129]),
                                inc=sS, sw=True)
                            qS.do(lambda e: e.activation(
                                sgb[:, 0:128], ub[:, 0:128], AF_.Sign), inc=sS)
                            u_from = sS.n
                            qV.wait(sS, u_from)
                        else:
                            bnr = bn_sb[:, r * 128:(r + 1) * 128]
                            qV.do(lambda e, bs=bs, off=off, bnr=bnr:
                                  e.scalar_tensor_tensor(
                                      ub[:, 0:128], bs.t[:, off:off + 128],
                                      ALPHA_L1, bnr, ALU.mult, ALU.add), inc=sV, sw=True)
                            qS.wait(sV, sV.n)
                            qS.do(lambda e: e.activation(
                                gb[:, 0:128], ub[:, 0:128], AF_.Gelu), inc=sS)
                            qS.do(lambda e: e.activation(
                                sgb[:, 0:128], ub[:, 0:128], AF_.Sign), inc=sS)
                            qV.wait(sS, sS.n)
                        qV.do(lambda e: e.tensor_tensor(
                            aub[:, 0:128], ub[:, 0:128], sgb[:, 0:128],
                            ALU.mult), inc=sV, sw=True)
                        qV.do(lambda e: e.tensor_scalar(
                            aub[:, 128:256], aub[:, 0:128], 1e-20, None,
                            ALU.max), inc=sV, sw=True)
                        qV.do(lambda e: e.reciprocal(aub[:, 256:384],
                                                     aub[:, 128:256]), inc=sV, sw=True)
                        qV.do(lambda e: e.tensor_tensor(
                            phb[:, 0:128], gb[:, 0:128], aub[:, 256:384],
                            ALU.mult), inc=sV, sw=True)
                        qV.do(lambda e: e.tensor_tensor(
                            phb[:, 0:128], phb[:, 0:128], sgb[:, 0:128],
                            ALU.mult), inc=sV, sw=True)
                        qV.do(lambda e: e.tensor_scalar(
                            linb[:, 0:128], ub[:, 0:128], C0, 0.5,
                            ALU.mult, ALU.add), inc=sV, sw=True)
                        qV.do(lambda e: e.tensor_scalar(
                            msku8[:, 0:128], aub[:, 0:128], 1e-3, None,
                            ALU.is_lt), inc=sV, sw=True)
                        if r == 0:
                            qV.do(lambda e: e.select(
                                linb[:, 128:256], msku8[:, 0:128],
                                linb[:, 0:128], phb[:, 0:128]), inc=sV, sw=True)
                            qV.do(lambda e, jti=jti: e.tensor_tensor(
                                attnb[:, jti * 128:(jti + 1) * 128],
                                linb[:, 128:256], m01[:], ALU.mult), inc=sV, sw=True)
                        else:
                            qV.do(lambda e, jti=jti: e.select(
                                attnb[:, jti * 128:(jti + 1) * 128],
                                msku8[:, 0:128], linb[:, 0:128],
                                phb[:, 0:128]), inc=sV, sw=True)
                    bs.freed = (sV, sV.n)
                attn_ready = sV.n
                for g in range((nj + 3) // 4):
                    bk = bT[g % 2]
                    bk.claim(qT)
                    qT.wait(sV, attn_ready)
                    lim = min(4, nj - g * 4)
                    for j in range(lim):
                        jti = g * 4 + j
                        qT.do(lambda e, bk=bk, j=j, jti=jti, lim=lim: e.matmul(
                            bk.t[:, j * 128:(j + 1) * 128],
                            attnb[:, jti * 128:(jti + 1) * 128], identb[:],
                            is_transpose=True, start=(j == 0),
                            stop=(j == lim - 1)),
                            inc=sT if j == lim - 1 else None)
                    bk.ready = sT.n
                    l1a_state["attnb_free"] = bk.ready
                    qS.wait(sT, bk.ready)
                    qS.wait(sT, l1a_state["atT_free"])
                    for j in range(lim):
                        jti = g * 4 + j
                        qS.do(lambda e, bk=bk, j=j, jti=jti: e.copy(
                            atT[jti], bk.t[:, j * 128:(j + 1) * 128]),
                            inc=sS if j == lim - 1 else None)
                    bk.freed = (sS, sS.n)
                at_ready = sS.n
                qT.wait(sS, at_ready)
                qT.wait(sS, v_ready)
                b0 = bA[2]
                b1 = bA[3]
                b0.claim(qT)
                b1.claim(qT)
                for jti in range(nj):
                    for n, b in ((0, b0), (1, b1)):
                        qT.do(lambda e, jti=jti, n=n, b=b, nj=nj: e.matmul(
                            b.t[:], atT[jti],
                            v_sb[jti][:, n * 512:(n + 1) * 512],
                            start=(jti == 0), stop=(jti == nj - 1)),
                            inc=sT if (jti == nj - 1 and n == 1) else None)
                rdy = sT.n
                l1a_state["atT_free"] = rdy
                for n, b in ((0, b0), (1, b1)):
                    b.ready = rdy
                    qV.wait(sT, rdy)
                    xa = x_sb[:, a * D + n * 512:a * D + (n + 1) * 512]
                    qV.do(lambda e, b=b, xa=xa: e.tensor_tensor(
                        xa, xa, b.t[:], ALU.add), inc=sV)
                    b.freed = (sV, sV.n)

        STAGE_FN = {"l0f_a": emit_l0_ffn, "l0a": emit_l0_attn,
                    "l0f_b": emit_l0_ffn, "l1prep": emit_l1_prep,
                    "l1f_a": emit_l1_ffn, "l1a": emit_l1_attn,
                    "l1f_b": emit_l1_ffn}
        for st_name in stages:
            STAGE_FN[st_name]()

        qY.wait(sV, sV.n)
        qY.wait(sS, sS.n)
        qY.wait(sT, sT.n)
        for t in range(NT):
            qY.do(lambda e, t=t: e.dma_start(
                out=y_out[t], in_=x_sb[:, t * D:(t + 1) * D]), inc=sST, k=16)
        qY.wait(sST, sST.n)
        qY.wait(sWS, sWS.n)
        qG.wait(sWG, sWG.n)

        blk = ctx.enter_context(nc.Block())

        @blk.tensor
        def _(e):
            for f in qT.ops:
                f(e)

        @blk.vector
        def _(e):
            for f in qV.ops:
                f(e)

        @blk.scalar
        def _(e):
            for f in qS.ops:
                f(e)

        @blk.sync
        def _(e):
            for f in qY.ops:
                f(e)

        @blk.gpsimd
        def _(e):
            for f in qG.ops:
                f(e)

    return nc


def _host_consts():
    f32 = np.float32
    L = {}
    p = np.arange(128)[:, None]
    fq = np.arange(128)[None, :]
    L["identb"] = np.eye(128, dtype=ml_dtypes.bfloat16)
    L["identf"] = np.eye(128, dtype=f32)
    L["ones_row"] = np.ones((1, 128), f32)
    L["ones_col"] = np.ones((128, 1), f32)
    L["jrel"] = np.ascontiguousarray(
        np.broadcast_to(np.arange(384, dtype=f32)[None, :], (128, 384)))
    L["madd"] = np.where(fq <= p, 0.0, -1e30).astype(f32)
    L["m01"] = (fq <= p).astype(f32)
    L["negp"] = (-(np.arange(128, dtype=f32)[:, None]
                   + 128.0 * np.arange(3, dtype=f32)[None, :]))
    return L


def _host_weights(inputs):
    f32 = np.float32
    L = {}
    wq = np.asarray(inputs["L0_attn_Wq"], f32)
    wk = np.asarray(inputs["L0_attn_Wk"], f32)
    wv = np.asarray(inputs["L0_attn_Wv"], f32)
    wo = np.asarray(inputs["L0_attn_Wo"], f32)
    L["wqT"] = np.ascontiguousarray(
        wq.reshape(KD, 128, KD, 128).transpose(0, 3, 2, 1)).reshape(KD, 128, KD * 128)
    L["wkT"] = np.ascontiguousarray(
        wk.reshape(KD, 128, KD, 128).transpose(0, 3, 2, 1)).reshape(KD, 128, KD * 128)
    L["wvT"] = np.ascontiguousarray(
        wv.reshape(D, KD, 128).transpose(2, 1, 0)).reshape(128, KD * D)
    L["woT"] = np.ascontiguousarray(
        wo.reshape(D, KD, 128).transpose(2, 1, 0)).reshape(128, KD * D)
    L["bq_c"] = np.ascontiguousarray(
        np.asarray(inputs["L0_attn_bq"], f32).reshape(KD, 128, 1))
    L["bk_c"] = np.ascontiguousarray(
        np.asarray(inputs["L0_attn_bk"], f32).reshape(KD, 128, 1))
    L["bv_row"] = np.asarray(inputs["L0_attn_bv"], f32).reshape(1, D)
    L["bo_row"] = np.asarray(inputs["L0_attn_bo"], f32).reshape(1, D)

    w12 = np.asarray(inputs["L0_ffn_w12"], f32)
    t = w12.reshape(E, 2, 16, 128, KD, 128)
    L["w12P"] = np.ascontiguousarray(
        t.transpose(0, 5, 2, 4, 1, 3).reshape(E, 128, 16, 2048))
    w3 = np.asarray(inputs["L0_ffn_w3"], f32)
    L["w3T"] = np.ascontiguousarray(
        w3.reshape(E, D, 16, 128).transpose(0, 3, 2, 1)).reshape(E, 128, 16 * 1024)
    gwv = np.asarray(inputs["L0_ffn_gate_w"], f32)
    L["gwT"] = np.ascontiguousarray(
        gwv.reshape(E, KD, 128).transpose(2, 1, 0)).reshape(128, KD * E)

    wqk = np.asarray(inputs["L1_attn_wqk"], f32)
    L["wqk1"] = np.ascontiguousarray(
        wqk.reshape(KD, 128, 128).transpose(1, 0, 2)).reshape(128, KD * 128)
    L["bqk_c"] = np.asarray(inputs["L1_attn_bqk"], f32).reshape(128, 1)
    gm = np.asarray(inputs["L1_attn_gamma"], f32)
    bt = np.asarray(inputs["L1_attn_beta"], f32)
    L["gb_c"] = np.ascontiguousarray(
        np.stack([gm[0], gm[1], bt[0], bt[1]])[:, :, None])
    wv1 = np.asarray(inputs["L1_attn_wv"], f32)
    L["wv1T"] = np.ascontiguousarray(
        wv1.reshape(KD, 128, D).transpose(1, 0, 2)).reshape(128, KD * D)
    L["bv1_row"] = np.asarray(inputs["L1_attn_bv"], f32).reshape(1, D)

    tbl = np.asarray(inputs["L1_attn_rel_bias"], f32)
    pp = np.arange(128)[:, None]
    ff = np.arange(128)[None, :]
    braw = np.zeros((2, 128, 128), f32)
    for r in range(2):
        dist = 128 * r + pp - ff
        n = np.maximum(dist, 0)
        with np.errstate(divide="ignore", invalid="ignore"):
            vl = 16 + (np.log(np.maximum(n, 1).astype(np.float64) / 16.0)
                       / np.log(8.0) * 16).astype(np.int64)
        vl = np.minimum(vl, 31)
        bucket = np.where(n < 16, n, vl).astype(np.int64)
        braw[r] = tbl[bucket]
    L["braw"] = braw

    wg = np.asarray(inputs["L1_ffn_w_gate"], f32)
    wu = np.asarray(inputs["L1_ffn_w_up"], f32)
    wd = np.asarray(inputs["L1_ffn_w_down"], f32)
    L["wgT_raw"] = np.ascontiguousarray(
        wg.reshape(INNER, KD, 128).transpose(2, 1, 0)).reshape(128, KD * INNER)
    L["wuT_raw"] = np.ascontiguousarray(
        wu.reshape(INNER, KD, 128).transpose(2, 1, 0)).reshape(128, KD * INNER)
    L["wdT_raw"] = np.ascontiguousarray(
        wd.reshape(D, 32, 128).transpose(2, 1, 0)).reshape(128, 32 * D)
    L.update(_host_consts())
    return L


_NC_CACHE = {}


def kernel(**inputs):
    key = "all"
    if key not in _NC_CACHE:
        _NC_CACHE[key] = build_nc(ALL_STAGES)
    nc = _NC_CACHE[key]
    shared = _host_weights(inputs)
    x = np.asarray(inputs["L0_attn_x"], np.float32)
    in_maps = []
    for c in range(8):
        m = dict(shared)
        m["x_in"] = np.ascontiguousarray(x[c].reshape(NT, 128, D))
        in_maps.append(m)
    res = run_bass_kernel_spmd(nc, in_maps, core_ids=list(range(8)))
    out = np.stack([r["y"].reshape(T, D) for r in res.results])
    return out.astype(np.float32)
